# version 03
# baseline (speedup 1.0000x reference)
import sys
sys.path.insert(0, "/opt/trn_rl_repo")
import numpy as np
from contextlib import ExitStack
import concourse.bass as bass
import concourse.mybir as mybir
from concourse.bass_utils import run_bass_kernel_spmd

F32 = mybir.dt.float32
BF16 = mybir.dt.bfloat16
AF = mybir.ActivationFunctionType
ALU = mybir.AluOpType
AX = mybir.AxisListType
ts = bass.ts

T, D, NH, HD, NC = 8192, 2048, 16, 128, 8
TT = 512
NTT = T // TT
DC = D // 128
TPC = T // NC
E8, F, FS = 8, 1408, 2816
FC, FSC = F // 128, FS // 128
EPS = 1e-5
ISQ = float(1.0 / np.sqrt(HD))
VW = 136
NEG_BIG = -1e30


def head_pair(c):
    return c, 15 - c


def slope(h):
    return 2.0 ** (-(h + 1) / 2.0)


class Ctr:
    def __init__(self, nc, name, ctx):
        self.sem = ctx.enter_context(nc.semaphore(name))
        self.v = 0

    def inc(self, n=1):
        self.v += n
        return self.sem, n


def build_nc(phase="full", debug=False):
    nc = bass.Bass("TRN2", target_bir_lowering=False, debug=False, detect_race_conditions=False)
    ctx = ExitStack()
    dr = lambda name, shape, dt, kind="ExternalInput": nc.dram_tensor(name, shape, dt, kind=kind).ap()

    x_st = dr("x_st", [NTT, 128, DC * TT], F32)
    wq_st = dr("wq_st", [128, DC * 256], F32)
    wk_st = dr("wk_st", [128, DC * 256], F32)
    wv_st = dr("wv_st", [128, DC * 256], F32)
    md_st = dr("md_st", [128, 8 * TT], F32)
    bc_st = dr("bc_st", [128, 128], F32)
    ec_st = dr("ec_st", [128, 8], F32)
    has2 = phase == "full"
    if has2:
        wo_st = dr("wo_st", [128, DC * D], F32)
        xown_st = dr("xown_st", [32, 128, TT], F32)
        gate_st = dr("gate_st", [128, DC * 8], F32)
        idn_st = dr("idn_st", [128, 128], F32)
        sel_st = dr("sel_st", [8, 8 * 128], F32)
        gu_st = dr("gu_st", [E8, FC, 128, DC * 256], F32)
        dn_st = dr("dn_st", [E8, DC, 128, FC * 128], F32)
        sgu_st = dr("sgu_st", [FSC, 128, DC * 256], F32)
        sdn_st = dr("sdn_st", [DC, 128, FSC * 128], F32)
        yT = dr("yT", [D, TPC], F32, "ExternalOutput")
    has1b = phase in ("1b", "full")
    if has1b:
        cc_in = nc.dram_tensor("cc_in", [T, 2 * HD], BF16).ap()
        cc_out = nc.dram_tensor("cc_out", [NC * T, 2 * HD], BF16).ap()
        cc_stage = nc.dram_tensor("cc_stage", [NC * TPC, 2 * HD], BF16).ap()

    dbg = {}
    if debug:
        dbg["qT"] = dr("dbg_qT", [128, 2 * T], BF16, "ExternalOutput")
        dbg["kT"] = dr("dbg_kT", [128, 2 * T], BF16, "ExternalOutput")
        dbg["v"] = dr("dbg_v", [128, 2 * 64 * VW], BF16, "ExternalOutput")
        if has1b:
            dbg["attnT"] = dr("dbg_attnT", [128, DC * TPC], BF16, "ExternalOutput")
        if has2:
            dbg["y1"] = dr("dbg_y1", [128, DC * TPC], F32, "ExternalOutput")
            dbg["xn2"] = dr("dbg_xn2", [128, DC * TPC], BF16, "ExternalOutput")
            dbg["comb"] = dr("dbg_comb", [8, TPC], F32, "ExternalOutput")
            dbg["gsc"] = dr("dbg_gsc", [128, 64], F32, "ExternalOutput")
            dbg["gs2"] = dr("dbg_gs2", [128, 64], F32, "ExternalOutput")
            dbg["gct"] = dr("dbg_gct", [128, 64], F32, "ExternalOutput")
            dbg["gcl"] = dr("dbg_gcl", [128, 64], F32, "ExternalOutput")

    sbuf = lambda name, shape, dt=F32: ctx.enter_context(nc.sbuf_tensor(name, shape, dt))
    pb = [ctx.enter_context(nc.psum_tensor(f"pb{i}", [128, 512], F32)) for i in range(8)]

    S = {}
    for name in ["dx0", "dx1", "dw", "sq", "sqcons", "stats", "v1", "s2", "v2", "bc",
                 "evac", "qk", "qkfix", "vmm", "vevac", "xcons", "setup",
                 "tsu", "vsu", "p1a",
                 "ssc", "sdg", "sex", "sav", "sod", "scmb", "sot", "soc",
                 "scc", "rcv", "pb1",
                 "dwo", "swo", "svo", "y1e", "st2", "v21", "s21", "v22",
                 "bc2", "ev2", "xn2d", "glog", "gcmb", "gsel",
                 "dgu0", "dgu1", "ddn0", "ddn1", "gups", "sil", "uact", "dnps", "dnac",
                 "cbk", "cbt", "dxo0", "dxo1", "s22", "v23", "g1", "g2", "gtr",
                 "yst"]:
        S[name] = Ctr(nc, name, ctx)

    block = ctx.enter_context(nc.Block())

    p1 = ExitStack()
    sb1 = lambda name, shape, dt=F32: p1.enter_context(nc.sbuf_tensor(name, shape, dt))
    wq_sb = sb1("wq_sb", [128, DC * 256], BF16)
    wk_sb = sb1("wk_sb", [128, DC * 256], BF16)
    wv_sb = sb1("wv_sb", [128, DC * 256], BF16)
    xbuf = [sb1(f"xbuf{i}", [128, DC * TT], BF16) for i in range(2)]
    xsq = [sb1(f"xsq{i}", [128, TT], BF16) for i in range(2)]
    Q_sb = sb1("Q_sb", [128, 2 * T], BF16)
    K_sb = sb1("K_sb", [128, 2 * T], BF16)
    V_sb = sb1("V_sb", [128, 2 * 64 * VW], BF16)
    rows = sb1("rows", [1, 4 * TT])
    rw = [rows[0:1, i * TT:(i + 1) * TT] for i in range(4)]
    a_b = sb1("a_b", [128, TT])
    b_b = sb1("b_b", [128, TT])
    colp = [sb1(f"colp{i}", [128, 16]) for i in range(2)]
    negws = sb1("negws", [128, 8])
    wvb = sb1("wvb", [128, 256])
    ones_bf = sb1("ones_bf", [128, 8], BF16)
    onesf = sb1("onesf", [1, 128])
    one1 = sb1("one1", [1, 8])
    wsrow = sb1("wsrow", [1, 3 * 256])
    md_sb = sb1("md_sb", [128, 8 * TT])
    bcc_sb = sb1("bcc_sb", [128, 128])
    ec_sb = sb1("ec_sb", [128, 8])
    P_pool = [sb1(f"Pp{i}", [128, TT], BF16) for i in range(4)]
    tmpS = [sb1(f"tmpS{i}", [128, TT]) for i in range(2)]
    tmpf = tmpS
    tmpv = [tmpS[0][:, 0:256], tmpS[1][:, 0:256]]
    odsb = sb1("odsb", [128, 4 * VW])
    cmb = sb1("cmb", [128, 4 * VW])
    rcp = sb1("rcp", [128, 16])
    outp = sb1("outp", [128, 8 * 128], BF16)

    sched = []
    p_ctr = 0
    dgi_ctr = 0
    diag_pass_of = {}
    for h in range(2):
        for qi in range(NTT):
            ms = []
            for m in range(4 * qi + 4):
                if m < 4 * qi:
                    ms.append((m, p_ctr, "od", 4 * qi - m, None))
                else:
                    diag_pass_of[dgi_ctr] = p_ctr
                    ms.append((m, p_ctr, "dg", m - 4 * qi, dgi_ctr))
                    dgi_ctr += 1
                p_ctr += 1
            sched.append((h, qi, ms))
    NPASS = p_ctr

    @block.gpsimd
    def _(g):
        g.dma_start(out=wq_sb[:, :], in_=wq_st).then_inc(*S["dw"].inc(16))
        g.dma_start(out=wk_sb[:, :], in_=wk_st).then_inc(*S["dw"].inc(16))
        g.dma_start(out=wv_sb[:, :], in_=wv_st).then_inc(*S["dw"].inc(16))
        g.dma_start(out=md_sb[:, :], in_=md_st).then_inc(*S["dw"].inc(16))
        g.dma_start(out=bcc_sb[:, :], in_=bc_st).then_inc(*S["dw"].inc(16))
        g.dma_start(out=ec_sb[:, :], in_=ec_st).then_inc(*S["dw"].inc(16))
        for t in range(NTT):
            if t >= 2:
                g.wait_ge(S["vmm"].sem, 2 * t - 2)
            g.dma_start(out=xbuf[t % 2][:, :], in_=x_st[t]).then_inc(*S[f"dx{t % 2}"].inc(16))
        if has1b:
            g.wait_ge(S["soc"].sem, 128 * 16)
            g.collective_compute(
                "AllGather", ALU.bypass, replica_groups=[list(range(NC))],
                ins=[cc_in.opt()], outs=[cc_out.opt()],
            ).then_inc(*S["scc"].inc(1))

    @block.scalar
    def _(s):
        for t in range(NTT):
            b = t % 2
            s.wait_ge(S[f"dx{t % 2}"].sem, 16 * (t // 2 + 1))
            for di in range(DC):
                idx = t * DC + di
                if idx >= 2:
                    s.wait_ge(S["sqcons"].sem, idx - 1)
                s.activation(out=xsq[di % 2][:, :], in_=xbuf[b][:, di * TT:(di + 1) * TT],
                             func=AF.Square).then_inc(*S["sq"].inc(1))
            s.wait_ge(S["v1"].sem, t + 1)
            s.activation(out=rw[3], in_=rw[2], func=AF.Sqrt).then_inc(*S["s2"].inc(1))
        if has1b:
            s.wait_ge(S["dw"].sem, 96)
            for (h, qi, ms) in sched:
                for (m, p, kind, aux, dgi) in ms:
                    if p >= 4:
                        s.wait_ge(S["sav"].sem, p - 3)
                    if kind == "od":
                        s.wait_ge(S["ssc"].sem, p + 1)
                        s.activation(out=P_pool[p % 4][:, :], in_=pb[p % 4][:, :],
                                     func=AF.Exp, bias=bcc_sb[:, h * 64 + aux:h * 64 + aux + 1],
                                     scale=ISQ).then_inc(*S["sex"].inc(1))
                    else:
                        s.wait_ge(S["sdg"].sem, dgi + 1)
                        s.activation(out=P_pool[p % 4][:, :], in_=tmpS[dgi % 2][:, :],
                                     func=AF.Exp).then_inc(*S["sex"].inc(1))

    @block.tensor
    def _(tn):
        tn.wait_ge(S["dw"].sem, 96)
        tn.wait_ge(S["setup"].sem, 1)
        for wi, wsb in enumerate([wq_sb, wk_sb, wv_sb]):
            if wi:
                tn.wait_ge(S["vsu"].sem, wi)
            for di in range(DC):
                mm = tn.matmul(pb[5][0:1, 0:256], ones_bf[:, 0:1], wsb[:, di * 256:(di + 1) * 256],
                               start=(di == 0), stop=(di == DC - 1))
            mm.then_inc(*S["tsu"].inc(1))
        tn.wait_ge(S["vsu"].sem, 3)
        for wi in range(2):
            for fo in range(2):
                cidx = wi * 2 + fo
                tn.matmul(pb[5][:, cidx:cidx + 1],
                          wsrow[0:1, wi * 256 + fo * 128:wi * 256 + (fo + 1) * 128],
                          one1[0:1, 0:1], start=True, stop=True)
        tn.matmul(pb[5][:, 8:264], onesf[0:1, :], wsrow[0:1, 512:768], start=True, stop=True) \
            .then_inc(*S["tsu"].inc(1))
        tn.wait_ge(S["vsu"].sem, 4)
        for t in range(NTT):
            b = t % 2
            if t > 0:
                tn.wait_ge(S["evac"].sem, t)
            for di in range(DC):
                tn.wait_ge(S["sq"].sem, t * DC + di + 1)
                tn.matmul(pb[7][0:1, :], ones_bf[:, 0:1], xsq[di % 2][:, :],
                          start=(di == 0), stop=(di == DC - 1)) \
                    .then_inc(*S["sqcons"].inc(1))
                mm = tn.matmul(pb[6][0:1, :], ones_bf[:, 0:1], xbuf[b][:, di * TT:(di + 1) * TT],
                               start=(di == 0), stop=(di == DC - 1))
            mm.then_inc(*S["stats"].inc(1))
            tn.wait_ge(S["v2"].sem, t + 1)
            if t > 0:
                tn.wait_ge(S["evac"].sem, t)
            tn.matmul(pb[6][:, :], onesf[0:1, :], rw[3], start=True, stop=True)
            tn.matmul(pb[7][:, :], onesf[0:1, :], rw[1], start=True, stop=True)
            for sub in range(4):
                tn.matmul(pb[5][:, sub:sub + 1], rw[3][0:1, sub * 128:(sub + 1) * 128],
                          one1[0:1, 0:1], start=True, stop=True)
                mm = tn.matmul(pb[5][:, 4 + sub:5 + sub], rw[1][0:1, sub * 128:(sub + 1) * 128],
                               one1[0:1, 0:1], start=True, stop=True)
            mm.then_inc(*S["bc"].inc(1))
            tn.wait_ge(S["evac"].sem, t + 1)
            if t > 0:
                tn.wait_ge(S["qkfix"].sem, t)
            for pbo, wsb in ((0, wq_sb), (2, wk_sb)):
                for fo in range(2):
                    for di in range(DC):
                        mm = tn.matmul(pb[pbo + fo][:, :],
                                       wsb[:, di * 256 + fo * 128:di * 256 + (fo + 1) * 128],
                                       xbuf[b][:, di * TT:(di + 1) * TT],
                                       start=(di == 0), stop=(di == DC - 1))
            mm.then_inc(*S["qk"].inc(1))
            for rnd in range(2):
                if t > 0 or rnd > 0:
                    tn.wait_ge(S["vevac"].sem, 2 * t + rnd)
                mm = None
                for si in range(2):
                    sub = 2 * rnd + si
                    for di in range(DC):
                        mm = tn.matmul(pb[4][:, si * 256:(si + 1) * 256],
                                       xbuf[b][:, di * TT + sub * 128:di * TT + (sub + 1) * 128],
                                       wv_sb[:, di * 256:(di + 1) * 256],
                                       start=(di == 0), stop=(di == DC - 1))
                mm.then_inc(*S["vmm"].inc(1))
        if has1b:
            tn.wait_ge(S["qkfix"].sem, NTT)
            tn.wait_ge(S["vevac"].sem, 2 * NTT)
            grp = 0
            n_od_ev = 0
            for (h, qi, ms) in sched:
                has_od = qi > 0
                if has_od:
                    n_od_ev += 1
                for (m, p, kind, aux, dgi) in ms:
                    if p >= 4:
                        tn.wait_ge(S["sex"].sem, p - 3)
                    tn.matmul(pb[p % 4][:, :], K_sb[:, h * T + m * 128:h * T + (m + 1) * 128],
                              Q_sb[:, h * T + qi * TT:h * T + (qi + 1) * TT],
                              start=True, stop=True).then_inc(*S["ssc"].inc(1))
                    first_od = kind == "od" and m == 0
                    first_dg = kind == "dg" and aux == 0
                    last_od = kind == "od" and m == 4 * qi - 1
                    last_dg = kind == "dg" and aux == 3
                    if first_od or (first_dg and not has_od):
                        tn.wait_ge(S["scmb"].sem, grp)
                    if first_dg and has_od:
                        tn.wait_ge(S["sod"].sem, n_od_ev)
                    tn.wait_ge(S["sex"].sem, p + 1)
                    mm = None
                    for sub in range(4):
                        mm = tn.matmul(pb[4 + sub][:, 0:129],
                                       P_pool[p % 4][:, sub * 128:(sub + 1) * 128],
                                       V_sb[:, h * 64 * VW + m * VW:h * 64 * VW + m * VW + 129],
                                       start=(first_od or first_dg), stop=(last_od or last_dg))
                    mm.then_inc(*S["sav"].inc(1))
                grp += 1

    @block.vector
    def _(v):
        v.memset(ones_bf[:, :], 1.0)
        v.memset(onesf[:, :], 1.0)
        v.memset(one1[:, :], 1.0)
        v.memset(V_sb[:, :], 1.0).then_inc(*S["setup"].inc(1))
        for wi in range(3):
            v.wait_ge(S["tsu"].sem, wi + 1)
            if wi < 2:
                v.tensor_scalar_mul(wsrow[0:1, wi * 256:(wi + 1) * 256], pb[5][0:1, 0:256],
                                    -1.0).then_inc(*S["vsu"].inc(1))
            else:
                v.tensor_copy(wsrow[0:1, wi * 256:(wi + 1) * 256],
                              pb[5][0:1, 0:256]).then_inc(*S["vsu"].inc(1))
        v.wait_ge(S["tsu"].sem, 4)
        v.tensor_copy(negws[:, 0:4], pb[5][:, 0:4])
        v.tensor_copy(wvb[:, :], pb[5][:, 8:264]).then_inc(*S["vsu"].inc(1))
        for t in range(NTT):
            b2 = t % 2
            v.wait_ge(S["stats"].sem, t + 1)
            v.tensor_scalar_mul(rw[0], pb[6][0:1, :], 1.0 / D)
            v.tensor_scalar_mul(rw[1], pb[7][0:1, :], 1.0 / D)
            v.tensor_mul(rw[2], rw[0], rw[0])
            v.tensor_sub(rw[3], rw[1], rw[2])
            v.tensor_scalar_add(rw[1], rw[3], float(EPS))
            v.reciprocal(rw[2], rw[1]).then_inc(*S["v1"].inc(1))
            v.wait_ge(S["s2"].sem, t + 1)
            v.tensor_mul(rw[1], rw[0], rw[3]).then_inc(*S["v2"].inc(1))
            v.wait_ge(S["bc"].sem, t + 1)
            v.tensor_copy(a_b[:, :], pb[6][:, :])
            v.tensor_copy(b_b[:, :], pb[7][:, :])
            v.tensor_copy(colp[b2][:, 0:8], pb[5][:, 0:8]).then_inc(*S["evac"].inc(1))
            v.wait_ge(S["qk"].sem, t + 1)
            for pi, (pbo, dst, nwo) in enumerate([(0, Q_sb, 0), (2, K_sb, 2)]):
                for fo in range(2):
                    v.tensor_mul(tmpf[fo][:, :], pb[pbo + fo][:, :], a_b[:, :])
                    last = v.scalar_tensor_tensor(
                        dst[:, fo * T + t * TT:fo * T + (t + 1) * TT],
                        b_b[:, :], negws[:, nwo + fo:nwo + fo + 1], tmpf[fo][:, :],
                        ALU.mult, ALU.add)
            last.then_inc(*S["qkfix"].inc(1))
            for rnd in range(2):
                v.wait_ge(S["vmm"].sem, 2 * t + rnd + 1)
                for si in range(2):
                    sub = 2 * rnd + si
                    g64 = t * 4 + sub
                    v.tensor_scalar(tmpv[0][:, :], wvb[:, :],
                                    colp[b2][:, 4 + sub:5 + sub], None, ALU.mult)
                    v.tensor_scalar(tmpv[1][:, :], pb[4][:, si * 256:(si + 1) * 256],
                                    colp[b2][:, sub:sub + 1], None, ALU.mult)
                    for fo in range(2):
                        last = v.tensor_sub(
                            V_sb[:, fo * 64 * VW + g64 * VW:fo * 64 * VW + g64 * VW + 128],
                            tmpv[1][:, fo * 128:(fo + 1) * 128],
                            tmpv[0][:, fo * 128:(fo + 1) * 128])
                last.then_inc(*S["vevac"].inc(1))
        if has1b:
            v.wait_ge(S["dw"].sem, 96)
            grp = 0
            n_od_ev = 0
            oid = 0
            for (h, qi, ms) in sched:
                has_od = qi > 0
                for (m, p, kind, aux, dgi) in ms:
                    if kind == "dg":
                        v.wait_ge(S["ssc"].sem, p + 1)
                        if dgi >= 2:
                            v.wait_ge(S["sex"].sem, diag_pass_of[dgi - 2] + 1)
                        v.scalar_tensor_tensor(
                            tmpS[dgi % 2][:, :], pb[p % 4][:, :], ISQ,
                            md_sb[:, (h * 4 + aux) * TT:(h * 4 + aux + 1) * TT],
                            ALU.mult, ALU.add).then_inc(*S["sdg"].inc(1))
                    if kind == "od" and m == 4 * qi - 1:
                        v.wait_ge(S["sav"].sem, p + 1)
                        for sub in range(4):
                            last = v.tensor_copy(odsb[:, sub * VW:sub * VW + 129],
                                                 pb[4 + sub][:, 0:129])
                        n_od_ev += 1
                        last.then_inc(*S["sod"].inc(1))
                p_last = ms[-1][1]
                v.wait_ge(S["sav"].sem, p_last + 1)
                for sub in range(4):
                    if has_od:
                        v.tensor_scalar(cmb[:, sub * VW:sub * VW + 129],
                                        odsb[:, sub * VW:sub * VW + 129],
                                        ec_sb[:, h * 4 + sub:h * 4 + sub + 1], None, ALU.mult)
                        last = v.tensor_add(cmb[:, sub * VW:sub * VW + 129],
                                            cmb[:, sub * VW:sub * VW + 129],
                                            pb[4 + sub][:, 0:129])
                    else:
                        last = v.tensor_copy(cmb[:, sub * VW:sub * VW + 129],
                                             pb[4 + sub][:, 0:129])
                last.then_inc(*S["scmb"].inc(1))
                for sub in range(4):
                    v.reciprocal(rcp[:, sub:sub + 1], cmb[:, sub * VW + 128:sub * VW + 129])
                for sub in range(4):
                    slot = oid % 8
                    if oid >= 8:
                        v.wait_ge(S["soc"].sem, 16 * (oid - 7))
                    v.tensor_scalar(outp[:, slot * 128:(slot + 1) * 128],
                                    cmb[:, sub * VW:sub * VW + 128],
                                    rcp[:, sub:sub + 1], None, ALU.mult) \
                        .then_inc(*S["sot"].inc(1))
                    oid += 1
                grp += 1

    @block.sync
    def _(sy):
        if has1b:
            oid = 0
            for (h, qi, ms) in sched:
                for sub in range(4):
                    slot = oid % 8
                    sy.wait_ge(S["sot"].sem, oid + 1)
                    sy.dma_start(out=cc_in[qi * TT + sub * 128:qi * TT + (sub + 1) * 128,
                                           h * 128:(h + 1) * 128],
                                 in_=outp[:, slot * 128:(slot + 1) * 128]) \
                        .then_inc(*S["soc"].inc(16))
                    oid += 1
        if debug:
            sy.wait_ge(S["qkfix"].sem, NTT)
            sy.wait_ge(S["vevac"].sem, 2 * NTT)
            sy.dma_start(out=dbg["qT"], in_=Q_sb[:, :]).then_inc(*S["yst"].inc(16))
            sy.dma_start(out=dbg["kT"], in_=K_sb[:, :]).then_inc(*S["yst"].inc(16))
            sy.dma_start(out=dbg["v"], in_=V_sb[:, :]).then_inc(*S["yst"].inc(16))

    p1.close()
    p2 = ExitStack()
    sb2 = lambda name, shape, dt=F32: p2.enter_context(nc.sbuf_tensor(name, shape, dt))
    if has2:
        Y_sb = sb2("Y_sb", [128, DC * TPC])
        rows2 = sb2("rows2", [1, 4 * TT])
        rw2 = [rows2[0:1, i * TT:(i + 1) * TT] for i in range(4)]
        m2b = sb2("m2b", [128, TT])
        r2b = sb2("r2b", [128, TT])
        onesf2 = sb2("onesf2", [1, 128])
        ones_bf2 = sb2("ones_bf2", [128, 8], BF16)
    p2a = ExitStack()
    if has1b:
        outT_sb = p2a.enter_context(nc.sbuf_tensor("outT_sb", [128, DC * TPC], BF16))

        @block.sync
        def _(sy):
            rid = sy.partition_id()
            sy.wait_ge(S["scc"].sem, 1)
            cc4 = cc_out.rearrange("(r q) m -> r q m", q=T)
            stg = cc_stage.rearrange("(r q) m -> r q m", q=TPC)
            sy.dma_start(out=stg, in_=cc4[:, ts(rid, TPC), :]).then_inc(*S["rcv"].inc(16))
            sy.wait_ge(S["rcv"].sem, 16)
            for fc in range(DC):
                r, cb = fc // 2, fc % 2
                sy.dma_start_transpose(
                    outT_sb[:, fc * TPC:(fc + 1) * TPC],
                    cc_stage[r * TPC:(r + 1) * TPC, cb * 128:(cb + 1) * 128]) \
                    .then_inc(*S["rcv"].inc(16))
            sy.wait_ge(S["rcv"].sem, 16 * 17)
            if debug:
                sy.dma_start(out=dbg["attnT"], in_=outT_sb[:, :]).then_inc(*S["yst"].inc(16))


    if has2:
        woT_sb = p2a.enter_context(nc.sbuf_tensor("woT_sb", [128, DC * D], BF16))
        xow = [p2a.enter_context(nc.sbuf_tensor(f"xow{i}", [128, TT], F32)) for i in range(2)]

        NWO = 32

        @block.vector
        def _(v):
            v.wait_ge(S["soc"].sem, 128 * 16)
            if debug:
                v.wait_ge(S["yst"].sem, 48)
            v.memset(onesf2[:, :], 1.0)
            v.memset(ones_bf2[:, :], 1.0).then_inc(*S["setup"].inc(1))
            for idx in range(NWO):
                v.wait_ge(S["swo"].sem, idx + 1)
                v.wait_ge(S[f"dxo{idx % 2}"].sem, 16 * (idx // 2 + 1))
                fo, tt = idx // 2, idx % 2
                v.tensor_add(Y_sb[:, fo * TPC + tt * TT:fo * TPC + (tt + 1) * TT],
                             pb[idx % 2][:, :], xow[idx % 2][:, :]) \
                    .then_inc(*S["y1e"].inc(1))

        @block.gpsimd
        def _(g):
            if debug:
                g.wait_ge(S["yst"].sem, 48)
            for i in range(4):
                g.dma_start(out=woT_sb[:, i * 4 * D:(i + 1) * 4 * D],
                            in_=wo_st[:, i * 4 * D:(i + 1) * 4 * D]).then_inc(*S["dwo"].inc(16))

        @block.sync
        def _(sy):
            for idx in range(NWO):
                if idx >= 2:
                    sy.wait_ge(S["y1e"].sem, idx - 1)
                sy.dma_start(out=xow[idx % 2][:, :], in_=xown_st[idx]) \
                    .then_inc(*S[f"dxo{idx % 2}"].inc(16))

        @block.tensor
        def _(tn):
            tn.wait_ge(S["dwo"].sem, 64)
            tn.wait_ge(S["rcv"].sem, 16 * 17)
            tn.wait_ge(S["setup"].sem, 2)
            for idx in range(NWO):
                fo, tt = idx // 2, idx % 2
                if idx >= 2:
                    tn.wait_ge(S["y1e"].sem, idx - 1)
                for di in range(DC):
                    mm = tn.matmul(pb[idx % 2][:, :],
                                   woT_sb[:, di * D + fo * 128:di * D + (fo + 1) * 128],
                                   outT_sb[:, di * TPC + tt * TT:di * TPC + (tt + 1) * TT],
                                   start=(di == 0), stop=(di == DC - 1))
                mm.then_inc(*S["swo"].inc(1))

        p2a.close()
        xn2_sb = sb2("xn2_sb", [128, DC * TPC], BF16)
        act_sb = sb2("act_sb", [128, FSC * TPC], BF16)
        gu_buf = [sb2(f"gu_buf{i}", [128, DC * 256], BF16) for i in range(2)]
        dn_buf = [sb2(f"dn_buf{i}", [128, FSC * 128], BF16) for i in range(2)]
        gate_sb = sb2("gate_sb", [128, DC * 8], BF16)
        idn_sb = sb2("idn_sb", [128, 128])
        sel_sb = sb2("sel_sb", [8, 8 * 128])
        cT_sb = sb2("cT_sb", [8, TPC])
        ybf = [sb2(f"ybf{i}", [128, TT], BF16) for i in range(2)]
        ysq = [sb2(f"ysq{i}", [128, TT], BF16) for i in range(2)]
        sg_sb = [sb2(f"sg_sb{i}", [128, TT], BF16) for i in range(2)]
        u1_sb = [sb2(f"u1_sb{i}", [128, TT], BF16) for i in range(2)]
        tmp2 = [sb2("tmp2_0", [128, TT])] * 2
        cbt_ = [m2b, r2b]
        gsc = sb2("gsc", [128, 64])
        gs2 = sb2("gs2", [128, 64])
        gct = sb2("gct", [128, 64])
        gtA = sb2("gtA", [128, 64])
        gtB = sb2("gtB", [128, 64])
        gcl = sb2("gcl", [128, 64])

        JCs = [FC] * E8 + [FSC]
        jgl = []
        jj = 0
        for e in range(9):
            for j in range(JCs[e]):
                jgl.append((e, j, jj)); jj += 1
        TJC = jj
        hgl = []
        hh = 0
        for e in range(9):
            for h in range(DC):
                hgl.append((e, h, hh)); hh += 1
        THC = hh

        @block.gpsimd
        def _(g):
            g.wait_ge(S["y1e"].sem, 32)
            g.dma_start(out=gate_sb[:, :], in_=gate_st).then_inc(*S["dwo"].inc(16))
            g.dma_start(out=idn_sb[:, :], in_=idn_st).then_inc(*S["dwo"].inc(16))
            g.dma_start(out=sel_sb[:, :], in_=sel_st).then_inc(*S["dwo"].inc(16))
            for e in range(9):
                for j in range(JCs[e]):
                    jjx = sum(JCs[:e]) + j
                    if jjx >= 2:
                        g.wait_ge(S["gups"].sem, 4 * (jjx - 1))
                    src = gu_st[e, j] if e < E8 else sgu_st[j]
                    g.dma_start(out=gu_buf[jjx % 2][:, :], in_=src) \
                        .then_inc(*S[f"dgu{jjx % 2}"].inc(16))
                for h in range(DC):
                    hhx = e * DC + h
                    if hhx >= 2:
                        g.wait_ge(S["dnps"].sem, 2 * (hhx - 1))
                    if e < E8:
                        srcd = dn_st[e, h]
                        g.dma_start(out=dn_buf[hhx % 2][:, 0:FC * 128], in_=srcd) \
                            .then_inc(*S[f"ddn{hhx % 2}"].inc(16))
                    else:
                        g.dma_start(out=dn_buf[hhx % 2][:, :], in_=sdn_st[h]) \
                            .then_inc(*S[f"ddn{hhx % 2}"].inc(16))

        @block.scalar
        def _(s):
            for tt in range(2):
                for di in range(DC):
                    idx = tt * DC + di
                    s.wait_ge(S["v21"].sem, idx + 1)
                    if idx >= 2:
                        s.wait_ge(S["st2"].sem, 2 * (idx - 1))
                    s.activation(out=ysq[di % 2][:, :], in_=ybf[di % 2][:, :],
                                 func=AF.Square).then_inc(*S["s21"].inc(1))
                s.wait_ge(S["v22"].sem, tt + 1)
                s.activation(out=rw2[3], in_=rw2[2], func=AF.Sqrt).then_inc(*S["s22"].inc(1))
            s.wait_ge(S["g1"].sem, 16)
            for g8 in range(8):
                s.activation(out=gsc[:, 8 * g8:8 * g8 + 8], in_=pb[5][:, 8 * g8:8 * g8 + 8],
                             func=AF.Exp, bias=gcl[:, 8 + g8:9 + g8], scale=1.0) \
                    .then_inc(*S["g2"].inc(1))
            for (e, j, jjx) in jgl:
                for tt in range(2):
                    s.wait_ge(S["gups"].sem, 4 * jjx + 2 * tt + 1)
                    s.activation(out=sg_sb[tt][:, :], in_=pb[2 * tt][:, :],
                                 func=AF.Silu).then_inc(*S["sil"].inc(1))

        @block.tensor
        def _(tn):
            for tt in range(2):
                for di in range(DC):
                    idx = tt * DC + di
                    tn.wait_ge(S["s21"].sem, idx + 1)
                    if tt > 0 and di == 0:
                        tn.wait_ge(S["ev2"].sem, tt)
                    tn.matmul(pb[7][0:1, :], ones_bf2[:, 0:1], ysq[di % 2][:, :],
                              start=(di == 0), stop=(di == DC - 1)) \
                        .then_inc(*S["st2"].inc(1))
                    mm = tn.matmul(pb[6][0:1, :], ones_bf2[:, 0:1], ybf[di % 2][:, :],
                                   start=(di == 0), stop=(di == DC - 1)) \
                        .then_inc(*S["st2"].inc(1))
                tn.wait_ge(S["v23"].sem, tt + 1)
                tn.matmul(pb[6][:, :], onesf2[0:1, :], rw2[0], start=True, stop=True)
                tn.matmul(pb[7][:, :], onesf2[0:1, :], rw2[3], start=True, stop=True) \
                    .then_inc(*S["bc2"].inc(1))
            tn.wait_ge(S["xn2d"].sem, 32)
            tn.wait_ge(S["dwo"].sem, 112)
            for g8 in range(8):
                for di in range(DC):
                    mm = tn.matmul(pb[5][:, 8 * g8:8 * g8 + 8],
                                   xn2_sb[:, di * TPC + g8 * 128:di * TPC + (g8 + 1) * 128],
                                   gate_sb[:, di * 8:(di + 1) * 8],
                                   start=(di == 0), stop=(di == DC - 1))
                mm.then_inc(*S["glog"].inc(1))
            tn.wait_ge(S["gcmb"].sem, 8)
            for rnd in range(2):
                if rnd:
                    tn.wait_ge(S["gsel"].sem, rnd)
                for gi in range(4):
                    g8 = rnd * 4 + gi
                    tn.transpose(pb[4][0:8, gi * 128:(gi + 1) * 128],
                                 gct[:, 8 * g8:8 * g8 + 8], idn_sb[:, :]) \
                        .then_inc(*S["gtr"].inc(1))
            for e in range(9):
                if e < E8:
                    if e > 0:
                        tn.wait_ge(S["dnac"].sem, 32 * e)
                    tn.wait_ge(S["gsel"].sem, 2)
                    for tt in range(2):
                        tn.matmul(pb[4 + tt][:, :], sel_sb[0:8, e * 128:(e + 1) * 128],
                                  cT_sb[0:8, tt * TT:(tt + 1) * TT], start=True, stop=True) \
                            .then_inc(*S["cbt"].inc(1))
                for j in range(JCs[e]):
                    jjx = sum(JCs[:e]) + j
                    tn.wait_ge(S[f"dgu{jjx % 2}"].sem, 16 * (jjx // 2 + 1))
                    if e < E8 and j == 0:
                        tn.wait_ge(S["cbk"].sem, 2 * (e + 1))
                    for tt in range(2):
                        if jjx >= 1:
                            tn.wait_ge(S["sil"].sem, 2 * (jjx - 1) + tt + 1)
                            tn.wait_ge(S["uact"].sem, 2 * (jjx - 1) + tt + 1)
                        for di in range(DC):
                            mm = tn.matmul(pb[2 * tt][:, :],
                                           gu_buf[jjx % 2][:, di * 256:di * 256 + 128],
                                           xn2_sb[:, di * TPC + tt * TT:di * TPC + (tt + 1) * TT],
                                           start=(di == 0), stop=(di == DC - 1))
                        mm.then_inc(*S["gups"].inc(1))
                        for di in range(DC):
                            mm = tn.matmul(pb[2 * tt + 1][:, :],
                                           gu_buf[jjx % 2][:, di * 256 + 128:di * 256 + 256],
                                           xn2_sb[:, di * TPC + tt * TT:di * TPC + (tt + 1) * TT],
                                           start=(di == 0), stop=(di == DC - 1))
                        mm.then_inc(*S["gups"].inc(1))
                JC = JCs[e]
                for h in range(DC):
                    hhx = e * DC + h
                    tn.wait_ge(S[f"ddn{hhx % 2}"].sem, 16 * (hhx // 2 + 1))
                    tn.wait_ge(S["uact"].sem, 2 * (sum(JCs[:e]) + JC))
                    for tt in range(2):
                        slot = 2 * hhx + tt
                        if slot >= 2:
                            tn.wait_ge(S["dnac"].sem, slot - 1)
                        for fj in range(JC):
                            mm = tn.matmul(pb[4 + tt][:, :],
                                           dn_buf[hhx % 2][:, fj * 128:(fj + 1) * 128],
                                           act_sb[:, fj * TPC + tt * TT:fj * TPC + (tt + 1) * TT],
                                           start=(fj == 0), stop=(fj == JC - 1))
                        mm.then_inc(*S["dnps"].inc(1))

        @block.vector
        def _(v):
            v.wait_ge(S["y1e"].sem, 32)
            for tt in range(2):
                for di in range(DC):
                    idx = tt * DC + di
                    if idx >= 2:
                        v.wait_ge(S["s21"].sem, idx - 1)
                    v.tensor_copy(ybf[di % 2][:, :],
                                  Y_sb[:, di * TPC + tt * TT:di * TPC + (tt + 1) * TT]) \
                        .then_inc(*S["v21"].inc(1))
                v.wait_ge(S["st2"].sem, 32 * (tt + 1))
                v.tensor_scalar_mul(rw2[0], pb[6][0:1, :], 1.0 / D)
                v.tensor_scalar_mul(rw2[1], pb[7][0:1, :], 1.0 / D)
                v.tensor_mul(rw2[2], rw2[0], rw2[0])
                v.tensor_sub(rw2[3], rw2[1], rw2[2])
                v.tensor_scalar_add(rw2[1], rw2[3], float(EPS))
                v.reciprocal(rw2[2], rw2[1]).then_inc(*S["v22"].inc(1))
                v.wait_ge(S["s22"].sem, tt + 1)
                v.engine_nop().then_inc(*S["v23"].inc(1))
                v.wait_ge(S["bc2"].sem, tt + 1)
                v.tensor_copy(m2b[:, :], pb[6][:, :])
                v.tensor_copy(r2b[:, :], pb[7][:, :]).then_inc(*S["ev2"].inc(1))
                for di in range(DC):
                    v.tensor_sub(tmp2[di % 2][:, :],
                                 Y_sb[:, di * TPC + tt * TT:di * TPC + (tt + 1) * TT], m2b[:, :])
                    v.tensor_mul(xn2_sb[:, di * TPC + tt * TT:di * TPC + (tt + 1) * TT],
                                 tmp2[di % 2][:, :], r2b[:, :]).then_inc(*S["xn2d"].inc(1))
            v.wait_ge(S["glog"].sem, 8)
            for g8 in range(8):
                v.reduce_max(gcl[:, g8:g8 + 1], pb[5][:, 8 * g8:8 * g8 + 8], axis=AX.X)
            for g8 in range(8):
                v.tensor_scalar_mul(gcl[:, 8 + g8:9 + g8], gcl[:, g8:g8 + 1], -1.0)
            v.engine_nop().then_inc(*S["g1"].inc(16))
            for g8 in range(8):
                v.wait_ge(S["g2"].sem, g8 + 1)
                v.reduce_sum(gcl[:, 16 + g8:17 + g8], gsc[:, 8 * g8:8 * g8 + 8], axis=AX.X)
            for g8 in range(8):
                v.reciprocal(gcl[:, 24 + g8:25 + g8], gcl[:, 16 + g8:17 + g8])
            for g8 in range(8):
                v.tensor_scalar(gs2[:, 8 * g8:8 * g8 + 8], gsc[:, 8 * g8:8 * g8 + 8],
                                gcl[:, 24 + g8:25 + g8], None, ALU.mult)
            for g8 in range(8):
                v.reduce_max(gcl[:, 32 + g8:33 + g8], gs2[:, 8 * g8:8 * g8 + 8], axis=AX.X)
            for g8 in range(8):
                v.tensor_scalar(gtA[:, 8 * g8:8 * g8 + 8], gs2[:, 8 * g8:8 * g8 + 8],
                                gcl[:, 32 + g8:33 + g8], None, ALU.is_equal)
            for g8 in range(8):
                v.scalar_tensor_tensor(gtB[:, 8 * g8:8 * g8 + 8], gtA[:, 8 * g8:8 * g8 + 8],
                                       -1e9, gs2[:, 8 * g8:8 * g8 + 8], ALU.mult, ALU.add)
            for g8 in range(8):
                v.reduce_max(gcl[:, 40 + g8:41 + g8], gtB[:, 8 * g8:8 * g8 + 8], axis=AX.X)
            for g8 in range(8):
                v.tensor_scalar(gtA[:, 8 * g8:8 * g8 + 8], gs2[:, 8 * g8:8 * g8 + 8],
                                gcl[:, 40 + g8:41 + g8], None, ALU.is_ge)
            for g8 in range(8):
                v.tensor_mul(gct[:, 8 * g8:8 * g8 + 8], gs2[:, 8 * g8:8 * g8 + 8],
                             gtA[:, 8 * g8:8 * g8 + 8]).then_inc(*S["gcmb"].inc(1))
            for rnd in range(2):
                v.wait_ge(S["gtr"].sem, 4 * (rnd + 1))
                v.tensor_copy(cT_sb[0:8, rnd * TT:(rnd + 1) * TT], pb[4][0:8, 0:512]) \
                    .then_inc(*S["gsel"].inc(1))
            for e in range(9):
                if e < E8:
                    for tt in range(2):
                        v.wait_ge(S["cbt"].sem, 2 * e + tt + 1)
                        v.tensor_copy(cbt_[tt][:, :], pb[4 + tt][:, :]) \
                            .then_inc(*S["cbk"].inc(1))
                for j in range(JCs[e]):
                    jjx = sum(JCs[:e]) + j
                    for tt in range(2):
                        v.wait_ge(S["gups"].sem, 4 * jjx + 2 * tt + 2)
                        v.wait_ge(S["sil"].sem, 2 * jjx + tt + 1)
                        if e < E8:
                            v.tensor_mul(u1_sb[tt][:, :], pb[2 * tt + 1][:, :],
                                         cbt_[tt][:, :])
                            v.tensor_mul(act_sb[:, j * TPC + tt * TT:j * TPC + (tt + 1) * TT],
                                         u1_sb[tt][:, :], sg_sb[tt][:, :]) \
                                .then_inc(*S["uact"].inc(1))
                        else:
                            v.tensor_mul(act_sb[:, j * TPC + tt * TT:j * TPC + (tt + 1) * TT],
                                         pb[2 * tt + 1][:, :], sg_sb[tt][:, :]) \
                                .then_inc(*S["uact"].inc(1))
                for h in range(DC):
                    hhx = e * DC + h
                    for tt in range(2):
                        v.wait_ge(S["dnps"].sem, 2 * hhx + tt + 1)
                        v.tensor_add(Y_sb[:, h * TPC + tt * TT:h * TPC + (tt + 1) * TT],
                                     Y_sb[:, h * TPC + tt * TT:h * TPC + (tt + 1) * TT],
                                     pb[4 + tt][:, :]).then_inc(*S["dnac"].inc(1))

        @block.sync
        def _(sy):
            if debug:
                sy.wait_ge(S["xn2d"].sem, 32)
                sy.dma_start(out=dbg["xn2"], in_=xn2_sb[:, :]).then_inc(*S["yst"].inc(16))
                sy.wait_ge(S["gsel"].sem, 2)
                sy.dma_start(out=dbg["comb"], in_=cT_sb[0:8, :]).then_inc(*S["yst"].inc(16))
                sy.dma_start(out=dbg["gsc"], in_=gsc[:, :]).then_inc(*S["yst"].inc(16))
                sy.dma_start(out=dbg["gs2"], in_=gs2[:, :]).then_inc(*S["yst"].inc(16))
                sy.dma_start(out=dbg["gct"], in_=gct[:, :]).then_inc(*S["yst"].inc(16))
                sy.dma_start(out=dbg["gcl"], in_=gcl[:, :]).then_inc(*S["yst"].inc(16))
            sy.wait_ge(S["dnac"].sem, 32 * 9)
            y3 = yT.rearrange("(a p) m -> a p m", p=128)
            for di in range(DC):
                sy.dma_start(out=y3[di], in_=Y_sb[:, di * TPC:(di + 1) * TPC]) \
                    .then_inc(*S["yst"].inc(16))

    @block.sync
    def _(sy):
        sy.wait_ge(S["yst"].sem, S["yst"].v)

    if not has1b:
        p1.close()
    else:
        if not has2:
            p2a.close()
        p2.close()
    ctx.close()
    return nc



def stage_inputs(inp, phase="full"):
    x = np.asarray(inp["L0_attn_x"]).reshape(T, D)
    Wq = np.asarray(inp["L0_attn_Wq"])
    Wk = np.asarray(inp["L0_attn_Wk"])
    Wv = np.asarray(inp["L0_attn_Wv"])

    x_st = np.ascontiguousarray(
        x.reshape(NTT, TT, DC, 128).transpose(0, 3, 2, 1).reshape(NTT, 128, DC * TT))

    def wslice(W, c):
        hA, hB = head_pair(c)
        WT = W.T
        wc = np.concatenate([WT[:, 128 * hA:128 * (hA + 1)], WT[:, 128 * hB:128 * (hB + 1)]], axis=1)
        return np.ascontiguousarray(wc.reshape(DC, 128, 256).transpose(1, 0, 2).reshape(128, DC * 256))

    in_maps = []
    for c in range(NC):
        hA, hB = head_pair(c)
        sl = (slope(hA), slope(hB))
        kl = np.arange(128, dtype=np.float64)[:, None]
        ql = np.arange(TT, dtype=np.float64)[None, :]
        md = np.empty((128, 8 * TT), np.float32)
        for hi in range(2):
            for j in range(4):
                mm = sl[hi] * (128 * j + kl - ql)
                mm = np.where(128 * j + kl <= ql, mm, NEG_BIG)
                md[:, (hi * 4 + j) * TT:(hi * 4 + j + 1) * TT] = mm.astype(np.float32)
        bc = np.zeros((128, 128), np.float32)
        for hi in range(2):
            for dl in range(64):
                bc[:, hi * 64 + dl] = (sl[hi] * (kl[:, 0] - 128.0 * dl)).astype(np.float32)
        ec = np.zeros((128, 8), np.float32)
        for hi in range(2):
            for sub in range(4):
                ec[:, hi * 4 + sub] = np.exp(-sl[hi] * (128.0 * sub + kl[:, 0])).astype(np.float32)
        m = {
            "x_st": x_st, "wq_st": wslice(Wq, c), "wk_st": wslice(Wk, c), "wv_st": wslice(Wv, c),
            "md_st": md, "bc_st": bc, "ec_st": ec,
        }
        if phase == "full":
            m.update(stage_phase2(inp, c))
        in_maps.append(m)
    return in_maps


def stage_phase2(inp, c):
    Wo = np.asarray(inp["L0_attn_Wo"])
    gate = np.asarray(inp["L0_ffn_gate_weight"])
    gu = np.asarray(inp["L0_ffn_gate_up_weights"])
    dn = np.asarray(inp["L0_ffn_down_weights"])
    sgu = np.asarray(inp["L0_ffn_shared_gate_up"])
    sdn = np.asarray(inp["L0_ffn_shared_down"])
    x = np.asarray(inp["L0_attn_x"]).reshape(T, D)

    perm = []
    for r in range(NC):
        ha, hb = head_pair(r)
        perm.extend(range(128 * ha, 128 * ha + 128))
        perm.extend(range(128 * hb, 128 * hb + 128))
    woT = Wo.T[np.asarray(perm), :]
    wo_st = np.ascontiguousarray(woT.reshape(DC, 128, D).transpose(1, 0, 2).reshape(128, DC * D))
    xT = x.T
    xown = xT[:, TPC * c:TPC * (c + 1)]
    xown_st = np.ascontiguousarray(
        xown.reshape(DC, 128, 2, TT).transpose(0, 2, 1, 3).reshape(32, 128, TT))
    gate_st = np.ascontiguousarray(gate.T.reshape(DC, 128, 8).transpose(1, 0, 2).reshape(128, DC * 8))
    idn = np.eye(128, dtype=np.float32)
    sel = np.zeros((8, 8 * 128), np.float32)
    for e in range(8):
        sel[e, e * 128:(e + 1) * 128] = 1.0
    gug = gu[:, :F, :].reshape(E8, FC, 128, DC, 128)
    guu = gu[:, F:, :].reshape(E8, FC, 128, DC, 128)
    gu_st = np.empty((E8, FC, 128, DC * 256), np.float32)
    gg = gug.transpose(0, 1, 4, 3, 2)
    uu = guu.transpose(0, 1, 4, 3, 2)
    gu_st = np.ascontiguousarray(
        np.stack([gg, uu], axis=4).reshape(E8, FC, 128, DC, 2 * 128).reshape(E8, FC, 128, DC * 256))
    dn_st = np.ascontiguousarray(
        dn.reshape(E8, DC, 128, FC, 128).transpose(0, 1, 4, 3, 2).reshape(E8, DC, 128, FC * 128))
    sgug = sgu[:FS, :].reshape(FSC, 128, DC, 128)
    sguu = sgu[FS:, :].reshape(FSC, 128, DC, 128)
    sgu_st = np.ascontiguousarray(
        np.stack([sgug.transpose(0, 3, 2, 1), sguu.transpose(0, 3, 2, 1)], axis=3)
        .reshape(FSC, 128, DC, 256).reshape(FSC, 128, DC * 256))
    sdn_st = np.ascontiguousarray(
        sdn.reshape(DC, 128, FSC, 128).transpose(0, 3, 2, 1).reshape(DC, 128, FSC * 128))
    return {
        "wo_st": wo_st, "xown_st": xown_st, "gate_st": gate_st,
        "idn_st": idn, "sel_st": sel,
        "gu_st": gu_st, "dn_st": dn_st, "sgu_st": sgu_st, "sdn_st": sdn_st,
    }



def kernel(**inputs):
    in_maps = stage_inputs(inputs, phase="full")
    nc = build_nc(phase="full", debug=False)
    res = run_bass_kernel_spmd(nc, in_maps, core_ids=list(range(NC)))
    out = np.empty((1, T, D), np.float32)
    for c in range(NC):
        out[0, TPC * c:TPC * (c + 1), :] = np.asarray(res.results[c]["yT"]).T
    return out


# version 04
# speedup vs baseline: 1.3173x; 1.3173x over previous
import sys
sys.path.insert(0, "/opt/trn_rl_repo")
import numpy as np
from contextlib import ExitStack
import concourse.bass as bass
import concourse.mybir as mybir
from concourse.bass_utils import run_bass_kernel_spmd

F32 = mybir.dt.float32
BF16 = mybir.dt.bfloat16
AF = mybir.ActivationFunctionType
ALU = mybir.AluOpType
AX = mybir.AxisListType
ts = bass.ts

T, D, NH, HD, NC = 8192, 2048, 16, 128, 8
TT = 512
NTT = T // TT
DC = D // 128
TPC = T // NC
E8, F, FS = 8, 1408, 2816
FC, FSC = F // 128, FS // 128
EPS = 1e-5
ISQ = float(1.0 / np.sqrt(HD))
VW = 136
NEG_BIG = -1e30


def head_pair(c):
    return c, 15 - c


def slope(h):
    return 2.0 ** (-(h + 1) / 2.0)


class Ctr:
    def __init__(self, nc, name, ctx):
        self.sem = ctx.enter_context(nc.semaphore(name))
        self.v = 0

    def inc(self, n=1):
        self.v += n
        return self.sem, n


def build_nc(phase="full", debug=False):
    nc = bass.Bass("TRN2", target_bir_lowering=False, debug=False, detect_race_conditions=False)
    ctx = ExitStack()
    dr = lambda name, shape, dt, kind="ExternalInput": nc.dram_tensor(name, shape, dt, kind=kind).ap()

    x_st = dr("x_st", [NTT, 128, DC * TT], F32)
    wq_st = dr("wq_st", [128, DC * 256], F32)
    wk_st = dr("wk_st", [128, DC * 256], F32)
    wv_st = dr("wv_st", [128, DC * 256], F32)
    md_st = dr("md_st", [128, 8 * TT], F32)
    bc_st = dr("bc_st", [128, 128], F32)
    ec_st = dr("ec_st", [128, 8], F32)
    has2 = phase == "full"
    if has2:
        wo_st = dr("wo_st", [128, DC * D], F32)
        xown_st = dr("xown_st", [32, 128, TT], F32)
        gate_st = dr("gate_st", [128, DC * 8], F32)
        idn_st = dr("idn_st", [128, 128], F32)
        sel_st = dr("sel_st", [8, 8 * 128], F32)
        gu_st = dr("gu_st", [E8, FC, 128, DC * 256], F32)
        dn_st = dr("dn_st", [E8, DC, 128, FC * 128], F32)
        sgu_st = dr("sgu_st", [FSC, 128, DC * 256], F32)
        sdn_st = dr("sdn_st", [DC, 128, FSC * 128], F32)
        yT = dr("yT", [D, TPC], F32, "ExternalOutput")
    has1b = phase in ("1b", "full")
    if has1b:
        cc_in = nc.dram_tensor("cc_in", [T, 2 * HD], BF16).ap()
        cc_out = nc.dram_tensor("cc_out", [NC * T, 2 * HD], BF16).ap()
        cc_stage = nc.dram_tensor("cc_stage", [NC * TPC, 2 * HD], BF16).ap()

    dbg = {}
    if debug:
        dbg["qT"] = dr("dbg_qT", [128, 2 * T], BF16, "ExternalOutput")
        dbg["kT"] = dr("dbg_kT", [128, 2 * T], BF16, "ExternalOutput")
        dbg["v"] = dr("dbg_v", [128, 2 * 64 * VW], BF16, "ExternalOutput")
        if has1b:
            dbg["attnT"] = dr("dbg_attnT", [128, DC * TPC], BF16, "ExternalOutput")
        if has2:
            dbg["y1"] = dr("dbg_y1", [128, DC * TPC], F32, "ExternalOutput")
            dbg["xn2"] = dr("dbg_xn2", [128, DC * TPC], BF16, "ExternalOutput")
            dbg["comb"] = dr("dbg_comb", [8, TPC], F32, "ExternalOutput")


    sbuf = lambda name, shape, dt=F32: ctx.enter_context(nc.sbuf_tensor(name, shape, dt))
    pb = [ctx.enter_context(nc.psum_tensor(f"pb{i}", [128, 512], F32)) for i in range(8)]

    S = {}
    for name in ["dx0", "dx1", "dw", "sq", "sqcons", "stats", "v1", "s2", "v2", "bc",
                 "evac", "qk", "qkfix", "vmm", "vevac", "xcons", "setup",
                 "tsu", "vsu", "p1a",
                 "ssc", "sdg", "sex", "sav", "sod", "scmb", "sot", "soc",
                 "scc", "rcv", "pb1",
                 "dwo", "swo", "svo", "y1e", "st2", "v21", "s21", "v22",
                 "bc2", "ev2", "xn2d", "glog", "gcmb", "gsel",
                 "dgu0", "dgu1", "ddn0", "ddn1", "gups", "sil", "uact", "dnps", "dnac",
                 "cbk", "cbt", "dxo0", "dxo1", "s22", "v23", "g1", "g2", "gtr",
                 "yst"]:
        S[name] = Ctr(nc, name, ctx)

    block = ctx.enter_context(nc.Block())

    p1 = ExitStack()
    sb1 = lambda name, shape, dt=F32: p1.enter_context(nc.sbuf_tensor(name, shape, dt))
    wq_sb = sb1("wq_sb", [128, DC * 256], BF16)
    wk_sb = sb1("wk_sb", [128, DC * 256], BF16)
    wv_sb = sb1("wv_sb", [128, DC * 256], BF16)
    xbuf = [sb1(f"xbuf{i}", [128, DC * TT], BF16) for i in range(2)]
    xsq = [sb1(f"xsq{i}", [128, TT], BF16) for i in range(2)]
    Q_sb = sb1("Q_sb", [128, 2 * T], BF16)
    K_sb = sb1("K_sb", [128, 2 * T], BF16)
    V_sb = sb1("V_sb", [128, 2 * 64 * VW], BF16)
    rows = sb1("rows", [1, 4 * TT])
    rw = [rows[0:1, i * TT:(i + 1) * TT] for i in range(4)]
    a_b = sb1("a_b", [128, TT])
    b_b = sb1("b_b", [128, TT])
    colp = [sb1(f"colp{i}", [128, 16]) for i in range(2)]
    negws = sb1("negws", [128, 8])
    wvb = sb1("wvb", [128, 256])
    ones_bf = sb1("ones_bf", [128, 8], BF16)
    onesf = sb1("onesf", [1, 128])
    one1 = sb1("one1", [1, 8])
    wsrow = sb1("wsrow", [1, 3 * 256])
    md_sb = sb1("md_sb", [128, 8 * TT])
    bcc_sb = sb1("bcc_sb", [128, 128])
    ec_sb = sb1("ec_sb", [128, 8])
    P_pool = [sb1(f"Pp{i}", [128, TT], BF16) for i in range(4)]
    tmpS = [sb1(f"tmpS{i}", [128, TT]) for i in range(2)]
    tmpf = tmpS
    tmpv = [tmpS[0][:, 0:256], tmpS[1][:, 0:256]]
    odsb = sb1("odsb", [128, 4 * VW])
    cmb = sb1("cmb", [128, 4 * VW])
    rcp = sb1("rcp", [128, 16])
    outp = sb1("outp", [128, 8 * 128], BF16)

    sched = []
    p_ctr = 0
    dgi_ctr = 0
    diag_pass_of = {}
    for qi in range(NTT):
        for h in range(2):
            ms = []
            for m in range(4 * qi + 4):
                if m < 4 * qi:
                    ms.append((m, p_ctr, "od", 4 * qi - m, None))
                else:
                    diag_pass_of[dgi_ctr] = p_ctr
                    ms.append((m, p_ctr, "dg", m - 4 * qi, dgi_ctr))
                    dgi_ctr += 1
                p_ctr += 1
            sched.append((h, qi, ms))
    NPASS = p_ctr

    @block.gpsimd
    def _(g):
        g.dma_start(out=wq_sb[:, :], in_=wq_st).then_inc(*S["dw"].inc(16))
        g.dma_start(out=wk_sb[:, :], in_=wk_st).then_inc(*S["dw"].inc(16))
        g.dma_start(out=wv_sb[:, :], in_=wv_st).then_inc(*S["dw"].inc(16))
        g.dma_start(out=md_sb[:, :], in_=md_st).then_inc(*S["dw"].inc(16))
        g.dma_start(out=bcc_sb[:, :], in_=bc_st).then_inc(*S["dw"].inc(16))
        g.dma_start(out=ec_sb[:, :], in_=ec_st).then_inc(*S["dw"].inc(16))
        for t in range(NTT):
            if t >= 2:
                g.wait_ge(S["vmm"].sem, 2 * t - 2)
            g.dma_start(out=xbuf[t % 2][:, :], in_=x_st[t]).then_inc(*S[f"dx{t % 2}"].inc(16))
        if has1b:
            for i in range(NC):
                g.wait_ge(S["soc"].sem, 16 * 8 * (2 * i + 2))
                g.collective_compute(
                    "AllGather", ALU.bypass, replica_groups=[list(range(NC))],
                    ins=[cc_in[TPC * i:TPC * (i + 1), :].opt()],
                    outs=[cc_out[NC * TPC * i:NC * TPC * (i + 1), :].opt()],
                ).then_inc(*S["scc"].inc(1))

    @block.scalar
    def _(s):
        for t in range(NTT):
            b = t % 2
            s.wait_ge(S[f"dx{t % 2}"].sem, 16 * (t // 2 + 1))
            for di in range(DC):
                idx = t * DC + di
                if idx >= 2:
                    s.wait_ge(S["sqcons"].sem, idx - 1)
                s.activation(out=xsq[di % 2][:, :], in_=xbuf[b][:, di * TT:(di + 1) * TT],
                             func=AF.Square).then_inc(*S["sq"].inc(1))
            s.wait_ge(S["v1"].sem, t + 1)
            s.activation(out=rw[3], in_=rw[2], func=AF.Sqrt).then_inc(*S["s2"].inc(1))
        if has1b:
            s.wait_ge(S["dw"].sem, 96)
            for (h, qi, ms) in sched:
                for (m, p, kind, aux, dgi) in ms:
                    if p >= 4:
                        s.wait_ge(S["sav"].sem, p - 3)
                    if kind == "od":
                        s.wait_ge(S["ssc"].sem, p + 1)
                        s.activation(out=P_pool[p % 4][:, :], in_=pb[p % 4][:, :],
                                     func=AF.Exp, bias=bcc_sb[:, h * 64 + aux:h * 64 + aux + 1],
                                     scale=ISQ).then_inc(*S["sex"].inc(1))
                    else:
                        s.wait_ge(S["sdg"].sem, dgi + 1)
                        s.activation(out=P_pool[p % 4][:, :], in_=tmpS[dgi % 2][:, :],
                                     func=AF.Exp).then_inc(*S["sex"].inc(1))

    @block.tensor
    def _(tn):
        tn.wait_ge(S["dw"].sem, 96)
        tn.wait_ge(S["setup"].sem, 1)
        for wi, wsb in enumerate([wq_sb, wk_sb, wv_sb]):
            if wi:
                tn.wait_ge(S["vsu"].sem, wi)
            for di in range(DC):
                mm = tn.matmul(pb[5][0:1, 0:256], ones_bf[:, 0:1], wsb[:, di * 256:(di + 1) * 256],
                               start=(di == 0), stop=(di == DC - 1))
            mm.then_inc(*S["tsu"].inc(1))
        tn.wait_ge(S["vsu"].sem, 3)
        for wi in range(2):
            for fo in range(2):
                cidx = wi * 2 + fo
                tn.matmul(pb[5][:, cidx:cidx + 1],
                          wsrow[0:1, wi * 256 + fo * 128:wi * 256 + (fo + 1) * 128],
                          one1[0:1, 0:1], start=True, stop=True)
        tn.matmul(pb[5][:, 8:264], onesf[0:1, :], wsrow[0:1, 512:768], start=True, stop=True) \
            .then_inc(*S["tsu"].inc(1))
        tn.wait_ge(S["vsu"].sem, 4)
        for t in range(NTT):
            b = t % 2
            if t > 0:
                tn.wait_ge(S["evac"].sem, t)
            for di in range(DC):
                tn.wait_ge(S["sq"].sem, t * DC + di + 1)
                tn.matmul(pb[7][0:1, :], ones_bf[:, 0:1], xsq[di % 2][:, :],
                          start=(di == 0), stop=(di == DC - 1)) \
                    .then_inc(*S["sqcons"].inc(1))
                mm = tn.matmul(pb[6][0:1, :], ones_bf[:, 0:1], xbuf[b][:, di * TT:(di + 1) * TT],
                               start=(di == 0), stop=(di == DC - 1))
            mm.then_inc(*S["stats"].inc(1))
            tn.wait_ge(S["v2"].sem, t + 1)
            if t > 0:
                tn.wait_ge(S["evac"].sem, t)
            tn.matmul(pb[6][:, :], onesf[0:1, :], rw[3], start=True, stop=True)
            tn.matmul(pb[7][:, :], onesf[0:1, :], rw[1], start=True, stop=True)
            for sub in range(4):
                tn.matmul(pb[5][:, sub:sub + 1], rw[3][0:1, sub * 128:(sub + 1) * 128],
                          one1[0:1, 0:1], start=True, stop=True)
                mm = tn.matmul(pb[5][:, 4 + sub:5 + sub], rw[1][0:1, sub * 128:(sub + 1) * 128],
                               one1[0:1, 0:1], start=True, stop=True)
            mm.then_inc(*S["bc"].inc(1))
            tn.wait_ge(S["evac"].sem, t + 1)
            if t > 0:
                tn.wait_ge(S["qkfix"].sem, t)
            for pbo, wsb in ((0, wq_sb), (2, wk_sb)):
                for fo in range(2):
                    for di in range(DC):
                        mm = tn.matmul(pb[pbo + fo][:, :],
                                       wsb[:, di * 256 + fo * 128:di * 256 + (fo + 1) * 128],
                                       xbuf[b][:, di * TT:(di + 1) * TT],
                                       start=(di == 0), stop=(di == DC - 1))
            mm.then_inc(*S["qk"].inc(1))
            for rnd in range(2):
                if t > 0 or rnd > 0:
                    tn.wait_ge(S["vevac"].sem, 2 * t + rnd)
                mm = None
                for si in range(2):
                    sub = 2 * rnd + si
                    for di in range(DC):
                        mm = tn.matmul(pb[4][:, si * 256:(si + 1) * 256],
                                       xbuf[b][:, di * TT + sub * 128:di * TT + (sub + 1) * 128],
                                       wv_sb[:, di * 256:(di + 1) * 256],
                                       start=(di == 0), stop=(di == DC - 1))
                mm.then_inc(*S["vmm"].inc(1))
        if has1b:
            tn.wait_ge(S["qkfix"].sem, NTT)
            tn.wait_ge(S["vevac"].sem, 2 * NTT)
            flat = []
            grp = 0
            n_od_ev = 0
            for (h, qi, ms) in sched:
                has_od = qi > 0
                if has_od:
                    n_od_ev += 1
                for (m, p, kind, aux, dgi) in ms:
                    first_od = kind == "od" and m == 0
                    first_dg = kind == "dg" and aux == 0
                    last_od = kind == "od" and m == 4 * qi - 1
                    last_dg = kind == "dg" and aux == 3
                    flat.append(dict(h=h, qi=qi, m=m, p=p, grp=grp, n_od=n_od_ev,
                                     has_od=has_od, first=(first_od or first_dg),
                                     first_dg_od=(first_dg and has_od),
                                     start=(first_od or (first_dg and not has_od)),
                                     stop=(last_od or last_dg)))
                grp += 1
            DELAY = 2
            for idx in range(len(flat) + DELAY):
                if idx < len(flat):
                    f = flat[idx]
                    if f["p"] >= 4:
                        tn.wait_ge(S["sex"].sem, f["p"] - 3)
                    tn.matmul(pb[f["p"] % 4][:, :],
                              K_sb[:, f["h"] * T + f["m"] * 128:f["h"] * T + (f["m"] + 1) * 128],
                              Q_sb[:, f["h"] * T + f["qi"] * TT:f["h"] * T + (f["qi"] + 1) * TT],
                              start=True, stop=True).then_inc(*S["ssc"].inc(1))
                if idx >= DELAY:
                    f = flat[idx - DELAY]
                    if f["start"]:
                        tn.wait_ge(S["scmb"].sem, f["grp"])
                    if f["first_dg_od"] and f["m"] == 4 * f["qi"]:
                        tn.wait_ge(S["sod"].sem, f["n_od"])
                    tn.wait_ge(S["sex"].sem, f["p"] + 1)
                    mm = None
                    for sub in range(4):
                        mm = tn.matmul(pb[4 + sub][:, 0:129],
                                       P_pool[f["p"] % 4][:, sub * 128:(sub + 1) * 128],
                                       V_sb[:, f["h"] * 64 * VW + f["m"] * VW:
                                            f["h"] * 64 * VW + f["m"] * VW + 129],
                                       start=(f["start"] or (f["first_dg_od"])),
                                       stop=f["stop"])
                    mm.then_inc(*S["sav"].inc(1))

    @block.vector
    def _(v):
        v.memset(ones_bf[:, :], 1.0)
        v.memset(onesf[:, :], 1.0)
        v.memset(one1[:, :], 1.0)
        v.memset(V_sb[:, :], 1.0).then_inc(*S["setup"].inc(1))
        for wi in range(3):
            v.wait_ge(S["tsu"].sem, wi + 1)
            if wi < 2:
                v.tensor_scalar_mul(wsrow[0:1, wi * 256:(wi + 1) * 256], pb[5][0:1, 0:256],
                                    -1.0).then_inc(*S["vsu"].inc(1))
            else:
                v.tensor_copy(wsrow[0:1, wi * 256:(wi + 1) * 256],
                              pb[5][0:1, 0:256]).then_inc(*S["vsu"].inc(1))
        v.wait_ge(S["tsu"].sem, 4)
        v.tensor_copy(negws[:, 0:4], pb[5][:, 0:4])
        v.tensor_copy(wvb[:, :], pb[5][:, 8:264]).then_inc(*S["vsu"].inc(1))
        for t in range(NTT):
            b2 = t % 2
            v.wait_ge(S["stats"].sem, t + 1)
            v.tensor_scalar_mul(rw[0], pb[6][0:1, :], 1.0 / D)
            v.tensor_scalar_mul(rw[1], pb[7][0:1, :], 1.0 / D)
            v.tensor_mul(rw[2], rw[0], rw[0])
            v.tensor_sub(rw[3], rw[1], rw[2])
            v.tensor_scalar_add(rw[1], rw[3], float(EPS))
            v.reciprocal(rw[2], rw[1]).then_inc(*S["v1"].inc(1))
            v.wait_ge(S["s2"].sem, t + 1)
            v.tensor_mul(rw[1], rw[0], rw[3]).then_inc(*S["v2"].inc(1))
            v.wait_ge(S["bc"].sem, t + 1)
            v.tensor_copy(a_b[:, :], pb[6][:, :])
            v.tensor_copy(b_b[:, :], pb[7][:, :])
            v.tensor_copy(colp[b2][:, 0:8], pb[5][:, 0:8]).then_inc(*S["evac"].inc(1))
            v.wait_ge(S["qk"].sem, t + 1)
            for pi, (pbo, dst, nwo) in enumerate([(0, Q_sb, 0), (2, K_sb, 2)]):
                for fo in range(2):
                    v.tensor_mul(tmpf[fo][:, :], pb[pbo + fo][:, :], a_b[:, :])
                    last = v.scalar_tensor_tensor(
                        dst[:, fo * T + t * TT:fo * T + (t + 1) * TT],
                        b_b[:, :], negws[:, nwo + fo:nwo + fo + 1], tmpf[fo][:, :],
                        ALU.mult, ALU.add)
            last.then_inc(*S["qkfix"].inc(1))
            for rnd in range(2):
                v.wait_ge(S["vmm"].sem, 2 * t + rnd + 1)
                for si in range(2):
                    sub = 2 * rnd + si
                    g64 = t * 4 + sub
                    v.tensor_scalar(tmpv[0][:, :], wvb[:, :],
                                    colp[b2][:, 4 + sub:5 + sub], None, ALU.mult)
                    v.tensor_scalar(tmpv[1][:, :], pb[4][:, si * 256:(si + 1) * 256],
                                    colp[b2][:, sub:sub + 1], None, ALU.mult)
                    for fo in range(2):
                        last = v.tensor_sub(
                            V_sb[:, fo * 64 * VW + g64 * VW:fo * 64 * VW + g64 * VW + 128],
                            tmpv[1][:, fo * 128:(fo + 1) * 128],
                            tmpv[0][:, fo * 128:(fo + 1) * 128])
                last.then_inc(*S["vevac"].inc(1))
        if has1b:
            v.wait_ge(S["dw"].sem, 96)
            grp = 0
            n_od_ev = 0
            oid = 0
            for (h, qi, ms) in sched:
                has_od = qi > 0
                for (m, p, kind, aux, dgi) in ms:
                    if kind == "dg":
                        v.wait_ge(S["ssc"].sem, p + 1)
                        if dgi >= 2:
                            v.wait_ge(S["sex"].sem, diag_pass_of[dgi - 2] + 1)
                        v.scalar_tensor_tensor(
                            tmpS[dgi % 2][:, :], pb[p % 4][:, :], ISQ,
                            md_sb[:, (h * 4 + aux) * TT:(h * 4 + aux + 1) * TT],
                            ALU.mult, ALU.add).then_inc(*S["sdg"].inc(1))
                    if kind == "od" and m == 4 * qi - 1:
                        v.wait_ge(S["sav"].sem, p + 1)
                        for sub in range(4):
                            last = v.tensor_copy(odsb[:, sub * VW:sub * VW + 129],
                                                 pb[4 + sub][:, 0:129])
                        n_od_ev += 1
                        last.then_inc(*S["sod"].inc(1))
                p_last = ms[-1][1]
                v.wait_ge(S["sav"].sem, p_last + 1)
                for sub in range(4):
                    if has_od:
                        v.tensor_scalar(cmb[:, sub * VW:sub * VW + 129],
                                        odsb[:, sub * VW:sub * VW + 129],
                                        ec_sb[:, h * 4 + sub:h * 4 + sub + 1], None, ALU.mult)
                        last = v.tensor_add(cmb[:, sub * VW:sub * VW + 129],
                                            cmb[:, sub * VW:sub * VW + 129],
                                            pb[4 + sub][:, 0:129])
                    else:
                        last = v.tensor_copy(cmb[:, sub * VW:sub * VW + 129],
                                             pb[4 + sub][:, 0:129])
                last.then_inc(*S["scmb"].inc(1))
                for sub in range(4):
                    v.reciprocal(rcp[:, sub:sub + 1], cmb[:, sub * VW + 128:sub * VW + 129])
                for sub in range(4):
                    slot = oid % 8
                    if oid >= 8:
                        v.wait_ge(S["soc"].sem, 16 * (oid - 7))
                    v.tensor_scalar(outp[:, slot * 128:(slot + 1) * 128],
                                    cmb[:, sub * VW:sub * VW + 128],
                                    rcp[:, sub:sub + 1], None, ALU.mult) \
                        .then_inc(*S["sot"].inc(1))
                    oid += 1
                grp += 1

    @block.sync
    def _(sy):
        if has1b:
            oid = 0
            for (h, qi, ms) in sched:
                for sub in range(4):
                    slot = oid % 8
                    sy.wait_ge(S["sot"].sem, oid + 1)
                    sy.dma_start(out=cc_in[qi * TT + sub * 128:qi * TT + (sub + 1) * 128,
                                           h * 128:(h + 1) * 128],
                                 in_=outp[:, slot * 128:(slot + 1) * 128]) \
                        .then_inc(*S["soc"].inc(16))
                    oid += 1
        if debug:
            sy.wait_ge(S["qkfix"].sem, NTT)
            sy.wait_ge(S["vevac"].sem, 2 * NTT)
            sy.dma_start(out=dbg["qT"], in_=Q_sb[:, :]).then_inc(*S["yst"].inc(16))
            sy.dma_start(out=dbg["kT"], in_=K_sb[:, :]).then_inc(*S["yst"].inc(16))
            sy.dma_start(out=dbg["v"], in_=V_sb[:, :]).then_inc(*S["yst"].inc(16))

    p1.close()
    p2 = ExitStack()
    sb2 = lambda name, shape, dt=F32: p2.enter_context(nc.sbuf_tensor(name, shape, dt))
    if has2:
        Y_sb = sb2("Y_sb", [128, DC * TPC])
        rows2 = sb2("rows2", [1, 4 * TT])
        rw2 = [rows2[0:1, i * TT:(i + 1) * TT] for i in range(4)]
        m2b = sb2("m2b", [128, TT])
        r2b = sb2("r2b", [128, TT])
        onesf2 = sb2("onesf2", [1, 128])
        ones_bf2 = sb2("ones_bf2", [128, 8], BF16)
    p2a = ExitStack()
    if has1b:
        outT_sb = p2a.enter_context(nc.sbuf_tensor("outT_sb", [128, DC * TPC], BF16))

        @block.sync
        def _(sy):
            rid = sy.partition_id()
            sy.wait_ge(S["scc"].sem, rid + 1)
            cc5 = cc_out.rearrange("(s r q) m -> s r q m", r=NC, q=TPC)
            stg = cc_stage.rearrange("(r q) m -> r q m", q=TPC)
            sy.dma_start(out=stg, in_=cc5[ts(rid, 1), :, :, :][0]).then_inc(*S["rcv"].inc(16))
            sy.wait_ge(S["rcv"].sem, 16)
            for fc in range(DC):
                r, cb = fc // 2, fc % 2
                sy.dma_start_transpose(
                    outT_sb[:, fc * TPC:(fc + 1) * TPC],
                    cc_stage[r * TPC:(r + 1) * TPC, cb * 128:(cb + 1) * 128]) \
                    .then_inc(*S["rcv"].inc(16))
            sy.wait_ge(S["rcv"].sem, 16 * 17)
            if debug:
                sy.dma_start(out=dbg["attnT"], in_=outT_sb[:, :]).then_inc(*S["yst"].inc(16))


    if has2:
        woT_sb = p2a.enter_context(nc.sbuf_tensor("woT_sb", [128, DC * D], BF16))
        xow = [p2a.enter_context(nc.sbuf_tensor(f"xow{i}", [128, TT], F32)) for i in range(2)]

        NWO = 32

        @block.vector
        def _(v):
            v.wait_ge(S["soc"].sem, 128 * 16)
            if debug:
                v.wait_ge(S["yst"].sem, 48)
            v.memset(onesf2[:, :], 1.0)
            v.memset(ones_bf2[:, :], 1.0).then_inc(*S["setup"].inc(1))
            for idx in range(NWO):
                v.wait_ge(S["swo"].sem, idx + 1)
                v.wait_ge(S[f"dxo{idx % 2}"].sem, 16 * (idx // 2 + 1))
                fo, tt = idx // 2, idx % 2
                v.tensor_add(Y_sb[:, fo * TPC + tt * TT:fo * TPC + (tt + 1) * TT],
                             pb[idx % 2][:, :], xow[idx % 2][:, :]) \
                    .then_inc(*S["y1e"].inc(1))

        @block.gpsimd
        def _(g):
            if debug:
                g.wait_ge(S["yst"].sem, 48)
            for i in range(4):
                g.dma_start(out=woT_sb[:, i * 4 * D:(i + 1) * 4 * D],
                            in_=wo_st[:, i * 4 * D:(i + 1) * 4 * D]).then_inc(*S["dwo"].inc(16))

        @block.sync
        def _(sy):
            for idx in range(NWO):
                if idx >= 2:
                    sy.wait_ge(S["y1e"].sem, idx - 1)
                sy.dma_start(out=xow[idx % 2][:, :], in_=xown_st[idx]) \
                    .then_inc(*S[f"dxo{idx % 2}"].inc(16))

        @block.tensor
        def _(tn):
            tn.wait_ge(S["dwo"].sem, 64)
            tn.wait_ge(S["rcv"].sem, 16 * 17)
            tn.wait_ge(S["setup"].sem, 2)
            for idx in range(NWO):
                fo, tt = idx // 2, idx % 2
                if idx >= 2:
                    tn.wait_ge(S["y1e"].sem, idx - 1)
                for di in range(DC):
                    mm = tn.matmul(pb[idx % 2][:, :],
                                   woT_sb[:, di * D + fo * 128:di * D + (fo + 1) * 128],
                                   outT_sb[:, di * TPC + tt * TT:di * TPC + (tt + 1) * TT],
                                   start=(di == 0), stop=(di == DC - 1))
                mm.then_inc(*S["swo"].inc(1))

        p2a.close()
        xn2_sb = sb2("xn2_sb", [128, DC * TPC], BF16)
        act_sb = sb2("act_sb", [128, FSC * TPC], BF16)
        gu_buf = [sb2(f"gu_buf{i}", [128, DC * 256], BF16) for i in range(2)]
        dn_buf = [sb2(f"dn_buf{i}", [128, FSC * 128], BF16) for i in range(2)]
        gate_sb = sb2("gate_sb", [128, DC * 8], BF16)
        idn_sb = sb2("idn_sb", [128, 128])
        sel_sb = sb2("sel_sb", [8, 8 * 128])
        cT_sb = sb2("cT_sb", [8, TPC])
        ybf = [sb2(f"ybf{i}", [128, TT], BF16) for i in range(2)]
        ysq = [sb2(f"ysq{i}", [128, TT], BF16) for i in range(2)]
        sg_sb = [sb2(f"sg_sb{i}", [128, TT], BF16) for i in range(2)]
        u1_sb = [sb2(f"u1_sb{i}", [128, TT], BF16) for i in range(2)]
        tmp2 = [sb2("tmp2_0", [128, TT])] * 2
        cbt_ = [m2b, r2b]
        gsc = sb2("gsc", [128, 64])
        gs2 = sb2("gs2", [128, 64])
        gct = sb2("gct", [128, 64])
        gtA = sb2("gtA", [128, 64])
        gtB = sb2("gtB", [128, 64])
        gcl = sb2("gcl", [128, 64])

        JCs = [FC] * E8 + [FSC]
        jgl = []
        jj = 0
        for e in range(9):
            for j in range(JCs[e]):
                jgl.append((e, j, jj)); jj += 1
        TJC = jj
        hgl = []
        hh = 0
        for e in range(9):
            for h in range(DC):
                hgl.append((e, h, hh)); hh += 1
        THC = hh

        @block.gpsimd
        def _(g):
            g.wait_ge(S["y1e"].sem, 32)
            g.dma_start(out=gate_sb[:, :], in_=gate_st).then_inc(*S["dwo"].inc(16))
            g.dma_start(out=idn_sb[:, :], in_=idn_st).then_inc(*S["dwo"].inc(16))
            g.dma_start(out=sel_sb[:, :], in_=sel_st).then_inc(*S["dwo"].inc(16))
            for e in range(9):
                for j in range(JCs[e]):
                    jjx = sum(JCs[:e]) + j
                    if jjx >= 2:
                        g.wait_ge(S["gups"].sem, 4 * (jjx - 1))
                    src = gu_st[e, j] if e < E8 else sgu_st[j]
                    g.dma_start(out=gu_buf[jjx % 2][:, :], in_=src) \
                        .then_inc(*S[f"dgu{jjx % 2}"].inc(16))
                for h in range(DC):
                    hhx = e * DC + h
                    if hhx >= 2:
                        g.wait_ge(S["dnps"].sem, 2 * (hhx - 1))
                    if e < E8:
                        srcd = dn_st[e, h]
                        g.dma_start(out=dn_buf[hhx % 2][:, 0:FC * 128], in_=srcd) \
                            .then_inc(*S[f"ddn{hhx % 2}"].inc(16))
                    else:
                        g.dma_start(out=dn_buf[hhx % 2][:, :], in_=sdn_st[h]) \
                            .then_inc(*S[f"ddn{hhx % 2}"].inc(16))

        @block.scalar
        def _(s):
            for tt in range(2):
                for di in range(DC):
                    idx = tt * DC + di
                    s.wait_ge(S["v21"].sem, idx + 1)
                    if idx >= 2:
                        s.wait_ge(S["st2"].sem, 2 * (idx - 1))
                    s.activation(out=ysq[di % 2][:, :], in_=ybf[di % 2][:, :],
                                 func=AF.Square).then_inc(*S["s21"].inc(1))
                s.wait_ge(S["v22"].sem, tt + 1)
                s.activation(out=rw2[3], in_=rw2[2], func=AF.Sqrt).then_inc(*S["s22"].inc(1))
            s.wait_ge(S["g1"].sem, 16)
            for g8 in range(8):
                s.activation(out=gsc[:, 8 * g8:8 * g8 + 8], in_=pb[5][:, 8 * g8:8 * g8 + 8],
                             func=AF.Exp, bias=gcl[:, 8 + g8:9 + g8], scale=1.0) \
                    .then_inc(*S["g2"].inc(1))
            for (e, j, jjx) in jgl:
                for tt in range(2):
                    s.wait_ge(S["gups"].sem, 4 * jjx + 2 * tt + 1)
                    s.activation(out=sg_sb[tt][:, :], in_=pb[2 * tt][:, :],
                                 func=AF.Silu).then_inc(*S["sil"].inc(1))

        @block.tensor
        def _(tn):
            for tt in range(2):
                for di in range(DC):
                    idx = tt * DC + di
                    tn.wait_ge(S["s21"].sem, idx + 1)
                    if tt > 0 and di == 0:
                        tn.wait_ge(S["ev2"].sem, tt)
                    tn.matmul(pb[7][0:1, :], ones_bf2[:, 0:1], ysq[di % 2][:, :],
                              start=(di == 0), stop=(di == DC - 1)) \
                        .then_inc(*S["st2"].inc(1))
                    mm = tn.matmul(pb[6][0:1, :], ones_bf2[:, 0:1], ybf[di % 2][:, :],
                                   start=(di == 0), stop=(di == DC - 1)) \
                        .then_inc(*S["st2"].inc(1))
                tn.wait_ge(S["v23"].sem, tt + 1)
                tn.matmul(pb[6][:, :], onesf2[0:1, :], rw2[0], start=True, stop=True)
                tn.matmul(pb[7][:, :], onesf2[0:1, :], rw2[3], start=True, stop=True) \
                    .then_inc(*S["bc2"].inc(1))
            tn.wait_ge(S["xn2d"].sem, 32)
            tn.wait_ge(S["dwo"].sem, 112)
            for g8 in range(8):
                for di in range(DC):
                    mm = tn.matmul(pb[5][:, 8 * g8:8 * g8 + 8],
                                   xn2_sb[:, di * TPC + g8 * 128:di * TPC + (g8 + 1) * 128],
                                   gate_sb[:, di * 8:(di + 1) * 8],
                                   start=(di == 0), stop=(di == DC - 1))
                mm.then_inc(*S["glog"].inc(1))
            tn.wait_ge(S["gcmb"].sem, 8)
            for rnd in range(2):
                if rnd:
                    tn.wait_ge(S["gsel"].sem, rnd)
                for gi in range(4):
                    g8 = rnd * 4 + gi
                    tn.transpose(pb[4][0:8, gi * 128:(gi + 1) * 128],
                                 gct[:, 8 * g8:8 * g8 + 8], idn_sb[:, :]) \
                        .then_inc(*S["gtr"].inc(1))
            for e in range(9):
                if e < E8:
                    if e > 0:
                        tn.wait_ge(S["dnac"].sem, 32 * e)
                    tn.wait_ge(S["gsel"].sem, 2)
                    for tt in range(2):
                        tn.matmul(pb[4 + tt][:, :], sel_sb[0:8, e * 128:(e + 1) * 128],
                                  cT_sb[0:8, tt * TT:(tt + 1) * TT], start=True, stop=True) \
                            .then_inc(*S["cbt"].inc(1))
                for j in range(JCs[e]):
                    jjx = sum(JCs[:e]) + j
                    tn.wait_ge(S[f"dgu{jjx % 2}"].sem, 16 * (jjx // 2 + 1))
                    if e < E8 and j == 0:
                        tn.wait_ge(S["cbk"].sem, 2 * (e + 1))
                    for tt in range(2):
                        if jjx >= 1:
                            tn.wait_ge(S["sil"].sem, 2 * (jjx - 1) + tt + 1)
                            tn.wait_ge(S["uact"].sem, 2 * (jjx - 1) + tt + 1)
                        for di in range(DC):
                            mm = tn.matmul(pb[2 * tt][:, :],
                                           gu_buf[jjx % 2][:, di * 256:di * 256 + 128],
                                           xn2_sb[:, di * TPC + tt * TT:di * TPC + (tt + 1) * TT],
                                           start=(di == 0), stop=(di == DC - 1))
                        mm.then_inc(*S["gups"].inc(1))
                        for di in range(DC):
                            mm = tn.matmul(pb[2 * tt + 1][:, :],
                                           gu_buf[jjx % 2][:, di * 256 + 128:di * 256 + 256],
                                           xn2_sb[:, di * TPC + tt * TT:di * TPC + (tt + 1) * TT],
                                           start=(di == 0), stop=(di == DC - 1))
                        mm.then_inc(*S["gups"].inc(1))
                JC = JCs[e]
                for h in range(DC):
                    hhx = e * DC + h
                    tn.wait_ge(S[f"ddn{hhx % 2}"].sem, 16 * (hhx // 2 + 1))
                    tn.wait_ge(S["uact"].sem, 2 * (sum(JCs[:e]) + JC))
                    for tt in range(2):
                        slot = 2 * hhx + tt
                        if slot >= 2:
                            tn.wait_ge(S["dnac"].sem, slot - 1)
                        for fj in range(JC):
                            mm = tn.matmul(pb[4 + tt][:, :],
                                           dn_buf[hhx % 2][:, fj * 128:(fj + 1) * 128],
                                           act_sb[:, fj * TPC + tt * TT:fj * TPC + (tt + 1) * TT],
                                           start=(fj == 0), stop=(fj == JC - 1))
                        mm.then_inc(*S["dnps"].inc(1))

        @block.vector
        def _(v):
            v.wait_ge(S["y1e"].sem, 32)
            for tt in range(2):
                for di in range(DC):
                    idx = tt * DC + di
                    if idx >= 2:
                        v.wait_ge(S["s21"].sem, idx - 1)
                    v.tensor_copy(ybf[di % 2][:, :],
                                  Y_sb[:, di * TPC + tt * TT:di * TPC + (tt + 1) * TT]) \
                        .then_inc(*S["v21"].inc(1))
                v.wait_ge(S["st2"].sem, 32 * (tt + 1))
                v.tensor_scalar_mul(rw2[0], pb[6][0:1, :], 1.0 / D)
                v.tensor_scalar_mul(rw2[1], pb[7][0:1, :], 1.0 / D)
                v.tensor_mul(rw2[2], rw2[0], rw2[0])
                v.tensor_sub(rw2[3], rw2[1], rw2[2])
                v.tensor_scalar_add(rw2[1], rw2[3], float(EPS))
                v.reciprocal(rw2[2], rw2[1]).then_inc(*S["v22"].inc(1))
                v.wait_ge(S["s22"].sem, tt + 1)
                v.engine_nop().then_inc(*S["v23"].inc(1))
                v.wait_ge(S["bc2"].sem, tt + 1)
                v.tensor_copy(m2b[:, :], pb[6][:, :])
                v.tensor_copy(r2b[:, :], pb[7][:, :]).then_inc(*S["ev2"].inc(1))
                for di in range(DC):
                    v.tensor_sub(tmp2[di % 2][:, :],
                                 Y_sb[:, di * TPC + tt * TT:di * TPC + (tt + 1) * TT], m2b[:, :])
                    v.tensor_mul(xn2_sb[:, di * TPC + tt * TT:di * TPC + (tt + 1) * TT],
                                 tmp2[di % 2][:, :], r2b[:, :]).then_inc(*S["xn2d"].inc(1))
            v.wait_ge(S["glog"].sem, 8)
            for g8 in range(8):
                v.reduce_max(gcl[:, g8:g8 + 1], pb[5][:, 8 * g8:8 * g8 + 8], axis=AX.X)
            for g8 in range(8):
                v.tensor_scalar_mul(gcl[:, 8 + g8:9 + g8], gcl[:, g8:g8 + 1], -1.0)
            v.engine_nop().then_inc(*S["g1"].inc(16))
            for g8 in range(8):
                v.wait_ge(S["g2"].sem, g8 + 1)
                v.reduce_sum(gcl[:, 16 + g8:17 + g8], gsc[:, 8 * g8:8 * g8 + 8], axis=AX.X)
            for g8 in range(8):
                v.reciprocal(gcl[:, 24 + g8:25 + g8], gcl[:, 16 + g8:17 + g8])
            for g8 in range(8):
                v.tensor_scalar(gs2[:, 8 * g8:8 * g8 + 8], gsc[:, 8 * g8:8 * g8 + 8],
                                gcl[:, 24 + g8:25 + g8], None, ALU.mult)
            for g8 in range(8):
                v.reduce_max(gcl[:, 32 + g8:33 + g8], gs2[:, 8 * g8:8 * g8 + 8], axis=AX.X)
            for g8 in range(8):
                v.tensor_scalar(gtA[:, 8 * g8:8 * g8 + 8], gs2[:, 8 * g8:8 * g8 + 8],
                                gcl[:, 32 + g8:33 + g8], None, ALU.is_equal)
            for g8 in range(8):
                v.scalar_tensor_tensor(gtB[:, 8 * g8:8 * g8 + 8], gtA[:, 8 * g8:8 * g8 + 8],
                                       -1e9, gs2[:, 8 * g8:8 * g8 + 8], ALU.mult, ALU.add)
            for g8 in range(8):
                v.reduce_max(gcl[:, 40 + g8:41 + g8], gtB[:, 8 * g8:8 * g8 + 8], axis=AX.X)
            for g8 in range(8):
                v.tensor_scalar(gtA[:, 8 * g8:8 * g8 + 8], gs2[:, 8 * g8:8 * g8 + 8],
                                gcl[:, 40 + g8:41 + g8], None, ALU.is_ge)
            for g8 in range(8):
                v.tensor_mul(gct[:, 8 * g8:8 * g8 + 8], gs2[:, 8 * g8:8 * g8 + 8],
                             gtA[:, 8 * g8:8 * g8 + 8]).then_inc(*S["gcmb"].inc(1))
            for rnd in range(2):
                v.wait_ge(S["gtr"].sem, 4 * (rnd + 1))
                v.tensor_copy(cT_sb[0:8, rnd * TT:(rnd + 1) * TT], pb[4][0:8, 0:512]) \
                    .then_inc(*S["gsel"].inc(1))
            for e in range(9):
                if e < E8:
                    for tt in range(2):
                        v.wait_ge(S["cbt"].sem, 2 * e + tt + 1)
                        v.tensor_copy(cbt_[tt][:, :], pb[4 + tt][:, :]) \
                            .then_inc(*S["cbk"].inc(1))
                for j in range(JCs[e]):
                    jjx = sum(JCs[:e]) + j
                    for tt in range(2):
                        v.wait_ge(S["gups"].sem, 4 * jjx + 2 * tt + 2)
                        v.wait_ge(S["sil"].sem, 2 * jjx + tt + 1)
                        if e < E8:
                            v.tensor_mul(u1_sb[tt][:, :], pb[2 * tt + 1][:, :],
                                         cbt_[tt][:, :])
                            v.tensor_mul(act_sb[:, j * TPC + tt * TT:j * TPC + (tt + 1) * TT],
                                         u1_sb[tt][:, :], sg_sb[tt][:, :]) \
                                .then_inc(*S["uact"].inc(1))
                        else:
                            v.tensor_mul(act_sb[:, j * TPC + tt * TT:j * TPC + (tt + 1) * TT],
                                         pb[2 * tt + 1][:, :], sg_sb[tt][:, :]) \
                                .then_inc(*S["uact"].inc(1))
                for h in range(DC):
                    hhx = e * DC + h
                    for tt in range(2):
                        v.wait_ge(S["dnps"].sem, 2 * hhx + tt + 1)
                        v.tensor_add(Y_sb[:, h * TPC + tt * TT:h * TPC + (tt + 1) * TT],
                                     Y_sb[:, h * TPC + tt * TT:h * TPC + (tt + 1) * TT],
                                     pb[4 + tt][:, :]).then_inc(*S["dnac"].inc(1))

        @block.sync
        def _(sy):
            if debug:
                sy.wait_ge(S["xn2d"].sem, 32)
                sy.dma_start(out=dbg["xn2"], in_=xn2_sb[:, :]).then_inc(*S["yst"].inc(16))
                sy.wait_ge(S["gsel"].sem, 2)
                sy.dma_start(out=dbg["comb"], in_=cT_sb[0:8, :]).then_inc(*S["yst"].inc(16))

            sy.wait_ge(S["dnac"].sem, 32 * 9)
            y3 = yT.rearrange("(a p) m -> a p m", p=128)
            for di in range(DC):
                sy.dma_start(out=y3[di], in_=Y_sb[:, di * TPC:(di + 1) * TPC]) \
                    .then_inc(*S["yst"].inc(16))

    @block.sync
    def _(sy):
        sy.wait_ge(S["yst"].sem, S["yst"].v)

    if not has1b:
        p1.close()
    else:
        if not has2:
            p2a.close()
        p2.close()
    ctx.close()
    return nc



def stage_inputs(inp, phase="full"):
    x = np.asarray(inp["L0_attn_x"]).reshape(T, D)
    Wq = np.asarray(inp["L0_attn_Wq"])
    Wk = np.asarray(inp["L0_attn_Wk"])
    Wv = np.asarray(inp["L0_attn_Wv"])

    x_st = np.ascontiguousarray(
        x.reshape(NTT, TT, DC, 128).transpose(0, 3, 2, 1).reshape(NTT, 128, DC * TT))

    def wslice(W, c):
        hA, hB = head_pair(c)
        WT = W.T
        wc = np.concatenate([WT[:, 128 * hA:128 * (hA + 1)], WT[:, 128 * hB:128 * (hB + 1)]], axis=1)
        return np.ascontiguousarray(wc.reshape(DC, 128, 256).transpose(1, 0, 2).reshape(128, DC * 256))

    in_maps = []
    for c in range(NC):
        hA, hB = head_pair(c)
        sl = (slope(hA), slope(hB))
        kl = np.arange(128, dtype=np.float64)[:, None]
        ql = np.arange(TT, dtype=np.float64)[None, :]
        md = np.empty((128, 8 * TT), np.float32)
        for hi in range(2):
            for j in range(4):
                mm = sl[hi] * (128 * j + kl - ql)
                mm = np.where(128 * j + kl <= ql, mm, NEG_BIG)
                md[:, (hi * 4 + j) * TT:(hi * 4 + j + 1) * TT] = mm.astype(np.float32)
        bc = np.zeros((128, 128), np.float32)
        for hi in range(2):
            for dl in range(64):
                bc[:, hi * 64 + dl] = (sl[hi] * (kl[:, 0] - 128.0 * dl)).astype(np.float32)
        ec = np.zeros((128, 8), np.float32)
        for hi in range(2):
            for sub in range(4):
                ec[:, hi * 4 + sub] = np.exp(-sl[hi] * (128.0 * sub + kl[:, 0])).astype(np.float32)
        m = {
            "x_st": x_st, "wq_st": wslice(Wq, c), "wk_st": wslice(Wk, c), "wv_st": wslice(Wv, c),
            "md_st": md, "bc_st": bc, "ec_st": ec,
        }
        if phase == "full":
            m.update(stage_phase2(inp, c))
        in_maps.append(m)
    return in_maps


def stage_phase2(inp, c):
    Wo = np.asarray(inp["L0_attn_Wo"])
    gate = np.asarray(inp["L0_ffn_gate_weight"])
    gu = np.asarray(inp["L0_ffn_gate_up_weights"])
    dn = np.asarray(inp["L0_ffn_down_weights"])
    sgu = np.asarray(inp["L0_ffn_shared_gate_up"])
    sdn = np.asarray(inp["L0_ffn_shared_down"])
    x = np.asarray(inp["L0_attn_x"]).reshape(T, D)

    perm = []
    for r in range(NC):
        ha, hb = head_pair(r)
        perm.extend(range(128 * ha, 128 * ha + 128))
        perm.extend(range(128 * hb, 128 * hb + 128))
    woT = Wo.T[np.asarray(perm), :]
    wo_st = np.ascontiguousarray(woT.reshape(DC, 128, D).transpose(1, 0, 2).reshape(128, DC * D))
    xT = x.T
    xown = xT[:, TPC * c:TPC * (c + 1)]
    xown_st = np.ascontiguousarray(
        xown.reshape(DC, 128, 2, TT).transpose(0, 2, 1, 3).reshape(32, 128, TT))
    gate_st = np.ascontiguousarray(gate.T.reshape(DC, 128, 8).transpose(1, 0, 2).reshape(128, DC * 8))
    idn = np.eye(128, dtype=np.float32)
    sel = np.zeros((8, 8 * 128), np.float32)
    for e in range(8):
        sel[e, e * 128:(e + 1) * 128] = 1.0
    gug = gu[:, :F, :].reshape(E8, FC, 128, DC, 128)
    guu = gu[:, F:, :].reshape(E8, FC, 128, DC, 128)
    gu_st = np.empty((E8, FC, 128, DC * 256), np.float32)
    gg = gug.transpose(0, 1, 4, 3, 2)
    uu = guu.transpose(0, 1, 4, 3, 2)
    gu_st = np.ascontiguousarray(
        np.stack([gg, uu], axis=4).reshape(E8, FC, 128, DC, 2 * 128).reshape(E8, FC, 128, DC * 256))
    dn_st = np.ascontiguousarray(
        dn.reshape(E8, DC, 128, FC, 128).transpose(0, 1, 4, 3, 2).reshape(E8, DC, 128, FC * 128))
    sgug = sgu[:FS, :].reshape(FSC, 128, DC, 128)
    sguu = sgu[FS:, :].reshape(FSC, 128, DC, 128)
    sgu_st = np.ascontiguousarray(
        np.stack([sgug.transpose(0, 3, 2, 1), sguu.transpose(0, 3, 2, 1)], axis=3)
        .reshape(FSC, 128, DC, 256).reshape(FSC, 128, DC * 256))
    sdn_st = np.ascontiguousarray(
        sdn.reshape(DC, 128, FSC, 128).transpose(0, 3, 2, 1).reshape(DC, 128, FSC * 128))
    return {
        "wo_st": wo_st, "xown_st": xown_st, "gate_st": gate_st,
        "idn_st": idn, "sel_st": sel,
        "gu_st": gu_st, "dn_st": dn_st, "sgu_st": sgu_st, "sdn_st": sdn_st,
    }



def kernel(**inputs):
    in_maps = stage_inputs(inputs, phase="full")
    nc = build_nc(phase="full", debug=False)
    res = run_bass_kernel_spmd(nc, in_maps, core_ids=list(range(NC)))
    out = np.empty((1, T, D), np.float32)
    for c in range(NC):
        out[0, TPC * c:TPC * (c + 1), :] = np.asarray(res.results[c]["yT"]).T
    return out


# version 05
# speedup vs baseline: 1.3274x; 1.0077x over previous
import sys
sys.path.insert(0, "/opt/trn_rl_repo")
import numpy as np
from contextlib import ExitStack
import concourse.bass as bass
import concourse.mybir as mybir
from concourse.bass_utils import run_bass_kernel_spmd

F32 = mybir.dt.float32
BF16 = mybir.dt.bfloat16
AF = mybir.ActivationFunctionType
ALU = mybir.AluOpType
AX = mybir.AxisListType
ts = bass.ts

T, D, NH, HD, NC = 8192, 2048, 16, 128, 8
TT = 512
NTT = T // TT
DC = D // 128
TPC = T // NC
E8, F, FS = 8, 1408, 2816
FC, FSC = F // 128, FS // 128
EPS = 1e-5
ISQ = float(1.0 / np.sqrt(HD))
VW = 136
NEG_BIG = -1e30


def head_pair(c):
    return c, 15 - c


def slope(h):
    return 2.0 ** (-(h + 1) / 2.0)


class Ctr:
    def __init__(self, nc, name, ctx):
        self.sem = ctx.enter_context(nc.semaphore(name))
        self.v = 0

    def inc(self, n=1):
        self.v += n
        return self.sem, n


def build_nc(phase="full", debug=False):
    nc = bass.Bass("TRN2", target_bir_lowering=False, debug=False, detect_race_conditions=False)
    ctx = ExitStack()
    dr = lambda name, shape, dt, kind="ExternalInput": nc.dram_tensor(name, shape, dt, kind=kind).ap()

    x_st = dr("x_st", [NTT, 128, DC * TT], F32)
    wq_st = dr("wq_st", [128, DC * 256], F32)
    wk_st = dr("wk_st", [128, DC * 256], F32)
    wv_st = dr("wv_st", [128, DC * 256], F32)
    md_st = dr("md_st", [128, 8 * TT], F32)
    bc_st = dr("bc_st", [128, 128], F32)
    ec_st = dr("ec_st", [128, 8], F32)
    has2 = phase == "full"
    if has2:
        wo_st = dr("wo_st", [128, DC * D], F32)
        xown_st = dr("xown_st", [32, 128, TT], F32)
        gate_st = dr("gate_st", [128, DC * 8], F32)
        idn_st = dr("idn_st", [128, 128], F32)
        sel_st = dr("sel_st", [8, 8 * 128], F32)
        gu_st = dr("gu_st", [E8, FC, 128, DC * 256], F32)
        dn_st = dr("dn_st", [E8, DC, 128, FC * 128], F32)
        sgu_st = dr("sgu_st", [FSC, 128, DC * 256], F32)
        sdn_st = dr("sdn_st", [DC, 128, FSC * 128], F32)
        yT = dr("yT", [D, TPC], F32, "ExternalOutput")
    has1b = phase in ("1b", "full")
    if has1b:
        cc_in = nc.dram_tensor("cc_in", [T, 2 * HD], BF16).ap()
        cc_out = nc.dram_tensor("cc_out", [NC * T, 2 * HD], BF16).ap()
        cc_stage = nc.dram_tensor("cc_stage", [NC * TPC, 2 * HD], BF16).ap()

    dbg = {}
    if debug:
        dbg["qT"] = dr("dbg_qT", [128, 2 * T], BF16, "ExternalOutput")
        dbg["kT"] = dr("dbg_kT", [128, 2 * T], BF16, "ExternalOutput")
        dbg["v"] = dr("dbg_v", [128, 2 * 64 * VW], BF16, "ExternalOutput")
        if has1b:
            dbg["attnT"] = dr("dbg_attnT", [128, DC * TPC], BF16, "ExternalOutput")
        if has2:
            dbg["y1"] = dr("dbg_y1", [128, DC * TPC], F32, "ExternalOutput")
            dbg["xn2"] = dr("dbg_xn2", [128, DC * TPC], BF16, "ExternalOutput")
            dbg["comb"] = dr("dbg_comb", [8, TPC], F32, "ExternalOutput")


    sbuf = lambda name, shape, dt=F32: ctx.enter_context(nc.sbuf_tensor(name, shape, dt))
    pb = [ctx.enter_context(nc.psum_tensor(f"pb{i}", [128, 512], F32)) for i in range(8)]

    S = {}
    for name in ["dx0", "dx1", "dw", "sq", "sqcons", "stats", "v1", "s2", "v2", "bc",
                 "evac", "qk", "qkfix", "vmm", "vevac", "xcons", "setup",
                 "tsu", "vsu", "p1a",
                 "ssc", "sdg", "sex", "sav", "sod", "scmb", "sot", "soc",
                 "scc", "rcv", "pb1",
                 "dwo", "swo", "svo", "y1e", "st2", "v21", "s21", "v22",
                 "bc2", "ev2", "xn2d", "glog", "gcmb", "gsel",
                 "dgu0", "dgu1", "ddn0", "ddn1", "gups", "sil", "uact", "dnps", "dnac",
                 "cbk", "cbt", "dxo0", "dxo1", "s22", "v23", "g1", "g2", "gtr",
                 "yst"]:
        S[name] = Ctr(nc, name, ctx)

    block = ctx.enter_context(nc.Block())

    p1 = ExitStack()
    sb1 = lambda name, shape, dt=F32: p1.enter_context(nc.sbuf_tensor(name, shape, dt))
    wq_sb = sb1("wq_sb", [128, DC * 256], BF16)
    wk_sb = sb1("wk_sb", [128, DC * 256], BF16)
    wv_sb = sb1("wv_sb", [128, DC * 256], BF16)
    xbuf = [sb1(f"xbuf{i}", [128, DC * TT], BF16) for i in range(2)]
    xsq = [sb1(f"xsq{i}", [128, TT], BF16) for i in range(2)]
    Q_sb = sb1("Q_sb", [128, 2 * T], BF16)
    K_sb = sb1("K_sb", [128, 2 * T], BF16)
    V_sb = sb1("V_sb", [128, 2 * 64 * VW], BF16)
    rows = sb1("rows", [1, 4 * TT])
    rw = [rows[0:1, i * TT:(i + 1) * TT] for i in range(4)]
    a_b = sb1("a_b", [128, TT])
    b_b = sb1("b_b", [128, TT])
    colp = [sb1(f"colp{i}", [128, 16]) for i in range(2)]
    negws = sb1("negws", [128, 8])
    wvb = sb1("wvb", [128, 256])
    ones_bf = sb1("ones_bf", [128, 8], BF16)
    onesf = sb1("onesf", [1, 128])
    one1 = sb1("one1", [1, 8])
    wsrow = sb1("wsrow", [1, 3 * 256])
    md_sb = sb1("md_sb", [128, 8 * TT])
    bcc_sb = sb1("bcc_sb", [128, 128])
    ec_sb = sb1("ec_sb", [128, 8])
    P_pool = [sb1(f"Pp{i}", [128, TT], BF16) for i in range(4)]
    tmpS = [sb1(f"tmpS{i}", [128, TT]) for i in range(2)]
    tmpf = tmpS
    tmpv = [tmpS[0][:, 0:256], tmpS[1][:, 0:256]]
    odsb = sb1("odsb", [128, 4 * VW])
    cmb = sb1("cmb", [128, 4 * VW])
    rcp = sb1("rcp", [128, 16])
    outp = sb1("outp", [128, 8 * 128], BF16)

    sched = []
    p_ctr = 0
    dgi_ctr = 0
    diag_pass_of = {}
    for qi in range(NTT):
        for h in range(2):
            ms = []
            for m in range(4 * qi + 4):
                if m < 4 * qi:
                    ms.append((m, p_ctr, "od", 4 * qi - m, None))
                else:
                    diag_pass_of[dgi_ctr] = p_ctr
                    ms.append((m, p_ctr, "dg", m - 4 * qi, dgi_ctr))
                    dgi_ctr += 1
                p_ctr += 1
            sched.append((h, qi, ms))
    NPASS = p_ctr

    @block.gpsimd
    def _(g):
        g.dma_start(out=wq_sb[:, :], in_=wq_st).then_inc(*S["dw"].inc(16))
        g.dma_start(out=wk_sb[:, :], in_=wk_st).then_inc(*S["dw"].inc(16))
        g.dma_start(out=wv_sb[:, :], in_=wv_st).then_inc(*S["dw"].inc(16))
        g.dma_start(out=md_sb[:, :], in_=md_st).then_inc(*S["dw"].inc(16))
        g.dma_start(out=bcc_sb[:, :], in_=bc_st).then_inc(*S["dw"].inc(16))
        g.dma_start(out=ec_sb[:, :], in_=ec_st).then_inc(*S["dw"].inc(16))
        for t in range(NTT):
            if t >= 2:
                g.wait_ge(S["vmm"].sem, 2 * t - 2)
            g.dma_start(out=xbuf[t % 2][:, :], in_=x_st[t]).then_inc(*S[f"dx{t % 2}"].inc(16))
        if has1b:
            for i in range(NC):
                g.wait_ge(S["soc"].sem, 16 * 8 * (2 * i + 2))
                g.collective_compute(
                    "AllGather", ALU.bypass, replica_groups=[list(range(NC))],
                    ins=[cc_in[TPC * i:TPC * (i + 1), :].opt()],
                    outs=[cc_out[NC * TPC * i:NC * TPC * (i + 1), :].opt()],
                ).then_inc(*S["scc"].inc(1))

    @block.scalar
    def _(s):
        for t in range(NTT):
            b = t % 2
            s.wait_ge(S[f"dx{t % 2}"].sem, 16 * (t // 2 + 1))
            for di in range(DC):
                idx = t * DC + di
                if idx >= 2:
                    s.wait_ge(S["sqcons"].sem, idx - 1)
                s.activation(out=xsq[di % 2][:, :], in_=xbuf[b][:, di * TT:(di + 1) * TT],
                             func=AF.Square).then_inc(*S["sq"].inc(1))
            s.wait_ge(S["v1"].sem, t + 1)
            s.activation(out=rw[3], in_=rw[2], func=AF.Sqrt).then_inc(*S["s2"].inc(1))
        if has1b:
            s.wait_ge(S["dw"].sem, 96)
            for (h, qi, ms) in sched:
                for (m, p, kind, aux, dgi) in ms:
                    if p >= 4:
                        s.wait_ge(S["sav"].sem, p - 3)
                    if kind == "od":
                        s.wait_ge(S["ssc"].sem, p + 1)
                        s.activation(out=P_pool[p % 4][:, :], in_=pb[p % 4][:, :],
                                     func=AF.Exp, bias=bcc_sb[:, h * 64 + aux:h * 64 + aux + 1],
                                     scale=ISQ).then_inc(*S["sex"].inc(1))
                    else:
                        s.wait_ge(S["sdg"].sem, dgi + 1)
                        s.activation(out=P_pool[p % 4][:, :], in_=tmpS[dgi % 2][:, :],
                                     func=AF.Exp).then_inc(*S["sex"].inc(1))

    @block.tensor
    def _(tn):
        tn.wait_ge(S["dw"].sem, 96)
        tn.wait_ge(S["setup"].sem, 1)
        for wi, wsb in enumerate([wq_sb, wk_sb, wv_sb]):
            if wi:
                tn.wait_ge(S["vsu"].sem, wi)
            for di in range(DC):
                mm = tn.matmul(pb[5][0:1, 0:256], ones_bf[:, 0:1], wsb[:, di * 256:(di + 1) * 256],
                               start=(di == 0), stop=(di == DC - 1))
            mm.then_inc(*S["tsu"].inc(1))
        tn.wait_ge(S["vsu"].sem, 3)
        for wi in range(2):
            for fo in range(2):
                cidx = wi * 2 + fo
                tn.matmul(pb[5][:, cidx:cidx + 1],
                          wsrow[0:1, wi * 256 + fo * 128:wi * 256 + (fo + 1) * 128],
                          one1[0:1, 0:1], start=True, stop=True)
        tn.matmul(pb[5][:, 8:264], onesf[0:1, :], wsrow[0:1, 512:768], start=True, stop=True) \
            .then_inc(*S["tsu"].inc(1))
        tn.wait_ge(S["vsu"].sem, 4)
        def stats_1a(t):
            b = t % 2
            tn.wait_ge(S["evac"].sem, t)
            for di in range(DC):
                tn.wait_ge(S["sq"].sem, t * DC + di + 1)
                tn.matmul(pb[7][0:1, :], ones_bf[:, 0:1], xsq[di % 2][:, :],
                          start=(di == 0), stop=(di == DC - 1)) \
                    .then_inc(*S["sqcons"].inc(1))
                mm = tn.matmul(pb[6][0:1, :], ones_bf[:, 0:1], xbuf[b][:, di * TT:(di + 1) * TT],
                               start=(di == 0), stop=(di == DC - 1))
            mm.then_inc(*S["stats"].inc(1))

        def bcast_1a(t):
            tn.wait_ge(S["v2"].sem, t + 1)
            tn.wait_ge(S["evac"].sem, t)
            tn.matmul(pb[6][:, :], onesf[0:1, :], rw[3], start=True, stop=True)
            tn.matmul(pb[7][:, :], onesf[0:1, :], rw[1], start=True, stop=True)
            for sub in range(4):
                tn.matmul(pb[5][:, sub:sub + 1], rw[3][0:1, sub * 128:(sub + 1) * 128],
                          one1[0:1, 0:1], start=True, stop=True)
                mm = tn.matmul(pb[5][:, 4 + sub:5 + sub], rw[1][0:1, sub * 128:(sub + 1) * 128],
                               one1[0:1, 0:1], start=True, stop=True)
            mm.then_inc(*S["bc"].inc(1))

        stats_1a(0)
        bcast_1a(0)
        for t in range(NTT):
            b = t % 2
            tn.wait_ge(S["evac"].sem, t + 1)
            if t > 0:
                tn.wait_ge(S["qkfix"].sem, t)
            for pbo, wsb in ((0, wq_sb), (2, wk_sb)):
                for fo in range(2):
                    for di in range(DC):
                        mm = tn.matmul(pb[pbo + fo][:, :],
                                       wsb[:, di * 256 + fo * 128:di * 256 + (fo + 1) * 128],
                                       xbuf[b][:, di * TT:(di + 1) * TT],
                                       start=(di == 0), stop=(di == DC - 1))
            mm.then_inc(*S["qk"].inc(1))
            if t + 1 < NTT:
                stats_1a(t + 1)
            for rnd in range(2):
                if t > 0 or rnd > 0:
                    tn.wait_ge(S["vevac"].sem, 2 * t + rnd)
                mm = None
                for si in range(2):
                    sub = 2 * rnd + si
                    for di in range(DC):
                        mm = tn.matmul(pb[4][:, si * 256:(si + 1) * 256],
                                       xbuf[b][:, di * TT + sub * 128:di * TT + (sub + 1) * 128],
                                       wv_sb[:, di * 256:(di + 1) * 256],
                                       start=(di == 0), stop=(di == DC - 1))
                mm.then_inc(*S["vmm"].inc(1))
            if t + 1 < NTT:
                bcast_1a(t + 1)
        if has1b:
            tn.wait_ge(S["qkfix"].sem, NTT)
            tn.wait_ge(S["vevac"].sem, 2 * NTT)
            flat = []
            grp = 0
            n_od_ev = 0
            for (h, qi, ms) in sched:
                has_od = qi > 0
                if has_od:
                    n_od_ev += 1
                for (m, p, kind, aux, dgi) in ms:
                    first_od = kind == "od" and m == 0
                    first_dg = kind == "dg" and aux == 0
                    last_od = kind == "od" and m == 4 * qi - 1
                    last_dg = kind == "dg" and aux == 3
                    flat.append(dict(h=h, qi=qi, m=m, p=p, grp=grp, n_od=n_od_ev,
                                     has_od=has_od, first=(first_od or first_dg),
                                     first_dg_od=(first_dg and has_od),
                                     start=(first_od or (first_dg and not has_od)),
                                     stop=(last_od or last_dg)))
                grp += 1
            DELAY = 2
            for idx in range(len(flat) + DELAY):
                if idx < len(flat):
                    f = flat[idx]
                    if f["p"] >= 4:
                        tn.wait_ge(S["sex"].sem, f["p"] - 3)
                    tn.matmul(pb[f["p"] % 4][:, :],
                              K_sb[:, f["h"] * T + f["m"] * 128:f["h"] * T + (f["m"] + 1) * 128],
                              Q_sb[:, f["h"] * T + f["qi"] * TT:f["h"] * T + (f["qi"] + 1) * TT],
                              start=True, stop=True).then_inc(*S["ssc"].inc(1))
                if idx >= DELAY:
                    f = flat[idx - DELAY]
                    if f["start"]:
                        tn.wait_ge(S["scmb"].sem, f["grp"])
                    if f["first_dg_od"] and f["m"] == 4 * f["qi"]:
                        tn.wait_ge(S["sod"].sem, f["n_od"])
                    tn.wait_ge(S["sex"].sem, f["p"] + 1)
                    mm = None
                    for sub in range(4):
                        mm = tn.matmul(pb[4 + sub][:, 0:129],
                                       P_pool[f["p"] % 4][:, sub * 128:(sub + 1) * 128],
                                       V_sb[:, f["h"] * 64 * VW + f["m"] * VW:
                                            f["h"] * 64 * VW + f["m"] * VW + 129],
                                       start=(f["start"] or (f["first_dg_od"])),
                                       stop=f["stop"])
                    mm.then_inc(*S["sav"].inc(1))

    @block.vector
    def _(v):
        v.memset(ones_bf[:, :], 1.0)
        v.memset(onesf[:, :], 1.0)
        v.memset(one1[:, :], 1.0)
        v.memset(V_sb[:, :], 1.0).then_inc(*S["setup"].inc(1))
        for wi in range(3):
            v.wait_ge(S["tsu"].sem, wi + 1)
            if wi < 2:
                v.tensor_scalar_mul(wsrow[0:1, wi * 256:(wi + 1) * 256], pb[5][0:1, 0:256],
                                    -1.0).then_inc(*S["vsu"].inc(1))
            else:
                v.tensor_copy(wsrow[0:1, wi * 256:(wi + 1) * 256],
                              pb[5][0:1, 0:256]).then_inc(*S["vsu"].inc(1))
        v.wait_ge(S["tsu"].sem, 4)
        v.tensor_copy(negws[:, 0:4], pb[5][:, 0:4])
        v.tensor_copy(wvb[:, :], pb[5][:, 8:264]).then_inc(*S["vsu"].inc(1))
        for t in range(NTT):
            b2 = t % 2
            v.wait_ge(S["stats"].sem, t + 1)
            v.tensor_scalar_mul(rw[0], pb[6][0:1, :], 1.0 / D)
            v.tensor_scalar_mul(rw[1], pb[7][0:1, :], 1.0 / D)
            v.tensor_mul(rw[2], rw[0], rw[0])
            v.tensor_sub(rw[3], rw[1], rw[2])
            v.tensor_scalar_add(rw[1], rw[3], float(EPS))
            v.reciprocal(rw[2], rw[1]).then_inc(*S["v1"].inc(1))
            v.wait_ge(S["s2"].sem, t + 1)
            v.tensor_mul(rw[1], rw[0], rw[3]).then_inc(*S["v2"].inc(1))
            v.wait_ge(S["bc"].sem, t + 1)
            v.tensor_copy(a_b[:, :], pb[6][:, :])
            v.tensor_copy(b_b[:, :], pb[7][:, :])
            v.tensor_copy(colp[b2][:, 0:8], pb[5][:, 0:8]).then_inc(*S["evac"].inc(1))
            v.wait_ge(S["qk"].sem, t + 1)
            for pi, (pbo, dst, nwo) in enumerate([(0, Q_sb, 0), (2, K_sb, 2)]):
                for fo in range(2):
                    v.tensor_mul(tmpf[fo][:, :], pb[pbo + fo][:, :], a_b[:, :])
                    last = v.scalar_tensor_tensor(
                        dst[:, fo * T + t * TT:fo * T + (t + 1) * TT],
                        b_b[:, :], negws[:, nwo + fo:nwo + fo + 1], tmpf[fo][:, :],
                        ALU.mult, ALU.add)
            last.then_inc(*S["qkfix"].inc(1))
            for rnd in range(2):
                v.wait_ge(S["vmm"].sem, 2 * t + rnd + 1)
                for si in range(2):
                    sub = 2 * rnd + si
                    g64 = t * 4 + sub
                    v.tensor_scalar(tmpv[0][:, :], wvb[:, :],
                                    colp[b2][:, 4 + sub:5 + sub], None, ALU.mult)
                    v.tensor_scalar(tmpv[1][:, :], pb[4][:, si * 256:(si + 1) * 256],
                                    colp[b2][:, sub:sub + 1], None, ALU.mult)
                    for fo in range(2):
                        last = v.tensor_sub(
                            V_sb[:, fo * 64 * VW + g64 * VW:fo * 64 * VW + g64 * VW + 128],
                            tmpv[1][:, fo * 128:(fo + 1) * 128],
                            tmpv[0][:, fo * 128:(fo + 1) * 128])
                last.then_inc(*S["vevac"].inc(1))
        if has1b:
            v.wait_ge(S["dw"].sem, 96)
            grp = 0
            n_od_ev = 0
            oid = 0
            for (h, qi, ms) in sched:
                has_od = qi > 0
                for (m, p, kind, aux, dgi) in ms:
                    if kind == "dg":
                        v.wait_ge(S["ssc"].sem, p + 1)
                        if dgi >= 2:
                            v.wait_ge(S["sex"].sem, diag_pass_of[dgi - 2] + 1)
                        v.scalar_tensor_tensor(
                            tmpS[dgi % 2][:, :], pb[p % 4][:, :], ISQ,
                            md_sb[:, (h * 4 + aux) * TT:(h * 4 + aux + 1) * TT],
                            ALU.mult, ALU.add).then_inc(*S["sdg"].inc(1))
                    if kind == "od" and m == 4 * qi - 1:
                        v.wait_ge(S["sav"].sem, p + 1)
                        for sub in range(4):
                            last = v.tensor_copy(odsb[:, sub * VW:sub * VW + 129],
                                                 pb[4 + sub][:, 0:129])
                        n_od_ev += 1
                        last.then_inc(*S["sod"].inc(1))
                p_last = ms[-1][1]
                v.wait_ge(S["sav"].sem, p_last + 1)
                for sub in range(4):
                    if has_od:
                        v.tensor_scalar(cmb[:, sub * VW:sub * VW + 129],
                                        odsb[:, sub * VW:sub * VW + 129],
                                        ec_sb[:, h * 4 + sub:h * 4 + sub + 1], None, ALU.mult)
                        last = v.tensor_add(cmb[:, sub * VW:sub * VW + 129],
                                            cmb[:, sub * VW:sub * VW + 129],
                                            pb[4 + sub][:, 0:129])
                    else:
                        last = v.tensor_copy(cmb[:, sub * VW:sub * VW + 129],
                                             pb[4 + sub][:, 0:129])
                last.then_inc(*S["scmb"].inc(1))
                for sub in range(4):
                    v.reciprocal(rcp[:, sub:sub + 1], cmb[:, sub * VW + 128:sub * VW + 129])
                for sub in range(4):
                    slot = oid % 8
                    if oid >= 8:
                        v.wait_ge(S["soc"].sem, 16 * (oid - 7))
                    v.tensor_scalar(outp[:, slot * 128:(slot + 1) * 128],
                                    cmb[:, sub * VW:sub * VW + 128],
                                    rcp[:, sub:sub + 1], None, ALU.mult) \
                        .then_inc(*S["sot"].inc(1))
                    oid += 1
                grp += 1

    @block.sync
    def _(sy):
        if has1b:
            oid = 0
            for (h, qi, ms) in sched:
                for sub in range(4):
                    slot = oid % 8
                    sy.wait_ge(S["sot"].sem, oid + 1)
                    sy.dma_start(out=cc_in[qi * TT + sub * 128:qi * TT + (sub + 1) * 128,
                                           h * 128:(h + 1) * 128],
                                 in_=outp[:, slot * 128:(slot + 1) * 128]) \
                        .then_inc(*S["soc"].inc(16))
                    oid += 1
        if debug:
            sy.wait_ge(S["qkfix"].sem, NTT)
            sy.wait_ge(S["vevac"].sem, 2 * NTT)
            sy.dma_start(out=dbg["qT"], in_=Q_sb[:, :]).then_inc(*S["yst"].inc(16))
            sy.dma_start(out=dbg["kT"], in_=K_sb[:, :]).then_inc(*S["yst"].inc(16))
            sy.dma_start(out=dbg["v"], in_=V_sb[:, :]).then_inc(*S["yst"].inc(16))

    p1.close()
    p2 = ExitStack()
    sb2 = lambda name, shape, dt=F32: p2.enter_context(nc.sbuf_tensor(name, shape, dt))
    if has2:
        Y_sb = sb2("Y_sb", [128, DC * TPC])
        rows2 = sb2("rows2", [1, 4 * TT])
        rw2 = [rows2[0:1, i * TT:(i + 1) * TT] for i in range(4)]
        m2b = sb2("m2b", [128, TT])
        r2b = sb2("r2b", [128, TT])
        onesf2 = sb2("onesf2", [1, 128])
        ones_bf2 = sb2("ones_bf2", [128, 8], BF16)
    p2a = ExitStack()
    if has1b:
        outT_sb = p2a.enter_context(nc.sbuf_tensor("outT_sb", [128, DC * TPC], BF16))

        @block.sync
        def _(sy):
            rid = sy.partition_id()
            sy.wait_ge(S["scc"].sem, rid + 1)
            cc5 = cc_out.rearrange("(s r q) m -> s r q m", r=NC, q=TPC)
            stg = cc_stage.rearrange("(r q) m -> r q m", q=TPC)
            sy.dma_start(out=stg, in_=cc5[ts(rid, 1), :, :, :][0]).then_inc(*S["rcv"].inc(16))
            sy.wait_ge(S["rcv"].sem, 16)
            for fc in range(DC):
                r, cb = fc // 2, fc % 2
                sy.dma_start_transpose(
                    outT_sb[:, fc * TPC:(fc + 1) * TPC],
                    cc_stage[r * TPC:(r + 1) * TPC, cb * 128:(cb + 1) * 128]) \
                    .then_inc(*S["rcv"].inc(16))
            sy.wait_ge(S["rcv"].sem, 16 * 17)
            if debug:
                sy.dma_start(out=dbg["attnT"], in_=outT_sb[:, :]).then_inc(*S["yst"].inc(16))


    if has2:
        woT_sb = p2a.enter_context(nc.sbuf_tensor("woT_sb", [128, DC * D], BF16))
        xow = [p2a.enter_context(nc.sbuf_tensor(f"xow{i}", [128, TT], F32)) for i in range(2)]

        NWO = 32

        @block.vector
        def _(v):
            v.wait_ge(S["soc"].sem, 128 * 16)
            if debug:
                v.wait_ge(S["yst"].sem, 48)
            v.memset(onesf2[:, :], 1.0)
            v.memset(ones_bf2[:, :], 1.0).then_inc(*S["setup"].inc(1))
            for idx in range(NWO):
                v.wait_ge(S["swo"].sem, idx + 1)
                v.wait_ge(S[f"dxo{idx % 2}"].sem, 16 * (idx // 2 + 1))
                fo, tt = idx // 2, idx % 2
                v.tensor_add(Y_sb[:, fo * TPC + tt * TT:fo * TPC + (tt + 1) * TT],
                             pb[idx % 2][:, :], xow[idx % 2][:, :]) \
                    .then_inc(*S["y1e"].inc(1))

        @block.gpsimd
        def _(g):
            if debug:
                g.wait_ge(S["yst"].sem, 48)
            for i in range(4):
                g.dma_start(out=woT_sb[:, i * 4 * D:(i + 1) * 4 * D],
                            in_=wo_st[:, i * 4 * D:(i + 1) * 4 * D]).then_inc(*S["dwo"].inc(16))

        @block.sync
        def _(sy):
            for idx in range(NWO):
                if idx >= 2:
                    sy.wait_ge(S["y1e"].sem, idx - 1)
                sy.dma_start(out=xow[idx % 2][:, :], in_=xown_st[idx]) \
                    .then_inc(*S[f"dxo{idx % 2}"].inc(16))

        @block.tensor
        def _(tn):
            tn.wait_ge(S["dwo"].sem, 64)
            tn.wait_ge(S["rcv"].sem, 16 * 17)
            tn.wait_ge(S["setup"].sem, 2)
            for idx in range(NWO):
                fo, tt = idx // 2, idx % 2
                if idx >= 2:
                    tn.wait_ge(S["y1e"].sem, idx - 1)
                for di in range(DC):
                    mm = tn.matmul(pb[idx % 2][:, :],
                                   woT_sb[:, di * D + fo * 128:di * D + (fo + 1) * 128],
                                   outT_sb[:, di * TPC + tt * TT:di * TPC + (tt + 1) * TT],
                                   start=(di == 0), stop=(di == DC - 1))
                mm.then_inc(*S["swo"].inc(1))

        p2a.close()
        xn2_sb = sb2("xn2_sb", [128, DC * TPC], BF16)
        act_sb = sb2("act_sb", [128, FSC * TPC], BF16)
        gu_buf = [sb2(f"gu_buf{i}", [128, DC * 256], BF16) for i in range(2)]
        dn_buf = [sb2(f"dn_buf{i}", [128, FSC * 128], BF16) for i in range(2)]
        gate_sb = sb2("gate_sb", [128, DC * 8], BF16)
        idn_sb = sb2("idn_sb", [128, 128])
        sel_sb = sb2("sel_sb", [8, 8 * 128])
        cT_sb = sb2("cT_sb", [8, TPC])
        ybf = [sb2(f"ybf{i}", [128, TT], BF16) for i in range(2)]
        ysq = [sb2(f"ysq{i}", [128, TT], BF16) for i in range(2)]
        sg_sb = [sb2(f"sg_sb{i}", [128, TT], BF16) for i in range(2)]
        u1_sb = [sb2(f"u1_sb{i}", [128, TT], BF16) for i in range(2)]
        tmp2 = [sb2("tmp2_0", [128, TT])] * 2
        cbt_ = [m2b, r2b]
        gsc = sb2("gsc", [128, 64])
        gs2 = sb2("gs2", [128, 64])
        gct = sb2("gct", [128, 64])
        gtA = sb2("gtA", [128, 64])
        gtB = sb2("gtB", [128, 64])
        gcl = sb2("gcl", [128, 64])

        JCs = [FC] * E8 + [FSC]
        EORDER = [E8] + list(range(E8))
        JPRE = {}
        acc = 0
        for e in EORDER:
            JPRE[e] = acc
            acc += JCs[e]
        TJC = acc
        CBPOS = {e: i for i, e in enumerate(EORDER) if e < E8}
        jgl = []
        for e in EORDER:
            for j in range(JCs[e]):
                jgl.append((e, j, JPRE[e] + j))
        hgl = []
        for pos, e in enumerate(EORDER):
            for h in range(DC):
                hgl.append((e, h, pos * DC + h))

        @block.gpsimd
        def _(g):
            g.wait_ge(S["y1e"].sem, 32)
            g.dma_start(out=gate_sb[:, :], in_=gate_st).then_inc(*S["dwo"].inc(16))
            g.dma_start(out=idn_sb[:, :], in_=idn_st).then_inc(*S["dwo"].inc(16))
            g.dma_start(out=sel_sb[:, :], in_=sel_st).then_inc(*S["dwo"].inc(16))
            for pos, e in enumerate(EORDER):
                for j in range(JCs[e]):
                    jjx = JPRE[e] + j
                    if jjx >= 2:
                        g.wait_ge(S["gups"].sem, 4 * (jjx - 1))
                    src = gu_st[e, j] if e < E8 else sgu_st[j]
                    g.dma_start(out=gu_buf[jjx % 2][:, :], in_=src) \
                        .then_inc(*S[f"dgu{jjx % 2}"].inc(16))
                for h in range(DC):
                    hhx = pos * DC + h
                    if hhx >= 2:
                        g.wait_ge(S["dnps"].sem, 2 * (hhx - 1))
                    if e < E8:
                        g.dma_start(out=dn_buf[hhx % 2][:, 0:FC * 128], in_=dn_st[e, h]) \
                            .then_inc(*S[f"ddn{hhx % 2}"].inc(16))
                    else:
                        g.dma_start(out=dn_buf[hhx % 2][:, :], in_=sdn_st[h]) \
                            .then_inc(*S[f"ddn{hhx % 2}"].inc(16))

        @block.scalar
        def _(s):
            for tt in range(2):
                for di in range(DC):
                    idx = tt * DC + di
                    s.wait_ge(S["v21"].sem, idx + 1)
                    if idx >= 2:
                        s.wait_ge(S["st2"].sem, 2 * (idx - 1))
                    s.activation(out=ysq[di % 2][:, :], in_=ybf[di % 2][:, :],
                                 func=AF.Square).then_inc(*S["s21"].inc(1))
                s.wait_ge(S["v22"].sem, tt + 1)
                s.activation(out=rw2[3], in_=rw2[2], func=AF.Sqrt).then_inc(*S["s22"].inc(1))
            s.wait_ge(S["g1"].sem, 16)
            for g8 in range(8):
                s.activation(out=gsc[:, 8 * g8:8 * g8 + 8], in_=pb[5][:, 8 * g8:8 * g8 + 8],
                             func=AF.Exp, bias=gcl[:, 8 + g8:9 + g8], scale=1.0) \
                    .then_inc(*S["g2"].inc(1))
            for (e, j, jjx) in jgl:
                for tt in range(2):
                    s.wait_ge(S["gups"].sem, 4 * jjx + 2 * tt + 1)
                    s.activation(out=sg_sb[tt][:, :], in_=pb[2 * tt][:, :],
                                 func=AF.Silu).then_inc(*S["sil"].inc(1))

        @block.tensor
        def _(tn):
            for tt in range(2):
                for di in range(DC):
                    idx = tt * DC + di
                    tn.wait_ge(S["s21"].sem, idx + 1)
                    if tt > 0 and di == 0:
                        tn.wait_ge(S["ev2"].sem, tt)
                    tn.matmul(pb[7][0:1, :], ones_bf2[:, 0:1], ysq[di % 2][:, :],
                              start=(di == 0), stop=(di == DC - 1)) \
                        .then_inc(*S["st2"].inc(1))
                    mm = tn.matmul(pb[6][0:1, :], ones_bf2[:, 0:1], ybf[di % 2][:, :],
                                   start=(di == 0), stop=(di == DC - 1)) \
                        .then_inc(*S["st2"].inc(1))
                tn.wait_ge(S["v23"].sem, tt + 1)
                tn.matmul(pb[6][:, :], onesf2[0:1, :], rw2[0], start=True, stop=True)
                tn.matmul(pb[7][:, :], onesf2[0:1, :], rw2[3], start=True, stop=True) \
                    .then_inc(*S["bc2"].inc(1))
            tn.wait_ge(S["xn2d"].sem, 32)
            tn.wait_ge(S["dwo"].sem, 112)
            for g8 in range(8):
                for di in range(DC):
                    mm = tn.matmul(pb[5][:, 8 * g8:8 * g8 + 8],
                                   xn2_sb[:, di * TPC + g8 * 128:di * TPC + (g8 + 1) * 128],
                                   gate_sb[:, di * 8:(di + 1) * 8],
                                   start=(di == 0), stop=(di == DC - 1))
                mm.then_inc(*S["glog"].inc(1))
            tn.wait_ge(S["gcmb"].sem, 8)
            for rnd in range(2):
                if rnd:
                    tn.wait_ge(S["gsel"].sem, rnd)
                for gi in range(4):
                    g8 = rnd * 4 + gi
                    tn.transpose(pb[4][0:8, gi * 128:(gi + 1) * 128],
                                 gct[:, 8 * g8:8 * g8 + 8], idn_sb[:, :]) \
                        .then_inc(*S["gtr"].inc(1))
            for pos, e in enumerate(EORDER):
                if e < E8:
                    tn.wait_ge(S["dnac"].sem, 32 * pos)
                    tn.wait_ge(S["gsel"].sem, 2)
                    for tt in range(2):
                        tn.matmul(pb[4 + tt][:, :], sel_sb[0:8, e * 128:(e + 1) * 128],
                                  cT_sb[0:8, tt * TT:(tt + 1) * TT], start=True, stop=True) \
                            .then_inc(*S["cbt"].inc(1))
                for j in range(JCs[e]):
                    jjx = JPRE[e] + j
                    tn.wait_ge(S[f"dgu{jjx % 2}"].sem, 16 * (jjx // 2 + 1))
                    if e < E8 and j == 0:
                        tn.wait_ge(S["cbk"].sem, 2 * CBPOS[e])
                    for tt in range(2):
                        if jjx >= 1:
                            tn.wait_ge(S["sil"].sem, 2 * (jjx - 1) + tt + 1)
                            tn.wait_ge(S["uact"].sem, 2 * (jjx - 1) + tt + 1)
                        for di in range(DC):
                            mm = tn.matmul(pb[2 * tt][:, :],
                                           gu_buf[jjx % 2][:, di * 256:di * 256 + 128],
                                           xn2_sb[:, di * TPC + tt * TT:di * TPC + (tt + 1) * TT],
                                           start=(di == 0), stop=(di == DC - 1))
                        mm.then_inc(*S["gups"].inc(1))
                        for di in range(DC):
                            mm = tn.matmul(pb[2 * tt + 1][:, :],
                                           gu_buf[jjx % 2][:, di * 256 + 128:di * 256 + 256],
                                           xn2_sb[:, di * TPC + tt * TT:di * TPC + (tt + 1) * TT],
                                           start=(di == 0), stop=(di == DC - 1))
                        mm.then_inc(*S["gups"].inc(1))
                JC = JCs[e]
                for h in range(DC):
                    hhx = pos * DC + h
                    tn.wait_ge(S[f"ddn{hhx % 2}"].sem, 16 * (hhx // 2 + 1))
                    tn.wait_ge(S["uact"].sem, 2 * (JPRE[e] + JC))
                    for tt in range(2):
                        slot = 2 * hhx + tt
                        if slot >= 2:
                            tn.wait_ge(S["dnac"].sem, slot - 1)
                        for fj in range(JC):
                            mm = tn.matmul(pb[4 + tt][:, :],
                                           dn_buf[hhx % 2][:, fj * 128:(fj + 1) * 128],
                                           act_sb[:, fj * TPC + tt * TT:fj * TPC + (tt + 1) * TT],
                                           start=(fj == 0), stop=(fj == JC - 1))
                        mm.then_inc(*S["dnps"].inc(1))

        @block.vector
        def _(v):
            v.wait_ge(S["y1e"].sem, 32)
            for tt in range(2):
                for di in range(DC):
                    idx = tt * DC + di
                    if idx >= 2:
                        v.wait_ge(S["s21"].sem, idx - 1)
                    v.tensor_copy(ybf[di % 2][:, :],
                                  Y_sb[:, di * TPC + tt * TT:di * TPC + (tt + 1) * TT]) \
                        .then_inc(*S["v21"].inc(1))
                v.wait_ge(S["st2"].sem, 32 * (tt + 1))
                v.tensor_scalar_mul(rw2[0], pb[6][0:1, :], 1.0 / D)
                v.tensor_scalar_mul(rw2[1], pb[7][0:1, :], 1.0 / D)
                v.tensor_mul(rw2[2], rw2[0], rw2[0])
                v.tensor_sub(rw2[3], rw2[1], rw2[2])
                v.tensor_scalar_add(rw2[1], rw2[3], float(EPS))
                v.reciprocal(rw2[2], rw2[1]).then_inc(*S["v22"].inc(1))
                v.wait_ge(S["s22"].sem, tt + 1)
                v.engine_nop().then_inc(*S["v23"].inc(1))
                v.wait_ge(S["bc2"].sem, tt + 1)
                v.tensor_copy(m2b[:, :], pb[6][:, :])
                v.tensor_copy(r2b[:, :], pb[7][:, :]).then_inc(*S["ev2"].inc(1))
                for di in range(DC):
                    v.tensor_sub(tmp2[di % 2][:, :],
                                 Y_sb[:, di * TPC + tt * TT:di * TPC + (tt + 1) * TT], m2b[:, :])
                    v.tensor_mul(xn2_sb[:, di * TPC + tt * TT:di * TPC + (tt + 1) * TT],
                                 tmp2[di % 2][:, :], r2b[:, :]).then_inc(*S["xn2d"].inc(1))
            v.wait_ge(S["glog"].sem, 8)
            for g8 in range(8):
                v.reduce_max(gcl[:, g8:g8 + 1], pb[5][:, 8 * g8:8 * g8 + 8], axis=AX.X)
            for g8 in range(8):
                v.tensor_scalar_mul(gcl[:, 8 + g8:9 + g8], gcl[:, g8:g8 + 1], -1.0)
            v.engine_nop().then_inc(*S["g1"].inc(16))
            for g8 in range(8):
                v.wait_ge(S["g2"].sem, g8 + 1)
                v.reduce_sum(gcl[:, 16 + g8:17 + g8], gsc[:, 8 * g8:8 * g8 + 8], axis=AX.X)
            for g8 in range(8):
                v.reciprocal(gcl[:, 24 + g8:25 + g8], gcl[:, 16 + g8:17 + g8])
            for g8 in range(8):
                v.tensor_scalar(gs2[:, 8 * g8:8 * g8 + 8], gsc[:, 8 * g8:8 * g8 + 8],
                                gcl[:, 24 + g8:25 + g8], None, ALU.mult)
            for g8 in range(8):
                v.reduce_max(gcl[:, 32 + g8:33 + g8], gs2[:, 8 * g8:8 * g8 + 8], axis=AX.X)
            for g8 in range(8):
                v.tensor_scalar(gtA[:, 8 * g8:8 * g8 + 8], gs2[:, 8 * g8:8 * g8 + 8],
                                gcl[:, 32 + g8:33 + g8], None, ALU.is_equal)
            for g8 in range(8):
                v.scalar_tensor_tensor(gtB[:, 8 * g8:8 * g8 + 8], gtA[:, 8 * g8:8 * g8 + 8],
                                       -1e9, gs2[:, 8 * g8:8 * g8 + 8], ALU.mult, ALU.add)
            for g8 in range(8):
                v.reduce_max(gcl[:, 40 + g8:41 + g8], gtB[:, 8 * g8:8 * g8 + 8], axis=AX.X)
            for g8 in range(8):
                v.tensor_scalar(gtA[:, 8 * g8:8 * g8 + 8], gs2[:, 8 * g8:8 * g8 + 8],
                                gcl[:, 40 + g8:41 + g8], None, ALU.is_ge)
            for g8 in range(8):
                v.tensor_mul(gct[:, 8 * g8:8 * g8 + 8], gs2[:, 8 * g8:8 * g8 + 8],
                             gtA[:, 8 * g8:8 * g8 + 8]).then_inc(*S["gcmb"].inc(1))
            for rnd in range(2):
                v.wait_ge(S["gtr"].sem, 4 * (rnd + 1))
                v.tensor_copy(cT_sb[0:8, rnd * TT:(rnd + 1) * TT], pb[4][0:8, 0:512]) \
                    .then_inc(*S["gsel"].inc(1))
            for pos, e in enumerate(EORDER):
                if e < E8:
                    for tt in range(2):
                        v.wait_ge(S["cbt"].sem, 2 * (CBPOS[e] - 1) + tt + 1)
                        v.tensor_copy(cbt_[tt][:, :], pb[4 + tt][:, :]) \
                            .then_inc(*S["cbk"].inc(1))
                for j in range(JCs[e]):
                    jjx = JPRE[e] + j
                    for tt in range(2):
                        v.wait_ge(S["gups"].sem, 4 * jjx + 2 * tt + 2)
                        v.wait_ge(S["sil"].sem, 2 * jjx + tt + 1)
                        if e < E8:
                            v.tensor_mul(u1_sb[tt][:, :], pb[2 * tt + 1][:, :],
                                         cbt_[tt][:, :])
                            v.tensor_mul(act_sb[:, j * TPC + tt * TT:j * TPC + (tt + 1) * TT],
                                         u1_sb[tt][:, :], sg_sb[tt][:, :]) \
                                .then_inc(*S["uact"].inc(1))
                        else:
                            v.tensor_mul(act_sb[:, j * TPC + tt * TT:j * TPC + (tt + 1) * TT],
                                         pb[2 * tt + 1][:, :], sg_sb[tt][:, :]) \
                                .then_inc(*S["uact"].inc(1))
                for h in range(DC):
                    hhx = pos * DC + h
                    for tt in range(2):
                        v.wait_ge(S["dnps"].sem, 2 * hhx + tt + 1)
                        v.tensor_add(Y_sb[:, h * TPC + tt * TT:h * TPC + (tt + 1) * TT],
                                     Y_sb[:, h * TPC + tt * TT:h * TPC + (tt + 1) * TT],
                                     pb[4 + tt][:, :]).then_inc(*S["dnac"].inc(1))

        @block.sync
        def _(sy):
            if debug:
                sy.wait_ge(S["xn2d"].sem, 32)
                sy.dma_start(out=dbg["xn2"], in_=xn2_sb[:, :]).then_inc(*S["yst"].inc(16))
                sy.wait_ge(S["gsel"].sem, 2)
                sy.dma_start(out=dbg["comb"], in_=cT_sb[0:8, :]).then_inc(*S["yst"].inc(16))

            sy.wait_ge(S["dnac"].sem, 32 * 9)
            y3 = yT.rearrange("(a p) m -> a p m", p=128)
            for di in range(DC):
                sy.dma_start(out=y3[di], in_=Y_sb[:, di * TPC:(di + 1) * TPC]) \
                    .then_inc(*S["yst"].inc(16))

    @block.sync
    def _(sy):
        sy.wait_ge(S["yst"].sem, S["yst"].v)

    if not has1b:
        p1.close()
    else:
        if not has2:
            p2a.close()
        p2.close()
    ctx.close()
    return nc



def stage_inputs(inp, phase="full"):
    x = np.asarray(inp["L0_attn_x"]).reshape(T, D)
    Wq = np.asarray(inp["L0_attn_Wq"])
    Wk = np.asarray(inp["L0_attn_Wk"])
    Wv = np.asarray(inp["L0_attn_Wv"])

    x_st = np.ascontiguousarray(
        x.reshape(NTT, TT, DC, 128).transpose(0, 3, 2, 1).reshape(NTT, 128, DC * TT))

    def wslice(W, c):
        hA, hB = head_pair(c)
        WT = W.T
        wc = np.concatenate([WT[:, 128 * hA:128 * (hA + 1)], WT[:, 128 * hB:128 * (hB + 1)]], axis=1)
        return np.ascontiguousarray(wc.reshape(DC, 128, 256).transpose(1, 0, 2).reshape(128, DC * 256))

    in_maps = []
    for c in range(NC):
        hA, hB = head_pair(c)
        sl = (slope(hA), slope(hB))
        kl = np.arange(128, dtype=np.float64)[:, None]
        ql = np.arange(TT, dtype=np.float64)[None, :]
        md = np.empty((128, 8 * TT), np.float32)
        for hi in range(2):
            for j in range(4):
                mm = sl[hi] * (128 * j + kl - ql)
                mm = np.where(128 * j + kl <= ql, mm, NEG_BIG)
                md[:, (hi * 4 + j) * TT:(hi * 4 + j + 1) * TT] = mm.astype(np.float32)
        bc = np.zeros((128, 128), np.float32)
        for hi in range(2):
            for dl in range(64):
                bc[:, hi * 64 + dl] = (sl[hi] * (kl[:, 0] - 128.0 * dl)).astype(np.float32)
        ec = np.zeros((128, 8), np.float32)
        for hi in range(2):
            for sub in range(4):
                ec[:, hi * 4 + sub] = np.exp(-sl[hi] * (128.0 * sub + kl[:, 0])).astype(np.float32)
        m = {
            "x_st": x_st, "wq_st": wslice(Wq, c), "wk_st": wslice(Wk, c), "wv_st": wslice(Wv, c),
            "md_st": md, "bc_st": bc, "ec_st": ec,
        }
        if phase == "full":
            m.update(stage_phase2(inp, c))
        in_maps.append(m)
    return in_maps


def stage_phase2(inp, c):
    Wo = np.asarray(inp["L0_attn_Wo"])
    gate = np.asarray(inp["L0_ffn_gate_weight"])
    gu = np.asarray(inp["L0_ffn_gate_up_weights"])
    dn = np.asarray(inp["L0_ffn_down_weights"])
    sgu = np.asarray(inp["L0_ffn_shared_gate_up"])
    sdn = np.asarray(inp["L0_ffn_shared_down"])
    x = np.asarray(inp["L0_attn_x"]).reshape(T, D)

    perm = []
    for r in range(NC):
        ha, hb = head_pair(r)
        perm.extend(range(128 * ha, 128 * ha + 128))
        perm.extend(range(128 * hb, 128 * hb + 128))
    woT = Wo.T[np.asarray(perm), :]
    wo_st = np.ascontiguousarray(woT.reshape(DC, 128, D).transpose(1, 0, 2).reshape(128, DC * D))
    xT = x.T
    xown = xT[:, TPC * c:TPC * (c + 1)]
    xown_st = np.ascontiguousarray(
        xown.reshape(DC, 128, 2, TT).transpose(0, 2, 1, 3).reshape(32, 128, TT))
    gate_st = np.ascontiguousarray(gate.T.reshape(DC, 128, 8).transpose(1, 0, 2).reshape(128, DC * 8))
    idn = np.eye(128, dtype=np.float32)
    sel = np.zeros((8, 8 * 128), np.float32)
    for e in range(8):
        sel[e, e * 128:(e + 1) * 128] = 1.0
    gug = gu[:, :F, :].reshape(E8, FC, 128, DC, 128)
    guu = gu[:, F:, :].reshape(E8, FC, 128, DC, 128)
    gu_st = np.empty((E8, FC, 128, DC * 256), np.float32)
    gg = gug.transpose(0, 1, 4, 3, 2)
    uu = guu.transpose(0, 1, 4, 3, 2)
    gu_st = np.ascontiguousarray(
        np.stack([gg, uu], axis=4).reshape(E8, FC, 128, DC, 2 * 128).reshape(E8, FC, 128, DC * 256))
    dn_st = np.ascontiguousarray(
        dn.reshape(E8, DC, 128, FC, 128).transpose(0, 1, 4, 3, 2).reshape(E8, DC, 128, FC * 128))
    sgug = sgu[:FS, :].reshape(FSC, 128, DC, 128)
    sguu = sgu[FS:, :].reshape(FSC, 128, DC, 128)
    sgu_st = np.ascontiguousarray(
        np.stack([sgug.transpose(0, 3, 2, 1), sguu.transpose(0, 3, 2, 1)], axis=3)
        .reshape(FSC, 128, DC, 256).reshape(FSC, 128, DC * 256))
    sdn_st = np.ascontiguousarray(
        sdn.reshape(DC, 128, FSC, 128).transpose(0, 3, 2, 1).reshape(DC, 128, FSC * 128))
    return {
        "wo_st": wo_st, "xown_st": xown_st, "gate_st": gate_st,
        "idn_st": idn, "sel_st": sel,
        "gu_st": gu_st, "dn_st": dn_st, "sgu_st": sgu_st, "sdn_st": sdn_st,
    }



def kernel(**inputs):
    in_maps = stage_inputs(inputs, phase="full")
    nc = build_nc(phase="full", debug=False)
    res = run_bass_kernel_spmd(nc, in_maps, core_ids=list(range(NC)))
    out = np.empty((1, T, D), np.float32)
    for c in range(NC):
        out[0, TPC * c:TPC * (c + 1), :] = np.asarray(res.results[c]["yT"]).T
    return out


# version 06
# speedup vs baseline: 1.3357x; 1.0062x over previous
import sys
sys.path.insert(0, "/opt/trn_rl_repo")
import numpy as np
from contextlib import ExitStack
import concourse.bass as bass
import concourse.mybir as mybir
from concourse.bass_utils import run_bass_kernel_spmd

F32 = mybir.dt.float32
BF16 = mybir.dt.bfloat16
AF = mybir.ActivationFunctionType
ALU = mybir.AluOpType
AX = mybir.AxisListType
ts = bass.ts

T, D, NH, HD, NC = 8192, 2048, 16, 128, 8
TT = 512
NTT = T // TT
DC = D // 128
TPC = T // NC
E8, F, FS = 8, 1408, 2816
FC, FSC = F // 128, FS // 128
EPS = 1e-5
ISQ = float(1.0 / np.sqrt(HD))
VW = 136
NEG_BIG = -1e30


def head_pair(c):
    return c, 15 - c


def slope(h):
    return 2.0 ** (-(h + 1) / 2.0)


class Ctr:
    def __init__(self, nc, name, ctx):
        self.sem = ctx.enter_context(nc.semaphore(name))
        self.v = 0

    def inc(self, n=1):
        self.v += n
        return self.sem, n


def build_nc(phase="full", debug=False):
    nc = bass.Bass("TRN2", target_bir_lowering=False, debug=False, detect_race_conditions=False)
    ctx = ExitStack()
    dr = lambda name, shape, dt, kind="ExternalInput": nc.dram_tensor(name, shape, dt, kind=kind).ap()

    x_st = dr("x_st", [NTT, 128, DC * TT], F32)
    wq_st = dr("wq_st", [128, DC * 256], F32)
    wk_st = dr("wk_st", [128, DC * 256], F32)
    wv_st = dr("wv_st", [128, DC * 256], F32)
    md_st = dr("md_st", [128, 8 * TT], F32)
    bc_st = dr("bc_st", [128, 128], F32)
    ec_st = dr("ec_st", [128, 8], F32)
    has2 = phase == "full"
    if has2:
        wo_st = dr("wo_st", [128, DC * D], F32)
        xown_st = dr("xown_st", [32, 128, TT], F32)
        gate_st = dr("gate_st", [128, DC * 8], F32)
        idn_st = dr("idn_st", [128, 128], F32)
        sel_st = dr("sel_st", [8, 8 * 128], F32)
        gu_st = dr("gu_st", [E8, FC, 128, DC * 256], F32)
        dn_st = dr("dn_st", [E8, DC, 128, FC * 128], F32)
        sgu_st = dr("sgu_st", [FSC, 128, DC * 256], F32)
        sdn_st = dr("sdn_st", [DC, 128, FSC * 128], F32)
        yT = dr("yT", [D, TPC], F32, "ExternalOutput")
    has1b = phase in ("1b", "full")
    if has1b:
        cc_in = nc.dram_tensor("cc_in", [T, 2 * HD], BF16).ap()
        cc_out = nc.dram_tensor("cc_out", [NC * T, 2 * HD], BF16).ap()
        cc_stage = nc.dram_tensor("cc_stage", [NC * TPC, 2 * HD], BF16).ap()

    dbg = {}
    if debug:
        dbg["qT"] = dr("dbg_qT", [128, 2 * T], BF16, "ExternalOutput")
        dbg["kT"] = dr("dbg_kT", [128, 2 * T], BF16, "ExternalOutput")
        dbg["v"] = dr("dbg_v", [128, 2 * 64 * VW], BF16, "ExternalOutput")
        if has1b:
            dbg["attnT"] = dr("dbg_attnT", [128, DC * TPC], BF16, "ExternalOutput")
        if has2:
            dbg["y1"] = dr("dbg_y1", [128, DC * TPC], F32, "ExternalOutput")
            dbg["xn2"] = dr("dbg_xn2", [128, DC * TPC], BF16, "ExternalOutput")
            dbg["comb"] = dr("dbg_comb", [8, TPC], F32, "ExternalOutput")


    sbuf = lambda name, shape, dt=F32: ctx.enter_context(nc.sbuf_tensor(name, shape, dt))
    pb = [ctx.enter_context(nc.psum_tensor(f"pb{i}", [128, 512], F32)) for i in range(8)]

    S = {}
    for name in ["dx0", "dx1", "dw", "sq", "sqcons", "stats", "v1", "s2", "v2", "bc",
                 "evac", "qk", "qkfix", "vmm", "vevac", "xcons", "setup",
                 "tsu", "vsu", "p1a",
                 "ssc", "sdg", "sex", "sav", "sod", "scmb", "sot", "soc",
                 "scc", "rcv", "pb1",
                 "dwo", "swo", "svo", "y1e", "st2", "v21", "s21", "v22",
                 "bc2", "ev2", "xn2d", "glog", "gcmb", "gsel",
                 "dgu0", "dgu1", "ddn0", "ddn1", "gups", "sil", "uact", "dnps", "dnac",
                 "cbk", "cbt", "dxo0", "dxo1", "s22", "v23", "g1", "g2", "gtr",
                 "yst"]:
        S[name] = Ctr(nc, name, ctx)

    block = ctx.enter_context(nc.Block())

    p1 = ExitStack()
    sb1 = lambda name, shape, dt=F32: p1.enter_context(nc.sbuf_tensor(name, shape, dt))
    wq_sb = sb1("wq_sb", [128, DC * 256], BF16)
    wk_sb = sb1("wk_sb", [128, DC * 256], BF16)
    wv_sb = sb1("wv_sb", [128, DC * 256], BF16)
    xbuf = [sb1(f"xbuf{i}", [128, DC * TT], BF16) for i in range(2)]
    xsq = [sb1(f"xsq{i}", [128, TT], BF16) for i in range(4)]
    Q_sb = sb1("Q_sb", [128, 2 * T], BF16)
    K_sb = sb1("K_sb", [128, 2 * T], BF16)
    V_sb = sb1("V_sb", [128, 2 * 64 * VW], BF16)
    rows = sb1("rows", [1, 4 * TT])
    rw = [rows[0:1, i * TT:(i + 1) * TT] for i in range(4)]
    a_b = sb1("a_b", [128, TT])
    b_b = sb1("b_b", [128, TT])
    colp = [sb1(f"colp{i}", [128, 16]) for i in range(2)]
    negws = sb1("negws", [128, 8])
    wvb = sb1("wvb", [128, 256])
    ones_bf = sb1("ones_bf", [128, 8], BF16)
    onesf = sb1("onesf", [1, 128])
    one1 = sb1("one1", [1, 8])
    wsrow = sb1("wsrow", [1, 3 * 256])
    md_sb = sb1("md_sb", [128, 8 * TT])
    bcc_sb = sb1("bcc_sb", [128, 128])
    ec_sb = sb1("ec_sb", [128, 8])
    P_pool = [sb1(f"Pp{i}", [128, TT], BF16) for i in range(4)]
    tmpS = [sb1(f"tmpS{i}", [128, TT]) for i in range(2)]
    tmpf = tmpS
    tmpv = [tmpS[0][:, 0:256], tmpS[1][:, 0:256]]
    odsb = sb1("odsb", [128, 4 * VW])
    cmb = sb1("cmb", [128, 4 * VW])
    rcp = sb1("rcp", [128, 16])
    outp = sb1("outp", [128, 8 * 128], BF16)

    sched = []
    p_ctr = 0
    dgi_ctr = 0
    diag_pass_of = {}
    for qi in range(NTT):
        for h in range(2):
            ms = []
            for m in range(4 * qi + 4):
                if m < 4 * qi:
                    ms.append((m, p_ctr, "od", 4 * qi - m, None))
                else:
                    diag_pass_of[dgi_ctr] = p_ctr
                    ms.append((m, p_ctr, "dg", m - 4 * qi, dgi_ctr))
                    dgi_ctr += 1
                p_ctr += 1
            sched.append((h, qi, ms))
    NPASS = p_ctr

    @block.gpsimd
    def _(g):
        g.dma_start(out=wq_sb[:, :], in_=wq_st).then_inc(*S["dw"].inc(16))
        g.dma_start(out=wk_sb[:, :], in_=wk_st).then_inc(*S["dw"].inc(16))
        g.dma_start(out=wv_sb[:, :], in_=wv_st).then_inc(*S["dw"].inc(16))
        g.dma_start(out=md_sb[:, :], in_=md_st).then_inc(*S["dw"].inc(16))
        g.dma_start(out=bcc_sb[:, :], in_=bc_st).then_inc(*S["dw"].inc(16))
        g.dma_start(out=ec_sb[:, :], in_=ec_st).then_inc(*S["dw"].inc(16))
        for t in range(NTT):
            if t >= 2:
                g.wait_ge(S["vmm"].sem, 2 * t - 2)
            g.dma_start(out=xbuf[t % 2][:, :], in_=x_st[t]).then_inc(*S[f"dx{t % 2}"].inc(16))
        if has1b:
            for i in range(NC):
                g.wait_ge(S["soc"].sem, 16 * 8 * (2 * i + 2))
                g.collective_compute(
                    "AllGather", ALU.bypass, replica_groups=[list(range(NC))],
                    ins=[cc_in[TPC * i:TPC * (i + 1), :].opt()],
                    outs=[cc_out[NC * TPC * i:NC * TPC * (i + 1), :].opt()],
                ).then_inc(*S["scc"].inc(1))

    @block.scalar
    def _(s):
        for t in range(NTT):
            s.wait_ge(S["v1"].sem, t + 1)
            s.activation(out=rw[3], in_=rw[2], func=AF.Sqrt).then_inc(*S["s2"].inc(1))
        if has1b:
            s.wait_ge(S["dw"].sem, 96)
            for (h, qi, ms) in sched:
                for (m, p, kind, aux, dgi) in ms:
                    if p >= 4:
                        s.wait_ge(S["sav"].sem, p - 3)
                    if kind == "od":
                        s.wait_ge(S["ssc"].sem, p + 1)
                        s.activation(out=P_pool[p % 4][:, :], in_=pb[p % 4][:, :],
                                     func=AF.Exp, bias=bcc_sb[:, h * 64 + aux:h * 64 + aux + 1],
                                     scale=ISQ).then_inc(*S["sex"].inc(1))
                    else:
                        s.wait_ge(S["sdg"].sem, dgi + 1)
                        s.activation(out=P_pool[p % 4][:, :], in_=tmpS[dgi % 2][:, :],
                                     func=AF.Exp).then_inc(*S["sex"].inc(1))

    @block.tensor
    def _(tn):
        tn.wait_ge(S["dw"].sem, 96)
        tn.wait_ge(S["setup"].sem, 1)
        for wi, wsb in enumerate([wq_sb, wk_sb, wv_sb]):
            if wi:
                tn.wait_ge(S["vsu"].sem, wi)
            for di in range(DC):
                mm = tn.matmul(pb[5][0:1, 0:256], ones_bf[:, 0:1], wsb[:, di * 256:(di + 1) * 256],
                               start=(di == 0), stop=(di == DC - 1))
            mm.then_inc(*S["tsu"].inc(1))
        tn.wait_ge(S["vsu"].sem, 3)
        for wi in range(2):
            for fo in range(2):
                cidx = wi * 2 + fo
                tn.matmul(pb[5][:, cidx:cidx + 1],
                          wsrow[0:1, wi * 256 + fo * 128:wi * 256 + (fo + 1) * 128],
                          one1[0:1, 0:1], start=True, stop=True)
        tn.matmul(pb[5][:, 8:264], onesf[0:1, :], wsrow[0:1, 512:768], start=True, stop=True) \
            .then_inc(*S["tsu"].inc(1))
        tn.wait_ge(S["vsu"].sem, 4)
        def stats_1a(t):
            b = t % 2
            tn.wait_ge(S["evac"].sem, t)
            for di in range(DC):
                tn.wait_ge(S["sq"].sem, t * DC + di + 1)
                tn.matmul(pb[7][0:1, :], ones_bf[:, 0:1], xsq[di % 4][:, :],
                          start=(di == 0), stop=(di == DC - 1)) \
                    .then_inc(*S["sqcons"].inc(1))
                mm = tn.matmul(pb[6][0:1, :], ones_bf[:, 0:1], xbuf[b][:, di * TT:(di + 1) * TT],
                               start=(di == 0), stop=(di == DC - 1))
            mm.then_inc(*S["stats"].inc(1))

        def bcast_1a(t):
            tn.wait_ge(S["v2"].sem, t + 1)
            tn.wait_ge(S["evac"].sem, t)
            tn.matmul(pb[6][:, :], onesf[0:1, :], rw[3], start=True, stop=True)
            tn.matmul(pb[7][:, :], onesf[0:1, :], rw[1], start=True, stop=True)
            for sub in range(4):
                tn.matmul(pb[5][:, sub:sub + 1], rw[3][0:1, sub * 128:(sub + 1) * 128],
                          one1[0:1, 0:1], start=True, stop=True)
                mm = tn.matmul(pb[5][:, 4 + sub:5 + sub], rw[1][0:1, sub * 128:(sub + 1) * 128],
                               one1[0:1, 0:1], start=True, stop=True)
            mm.then_inc(*S["bc"].inc(1))

        stats_1a(0)
        bcast_1a(0)
        for t in range(NTT):
            b = t % 2
            tn.wait_ge(S["evac"].sem, t + 1)
            if t > 0:
                tn.wait_ge(S["qkfix"].sem, t)
            for pbo, wsb in ((0, wq_sb), (2, wk_sb)):
                for fo in range(2):
                    for di in range(DC):
                        mm = tn.matmul(pb[pbo + fo][:, :],
                                       wsb[:, di * 256 + fo * 128:di * 256 + (fo + 1) * 128],
                                       xbuf[b][:, di * TT:(di + 1) * TT],
                                       start=(di == 0), stop=(di == DC - 1))
            mm.then_inc(*S["qk"].inc(1))
            if t + 1 < NTT:
                stats_1a(t + 1)
            for rnd in range(2):
                if t > 0 or rnd > 0:
                    tn.wait_ge(S["vevac"].sem, 2 * t + rnd)
                mm = None
                for si in range(2):
                    sub = 2 * rnd + si
                    for di in range(DC):
                        mm = tn.matmul(pb[4][:, si * 256:(si + 1) * 256],
                                       xbuf[b][:, di * TT + sub * 128:di * TT + (sub + 1) * 128],
                                       wv_sb[:, di * 256:(di + 1) * 256],
                                       start=(di == 0), stop=(di == DC - 1))
                mm.then_inc(*S["vmm"].inc(1))
            if t + 1 < NTT:
                bcast_1a(t + 1)
        if has1b:
            tn.wait_ge(S["qkfix"].sem, NTT)
            tn.wait_ge(S["vevac"].sem, 2 * NTT)
            flat = []
            grp = 0
            n_od_ev = 0
            for (h, qi, ms) in sched:
                has_od = qi > 0
                if has_od:
                    n_od_ev += 1
                for (m, p, kind, aux, dgi) in ms:
                    first_od = kind == "od" and m == 0
                    first_dg = kind == "dg" and aux == 0
                    last_od = kind == "od" and m == 4 * qi - 1
                    last_dg = kind == "dg" and aux == 3
                    flat.append(dict(h=h, qi=qi, m=m, p=p, grp=grp, n_od=n_od_ev,
                                     has_od=has_od, first=(first_od or first_dg),
                                     first_dg_od=(first_dg and has_od),
                                     start=(first_od or (first_dg and not has_od)),
                                     stop=(last_od or last_dg)))
                grp += 1
            DELAY = 2
            for idx in range(len(flat) + DELAY):
                if idx < len(flat):
                    f = flat[idx]
                    if f["p"] >= 4:
                        tn.wait_ge(S["sex"].sem, f["p"] - 3)
                    tn.matmul(pb[f["p"] % 4][:, :],
                              K_sb[:, f["h"] * T + f["m"] * 128:f["h"] * T + (f["m"] + 1) * 128],
                              Q_sb[:, f["h"] * T + f["qi"] * TT:f["h"] * T + (f["qi"] + 1) * TT],
                              start=True, stop=True).then_inc(*S["ssc"].inc(1))
                if idx >= DELAY:
                    f = flat[idx - DELAY]
                    if f["start"]:
                        tn.wait_ge(S["scmb"].sem, f["grp"])
                    if f["first_dg_od"] and f["m"] == 4 * f["qi"]:
                        tn.wait_ge(S["sod"].sem, f["n_od"])
                    tn.wait_ge(S["sex"].sem, f["p"] + 1)
                    mm = None
                    for sub in range(4):
                        mm = tn.matmul(pb[4 + sub][:, 0:129],
                                       P_pool[f["p"] % 4][:, sub * 128:(sub + 1) * 128],
                                       V_sb[:, f["h"] * 64 * VW + f["m"] * VW:
                                            f["h"] * 64 * VW + f["m"] * VW + 129],
                                       start=(f["start"] or (f["first_dg_od"])),
                                       stop=f["stop"])
                    mm.then_inc(*S["sav"].inc(1))

    @block.vector
    def _(v):
        v.memset(ones_bf[:, :], 1.0)
        v.memset(onesf[:, :], 1.0)
        v.memset(one1[:, :], 1.0)
        v.memset(V_sb[:, :], 1.0).then_inc(*S["setup"].inc(1))
        for wi in range(3):
            v.wait_ge(S["tsu"].sem, wi + 1)
            if wi < 2:
                v.tensor_scalar_mul(wsrow[0:1, wi * 256:(wi + 1) * 256], pb[5][0:1, 0:256],
                                    -1.0).then_inc(*S["vsu"].inc(1))
            else:
                v.tensor_copy(wsrow[0:1, wi * 256:(wi + 1) * 256],
                              pb[5][0:1, 0:256]).then_inc(*S["vsu"].inc(1))
        v.wait_ge(S["tsu"].sem, 4)
        v.tensor_copy(negws[:, 0:4], pb[5][:, 0:4])
        v.tensor_copy(wvb[:, :], pb[5][:, 8:264]).then_inc(*S["vsu"].inc(1))
        def squares_1a(t):
            b = t % 2
            v.wait_ge(S[f"dx{t % 2}"].sem, 16 * (t // 2 + 1))
            for di in range(DC):
                idx = t * DC + di
                if idx >= 4:
                    v.wait_ge(S["sqcons"].sem, idx - 3)
                v.tensor_mul(xsq[di % 4][:, :], xbuf[b][:, di * TT:(di + 1) * TT],
                             xbuf[b][:, di * TT:(di + 1) * TT]).then_inc(*S["sq"].inc(1))

        squares_1a(0)
        for t in range(NTT):
            b2 = t % 2
            v.wait_ge(S["stats"].sem, t + 1)
            v.tensor_scalar_mul(rw[0], pb[6][0:1, :], 1.0 / D)
            v.tensor_scalar_mul(rw[1], pb[7][0:1, :], 1.0 / D)
            v.tensor_mul(rw[2], rw[0], rw[0])
            v.tensor_sub(rw[3], rw[1], rw[2])
            v.tensor_scalar_add(rw[1], rw[3], float(EPS))
            v.reciprocal(rw[2], rw[1]).then_inc(*S["v1"].inc(1))
            v.wait_ge(S["s2"].sem, t + 1)
            v.tensor_mul(rw[1], rw[0], rw[3]).then_inc(*S["v2"].inc(1))
            v.wait_ge(S["bc"].sem, t + 1)
            v.tensor_copy(a_b[:, :], pb[6][:, :])
            v.tensor_copy(b_b[:, :], pb[7][:, :])
            v.tensor_copy(colp[b2][:, 0:8], pb[5][:, 0:8]).then_inc(*S["evac"].inc(1))
            if t + 1 < NTT:
                squares_1a(t + 1)
            v.wait_ge(S["qk"].sem, t + 1)
            for pi, (pbo, dst, nwo) in enumerate([(0, Q_sb, 0), (2, K_sb, 2)]):
                for fo in range(2):
                    v.tensor_mul(tmpf[fo][:, :], pb[pbo + fo][:, :], a_b[:, :])
                    last = v.scalar_tensor_tensor(
                        dst[:, fo * T + t * TT:fo * T + (t + 1) * TT],
                        b_b[:, :], negws[:, nwo + fo:nwo + fo + 1], tmpf[fo][:, :],
                        ALU.mult, ALU.add)
            last.then_inc(*S["qkfix"].inc(1))
            for rnd in range(2):
                v.wait_ge(S["vmm"].sem, 2 * t + rnd + 1)
                for si in range(2):
                    sub = 2 * rnd + si
                    g64 = t * 4 + sub
                    v.tensor_scalar(tmpv[0][:, :], wvb[:, :],
                                    colp[b2][:, 4 + sub:5 + sub], None, ALU.mult)
                    v.tensor_scalar(tmpv[1][:, :], pb[4][:, si * 256:(si + 1) * 256],
                                    colp[b2][:, sub:sub + 1], None, ALU.mult)
                    for fo in range(2):
                        last = v.tensor_sub(
                            V_sb[:, fo * 64 * VW + g64 * VW:fo * 64 * VW + g64 * VW + 128],
                            tmpv[1][:, fo * 128:(fo + 1) * 128],
                            tmpv[0][:, fo * 128:(fo + 1) * 128])
                last.then_inc(*S["vevac"].inc(1))
        if has1b:
            v.wait_ge(S["dw"].sem, 96)
            grp = 0
            n_od_ev = 0
            oid = 0
            for (h, qi, ms) in sched:
                has_od = qi > 0
                for (m, p, kind, aux, dgi) in ms:
                    if kind == "dg":
                        v.wait_ge(S["ssc"].sem, p + 1)
                        if dgi >= 2:
                            v.wait_ge(S["sex"].sem, diag_pass_of[dgi - 2] + 1)
                        v.scalar_tensor_tensor(
                            tmpS[dgi % 2][:, :], pb[p % 4][:, :], ISQ,
                            md_sb[:, (h * 4 + aux) * TT:(h * 4 + aux + 1) * TT],
                            ALU.mult, ALU.add).then_inc(*S["sdg"].inc(1))
                    if kind == "od" and m == 4 * qi - 1:
                        v.wait_ge(S["sav"].sem, p + 1)
                        for sub in range(4):
                            last = v.tensor_copy(odsb[:, sub * VW:sub * VW + 129],
                                                 pb[4 + sub][:, 0:129])
                        n_od_ev += 1
                        last.then_inc(*S["sod"].inc(1))
                p_last = ms[-1][1]
                v.wait_ge(S["sav"].sem, p_last + 1)
                for sub in range(4):
                    if has_od:
                        v.tensor_scalar(cmb[:, sub * VW:sub * VW + 129],
                                        odsb[:, sub * VW:sub * VW + 129],
                                        ec_sb[:, h * 4 + sub:h * 4 + sub + 1], None, ALU.mult)
                        last = v.tensor_add(cmb[:, sub * VW:sub * VW + 129],
                                            cmb[:, sub * VW:sub * VW + 129],
                                            pb[4 + sub][:, 0:129])
                    else:
                        last = v.tensor_copy(cmb[:, sub * VW:sub * VW + 129],
                                             pb[4 + sub][:, 0:129])
                last.then_inc(*S["scmb"].inc(1))
                for sub in range(4):
                    v.reciprocal(rcp[:, sub:sub + 1], cmb[:, sub * VW + 128:sub * VW + 129])
                for sub in range(4):
                    slot = oid % 8
                    if oid >= 8:
                        v.wait_ge(S["soc"].sem, 16 * (oid - 7))
                    v.tensor_scalar(outp[:, slot * 128:(slot + 1) * 128],
                                    cmb[:, sub * VW:sub * VW + 128],
                                    rcp[:, sub:sub + 1], None, ALU.mult) \
                        .then_inc(*S["sot"].inc(1))
                    oid += 1
                grp += 1

    @block.sync
    def _(sy):
        if has1b:
            oid = 0
            for (h, qi, ms) in sched:
                for sub in range(4):
                    slot = oid % 8
                    sy.wait_ge(S["sot"].sem, oid + 1)
                    sy.dma_start(out=cc_in[qi * TT + sub * 128:qi * TT + (sub + 1) * 128,
                                           h * 128:(h + 1) * 128],
                                 in_=outp[:, slot * 128:(slot + 1) * 128]) \
                        .then_inc(*S["soc"].inc(16))
                    oid += 1
        if debug:
            sy.wait_ge(S["qkfix"].sem, NTT)
            sy.wait_ge(S["vevac"].sem, 2 * NTT)
            sy.dma_start(out=dbg["qT"], in_=Q_sb[:, :]).then_inc(*S["yst"].inc(16))
            sy.dma_start(out=dbg["kT"], in_=K_sb[:, :]).then_inc(*S["yst"].inc(16))
            sy.dma_start(out=dbg["v"], in_=V_sb[:, :]).then_inc(*S["yst"].inc(16))

    p1.close()
    p2 = ExitStack()
    sb2 = lambda name, shape, dt=F32: p2.enter_context(nc.sbuf_tensor(name, shape, dt))
    if has2:
        Y_sb = sb2("Y_sb", [128, DC * TPC])
        rows2 = sb2("rows2", [1, 4 * TT])
        rw2 = [rows2[0:1, i * TT:(i + 1) * TT] for i in range(4)]
        m2b = sb2("m2b", [128, TT])
        r2b = sb2("r2b", [128, TT])
        onesf2 = sb2("onesf2", [1, 128])
        ones_bf2 = sb2("ones_bf2", [128, 8], BF16)
    p2a = ExitStack()
    if has1b:
        outT_sb = p2a.enter_context(nc.sbuf_tensor("outT_sb", [128, DC * TPC], BF16))

        @block.sync
        def _(sy):
            rid = sy.partition_id()
            sy.wait_ge(S["scc"].sem, rid + 1)
            cc5 = cc_out.rearrange("(s r q) m -> s r q m", r=NC, q=TPC)
            stg = cc_stage.rearrange("(r q) m -> r q m", q=TPC)
            sy.dma_start(out=stg, in_=cc5[ts(rid, 1), :, :, :][0]).then_inc(*S["rcv"].inc(16))
            sy.wait_ge(S["rcv"].sem, 16)
            for fc in range(DC):
                r, cb = fc // 2, fc % 2
                sy.dma_start_transpose(
                    outT_sb[:, fc * TPC:(fc + 1) * TPC],
                    cc_stage[r * TPC:(r + 1) * TPC, cb * 128:(cb + 1) * 128]) \
                    .then_inc(*S["rcv"].inc(16))
            sy.wait_ge(S["rcv"].sem, 16 * 17)
            if debug:
                sy.dma_start(out=dbg["attnT"], in_=outT_sb[:, :]).then_inc(*S["yst"].inc(16))


    if has2:
        woT_sb = p2a.enter_context(nc.sbuf_tensor("woT_sb", [128, DC * D], BF16))
        xow = [p2a.enter_context(nc.sbuf_tensor(f"xow{i}", [128, TT], F32)) for i in range(2)]

        NWO = 32

        @block.vector
        def _(v):
            v.wait_ge(S["soc"].sem, 128 * 16)
            if debug:
                v.wait_ge(S["yst"].sem, 48)
            v.memset(onesf2[:, :], 1.0)
            v.memset(ones_bf2[:, :], 1.0).then_inc(*S["setup"].inc(1))
            for idx in range(NWO):
                v.wait_ge(S["swo"].sem, idx + 1)
                v.wait_ge(S[f"dxo{idx % 2}"].sem, 16 * (idx // 2 + 1))
                fo, tt = idx // 2, idx % 2
                v.tensor_add(Y_sb[:, fo * TPC + tt * TT:fo * TPC + (tt + 1) * TT],
                             pb[idx % 2][:, :], xow[idx % 2][:, :]) \
                    .then_inc(*S["y1e"].inc(1))

        @block.gpsimd
        def _(g):
            if debug:
                g.wait_ge(S["yst"].sem, 48)
            for i in range(4):
                g.dma_start(out=woT_sb[:, i * 4 * D:(i + 1) * 4 * D],
                            in_=wo_st[:, i * 4 * D:(i + 1) * 4 * D]).then_inc(*S["dwo"].inc(16))

        @block.sync
        def _(sy):
            for idx in range(NWO):
                if idx >= 2:
                    sy.wait_ge(S["y1e"].sem, idx - 1)
                sy.dma_start(out=xow[idx % 2][:, :], in_=xown_st[idx]) \
                    .then_inc(*S[f"dxo{idx % 2}"].inc(16))

        @block.tensor
        def _(tn):
            tn.wait_ge(S["dwo"].sem, 64)
            tn.wait_ge(S["rcv"].sem, 16 * 17)
            tn.wait_ge(S["setup"].sem, 2)
            for idx in range(NWO):
                fo, tt = idx // 2, idx % 2
                if idx >= 2:
                    tn.wait_ge(S["y1e"].sem, idx - 1)
                for di in range(DC):
                    mm = tn.matmul(pb[idx % 2][:, :],
                                   woT_sb[:, di * D + fo * 128:di * D + (fo + 1) * 128],
                                   outT_sb[:, di * TPC + tt * TT:di * TPC + (tt + 1) * TT],
                                   start=(di == 0), stop=(di == DC - 1))
                mm.then_inc(*S["swo"].inc(1))

        p2a.close()
        xn2_sb = sb2("xn2_sb", [128, DC * TPC], BF16)
        act_sb = sb2("act_sb", [128, FSC * TPC], BF16)
        gu_buf = [sb2(f"gu_buf{i}", [128, DC * 256], BF16) for i in range(2)]
        dn_buf = [sb2(f"dn_buf{i}", [128, FSC * 128], BF16) for i in range(2)]
        gate_sb = sb2("gate_sb", [128, DC * 8], BF16)
        idn_sb = sb2("idn_sb", [128, 128])
        sel_sb = sb2("sel_sb", [8, 8 * 128])
        cT_sb = sb2("cT_sb", [8, TPC])
        ybf = [sb2(f"ybf{i}", [128, TT], BF16) for i in range(2)]
        ysq = [sb2(f"ysq{i}", [128, TT], BF16) for i in range(2)]
        sg_sb = [sb2(f"sg_sb{i}", [128, TT], BF16) for i in range(2)]
        u1_sb = [sb2(f"u1_sb{i}", [128, TT], BF16) for i in range(2)]
        tmp2 = [sb2("tmp2_0", [128, TT])] * 2
        cbt_ = [m2b, r2b]
        gsc = sb2("gsc", [128, 64])
        gs2 = sb2("gs2", [128, 64])
        gct = sb2("gct", [128, 64])
        gtA = sb2("gtA", [128, 64])
        gtB = sb2("gtB", [128, 64])
        gcl = sb2("gcl", [128, 64])

        JCs = [FC] * E8 + [FSC]
        EORDER = [E8] + list(range(E8))
        JPRE = {}
        acc = 0
        for e in EORDER:
            JPRE[e] = acc
            acc += JCs[e]
        TJC = acc
        CBPOS = {e: i for i, e in enumerate(EORDER) if e < E8}
        jgl = []
        for e in EORDER:
            for j in range(JCs[e]):
                jgl.append((e, j, JPRE[e] + j))
        hgl = []
        for pos, e in enumerate(EORDER):
            for h in range(DC):
                hgl.append((e, h, pos * DC + h))

        @block.gpsimd
        def _(g):
            g.wait_ge(S["y1e"].sem, 32)
            g.dma_start(out=gate_sb[:, :], in_=gate_st).then_inc(*S["dwo"].inc(16))
            g.dma_start(out=idn_sb[:, :], in_=idn_st).then_inc(*S["dwo"].inc(16))
            g.dma_start(out=sel_sb[:, :], in_=sel_st).then_inc(*S["dwo"].inc(16))
            for pos, e in enumerate(EORDER):
                for j in range(JCs[e]):
                    jjx = JPRE[e] + j
                    if jjx >= 2:
                        g.wait_ge(S["gups"].sem, 4 * (jjx - 1))
                    src = gu_st[e, j] if e < E8 else sgu_st[j]
                    g.dma_start(out=gu_buf[jjx % 2][:, :], in_=src) \
                        .then_inc(*S[f"dgu{jjx % 2}"].inc(16))
                for h in range(DC):
                    hhx = pos * DC + h
                    if hhx >= 2:
                        g.wait_ge(S["dnps"].sem, 2 * (hhx - 1))
                    if e < E8:
                        g.dma_start(out=dn_buf[hhx % 2][:, 0:FC * 128], in_=dn_st[e, h]) \
                            .then_inc(*S[f"ddn{hhx % 2}"].inc(16))
                    else:
                        g.dma_start(out=dn_buf[hhx % 2][:, :], in_=sdn_st[h]) \
                            .then_inc(*S[f"ddn{hhx % 2}"].inc(16))

        @block.scalar
        def _(s):
            for tt in range(2):
                for di in range(DC):
                    idx = tt * DC + di
                    s.wait_ge(S["v21"].sem, idx + 1)
                    if idx >= 2:
                        s.wait_ge(S["st2"].sem, 2 * (idx - 1))
                    s.activation(out=ysq[di % 2][:, :], in_=ybf[di % 2][:, :],
                                 func=AF.Square).then_inc(*S["s21"].inc(1))
                s.wait_ge(S["v22"].sem, tt + 1)
                s.activation(out=rw2[3], in_=rw2[2], func=AF.Sqrt).then_inc(*S["s22"].inc(1))
            s.wait_ge(S["g1"].sem, 16)
            for g8 in range(8):
                s.activation(out=gsc[:, 8 * g8:8 * g8 + 8], in_=pb[5][:, 8 * g8:8 * g8 + 8],
                             func=AF.Exp, bias=gcl[:, 8 + g8:9 + g8], scale=1.0) \
                    .then_inc(*S["g2"].inc(1))
            for (e, j, jjx) in jgl:
                for tt in range(2):
                    s.wait_ge(S["gups"].sem, 4 * jjx + 2 * tt + 1)
                    s.activation(out=sg_sb[tt][:, :], in_=pb[2 * tt][:, :],
                                 func=AF.Silu).then_inc(*S["sil"].inc(1))

        @block.tensor
        def _(tn):
            for tt in range(2):
                for di in range(DC):
                    idx = tt * DC + di
                    tn.wait_ge(S["s21"].sem, idx + 1)
                    if tt > 0 and di == 0:
                        tn.wait_ge(S["ev2"].sem, tt)
                    tn.matmul(pb[7][0:1, :], ones_bf2[:, 0:1], ysq[di % 2][:, :],
                              start=(di == 0), stop=(di == DC - 1)) \
                        .then_inc(*S["st2"].inc(1))
                    mm = tn.matmul(pb[6][0:1, :], ones_bf2[:, 0:1], ybf[di % 2][:, :],
                                   start=(di == 0), stop=(di == DC - 1)) \
                        .then_inc(*S["st2"].inc(1))
                tn.wait_ge(S["v23"].sem, tt + 1)
                tn.matmul(pb[6][:, :], onesf2[0:1, :], rw2[0], start=True, stop=True)
                tn.matmul(pb[7][:, :], onesf2[0:1, :], rw2[3], start=True, stop=True) \
                    .then_inc(*S["bc2"].inc(1))
            tn.wait_ge(S["xn2d"].sem, 32)
            tn.wait_ge(S["dwo"].sem, 112)
            for g8 in range(8):
                for di in range(DC):
                    mm = tn.matmul(pb[5][:, 8 * g8:8 * g8 + 8],
                                   xn2_sb[:, di * TPC + g8 * 128:di * TPC + (g8 + 1) * 128],
                                   gate_sb[:, di * 8:(di + 1) * 8],
                                   start=(di == 0), stop=(di == DC - 1))
                mm.then_inc(*S["glog"].inc(1))
            tn.wait_ge(S["gcmb"].sem, 8)
            for rnd in range(2):
                if rnd:
                    tn.wait_ge(S["gsel"].sem, rnd)
                for gi in range(4):
                    g8 = rnd * 4 + gi
                    tn.transpose(pb[4][0:8, gi * 128:(gi + 1) * 128],
                                 gct[:, 8 * g8:8 * g8 + 8], idn_sb[:, :]) \
                        .then_inc(*S["gtr"].inc(1))
            for pos, e in enumerate(EORDER):
                if e < E8:
                    tn.wait_ge(S["dnac"].sem, 32 * pos)
                    tn.wait_ge(S["gsel"].sem, 2)
                    for tt in range(2):
                        tn.matmul(pb[4 + tt][:, :], sel_sb[0:8, e * 128:(e + 1) * 128],
                                  cT_sb[0:8, tt * TT:(tt + 1) * TT], start=True, stop=True) \
                            .then_inc(*S["cbt"].inc(1))
                for j in range(JCs[e]):
                    jjx = JPRE[e] + j
                    tn.wait_ge(S[f"dgu{jjx % 2}"].sem, 16 * (jjx // 2 + 1))
                    if e < E8 and j == 0:
                        tn.wait_ge(S["cbk"].sem, 2 * CBPOS[e])
                    for tt in range(2):
                        if jjx >= 1:
                            tn.wait_ge(S["sil"].sem, 2 * (jjx - 1) + tt + 1)
                            tn.wait_ge(S["uact"].sem, 2 * (jjx - 1) + tt + 1)
                        for di in range(DC):
                            mm = tn.matmul(pb[2 * tt][:, :],
                                           gu_buf[jjx % 2][:, di * 256:di * 256 + 128],
                                           xn2_sb[:, di * TPC + tt * TT:di * TPC + (tt + 1) * TT],
                                           start=(di == 0), stop=(di == DC - 1))
                        mm.then_inc(*S["gups"].inc(1))
                        for di in range(DC):
                            mm = tn.matmul(pb[2 * tt + 1][:, :],
                                           gu_buf[jjx % 2][:, di * 256 + 128:di * 256 + 256],
                                           xn2_sb[:, di * TPC + tt * TT:di * TPC + (tt + 1) * TT],
                                           start=(di == 0), stop=(di == DC - 1))
                        mm.then_inc(*S["gups"].inc(1))
                JC = JCs[e]
                for h in range(DC):
                    hhx = pos * DC + h
                    tn.wait_ge(S[f"ddn{hhx % 2}"].sem, 16 * (hhx // 2 + 1))
                    tn.wait_ge(S["uact"].sem, 2 * (JPRE[e] + JC))
                    for tt in range(2):
                        slot = 2 * hhx + tt
                        if slot >= 2:
                            tn.wait_ge(S["dnac"].sem, slot - 1)
                        for fj in range(JC):
                            mm = tn.matmul(pb[4 + tt][:, :],
                                           dn_buf[hhx % 2][:, fj * 128:(fj + 1) * 128],
                                           act_sb[:, fj * TPC + tt * TT:fj * TPC + (tt + 1) * TT],
                                           start=(fj == 0), stop=(fj == JC - 1))
                        mm.then_inc(*S["dnps"].inc(1))

        @block.vector
        def _(v):
            v.wait_ge(S["y1e"].sem, 32)
            for tt in range(2):
                for di in range(DC):
                    idx = tt * DC + di
                    if idx >= 2:
                        v.wait_ge(S["s21"].sem, idx - 1)
                    v.tensor_copy(ybf[di % 2][:, :],
                                  Y_sb[:, di * TPC + tt * TT:di * TPC + (tt + 1) * TT]) \
                        .then_inc(*S["v21"].inc(1))
                v.wait_ge(S["st2"].sem, 32 * (tt + 1))
                v.tensor_scalar_mul(rw2[0], pb[6][0:1, :], 1.0 / D)
                v.tensor_scalar_mul(rw2[1], pb[7][0:1, :], 1.0 / D)
                v.tensor_mul(rw2[2], rw2[0], rw2[0])
                v.tensor_sub(rw2[3], rw2[1], rw2[2])
                v.tensor_scalar_add(rw2[1], rw2[3], float(EPS))
                v.reciprocal(rw2[2], rw2[1]).then_inc(*S["v22"].inc(1))
                v.wait_ge(S["s22"].sem, tt + 1)
                v.engine_nop().then_inc(*S["v23"].inc(1))
                v.wait_ge(S["bc2"].sem, tt + 1)
                v.tensor_copy(m2b[:, :], pb[6][:, :])
                v.tensor_copy(r2b[:, :], pb[7][:, :]).then_inc(*S["ev2"].inc(1))
                for di in range(DC):
                    v.tensor_sub(tmp2[di % 2][:, :],
                                 Y_sb[:, di * TPC + tt * TT:di * TPC + (tt + 1) * TT], m2b[:, :])
                    v.tensor_mul(xn2_sb[:, di * TPC + tt * TT:di * TPC + (tt + 1) * TT],
                                 tmp2[di % 2][:, :], r2b[:, :]).then_inc(*S["xn2d"].inc(1))
            v.wait_ge(S["glog"].sem, 8)
            for g8 in range(8):
                v.reduce_max(gcl[:, g8:g8 + 1], pb[5][:, 8 * g8:8 * g8 + 8], axis=AX.X)
            for g8 in range(8):
                v.tensor_scalar_mul(gcl[:, 8 + g8:9 + g8], gcl[:, g8:g8 + 1], -1.0)
            v.engine_nop().then_inc(*S["g1"].inc(16))
            for g8 in range(8):
                v.wait_ge(S["g2"].sem, g8 + 1)
                v.reduce_sum(gcl[:, 16 + g8:17 + g8], gsc[:, 8 * g8:8 * g8 + 8], axis=AX.X)
            for g8 in range(8):
                v.reciprocal(gcl[:, 24 + g8:25 + g8], gcl[:, 16 + g8:17 + g8])
            for g8 in range(8):
                v.tensor_scalar(gs2[:, 8 * g8:8 * g8 + 8], gsc[:, 8 * g8:8 * g8 + 8],
                                gcl[:, 24 + g8:25 + g8], None, ALU.mult)
            for g8 in range(8):
                v.reduce_max(gcl[:, 32 + g8:33 + g8], gs2[:, 8 * g8:8 * g8 + 8], axis=AX.X)
            for g8 in range(8):
                v.tensor_scalar(gtA[:, 8 * g8:8 * g8 + 8], gs2[:, 8 * g8:8 * g8 + 8],
                                gcl[:, 32 + g8:33 + g8], None, ALU.is_equal)
            for g8 in range(8):
                v.scalar_tensor_tensor(gtB[:, 8 * g8:8 * g8 + 8], gtA[:, 8 * g8:8 * g8 + 8],
                                       -1e9, gs2[:, 8 * g8:8 * g8 + 8], ALU.mult, ALU.add)
            for g8 in range(8):
                v.reduce_max(gcl[:, 40 + g8:41 + g8], gtB[:, 8 * g8:8 * g8 + 8], axis=AX.X)
            for g8 in range(8):
                v.tensor_scalar(gtA[:, 8 * g8:8 * g8 + 8], gs2[:, 8 * g8:8 * g8 + 8],
                                gcl[:, 40 + g8:41 + g8], None, ALU.is_ge)
            for g8 in range(8):
                v.tensor_mul(gct[:, 8 * g8:8 * g8 + 8], gs2[:, 8 * g8:8 * g8 + 8],
                             gtA[:, 8 * g8:8 * g8 + 8]).then_inc(*S["gcmb"].inc(1))
            for rnd in range(2):
                v.wait_ge(S["gtr"].sem, 4 * (rnd + 1))
                v.tensor_copy(cT_sb[0:8, rnd * TT:(rnd + 1) * TT], pb[4][0:8, 0:512]) \
                    .then_inc(*S["gsel"].inc(1))
            for pos, e in enumerate(EORDER):
                if e < E8:
                    for tt in range(2):
                        v.wait_ge(S["cbt"].sem, 2 * (CBPOS[e] - 1) + tt + 1)
                        v.tensor_copy(cbt_[tt][:, :], pb[4 + tt][:, :]) \
                            .then_inc(*S["cbk"].inc(1))
                for j in range(JCs[e]):
                    jjx = JPRE[e] + j
                    for tt in range(2):
                        v.wait_ge(S["gups"].sem, 4 * jjx + 2 * tt + 2)
                        v.wait_ge(S["sil"].sem, 2 * jjx + tt + 1)
                        if e < E8:
                            v.tensor_mul(u1_sb[tt][:, :], pb[2 * tt + 1][:, :],
                                         cbt_[tt][:, :])
                            v.tensor_mul(act_sb[:, j * TPC + tt * TT:j * TPC + (tt + 1) * TT],
                                         u1_sb[tt][:, :], sg_sb[tt][:, :]) \
                                .then_inc(*S["uact"].inc(1))
                        else:
                            v.tensor_mul(act_sb[:, j * TPC + tt * TT:j * TPC + (tt + 1) * TT],
                                         pb[2 * tt + 1][:, :], sg_sb[tt][:, :]) \
                                .then_inc(*S["uact"].inc(1))
                for h in range(DC):
                    hhx = pos * DC + h
                    for tt in range(2):
                        v.wait_ge(S["dnps"].sem, 2 * hhx + tt + 1)
                        v.tensor_add(Y_sb[:, h * TPC + tt * TT:h * TPC + (tt + 1) * TT],
                                     Y_sb[:, h * TPC + tt * TT:h * TPC + (tt + 1) * TT],
                                     pb[4 + tt][:, :]).then_inc(*S["dnac"].inc(1))

        @block.sync
        def _(sy):
            if debug:
                sy.wait_ge(S["xn2d"].sem, 32)
                sy.dma_start(out=dbg["xn2"], in_=xn2_sb[:, :]).then_inc(*S["yst"].inc(16))
                sy.wait_ge(S["gsel"].sem, 2)
                sy.dma_start(out=dbg["comb"], in_=cT_sb[0:8, :]).then_inc(*S["yst"].inc(16))

            sy.wait_ge(S["dnac"].sem, 32 * 9)
            y3 = yT.rearrange("(a p) m -> a p m", p=128)
            for di in range(DC):
                sy.dma_start(out=y3[di], in_=Y_sb[:, di * TPC:(di + 1) * TPC]) \
                    .then_inc(*S["yst"].inc(16))

    @block.sync
    def _(sy):
        sy.wait_ge(S["yst"].sem, S["yst"].v)

    if not has1b:
        p1.close()
    else:
        if not has2:
            p2a.close()
        p2.close()
    ctx.close()
    return nc



def stage_inputs(inp, phase="full"):
    x = np.asarray(inp["L0_attn_x"]).reshape(T, D)
    Wq = np.asarray(inp["L0_attn_Wq"])
    Wk = np.asarray(inp["L0_attn_Wk"])
    Wv = np.asarray(inp["L0_attn_Wv"])

    x_st = np.ascontiguousarray(
        x.reshape(NTT, TT, DC, 128).transpose(0, 3, 2, 1).reshape(NTT, 128, DC * TT))

    def wslice(W, c):
        hA, hB = head_pair(c)
        WT = W.T
        wc = np.concatenate([WT[:, 128 * hA:128 * (hA + 1)], WT[:, 128 * hB:128 * (hB + 1)]], axis=1)
        return np.ascontiguousarray(wc.reshape(DC, 128, 256).transpose(1, 0, 2).reshape(128, DC * 256))

    in_maps = []
    for c in range(NC):
        hA, hB = head_pair(c)
        sl = (slope(hA), slope(hB))
        kl = np.arange(128, dtype=np.float64)[:, None]
        ql = np.arange(TT, dtype=np.float64)[None, :]
        md = np.empty((128, 8 * TT), np.float32)
        for hi in range(2):
            for j in range(4):
                mm = sl[hi] * (128 * j + kl - ql)
                mm = np.where(128 * j + kl <= ql, mm, NEG_BIG)
                md[:, (hi * 4 + j) * TT:(hi * 4 + j + 1) * TT] = mm.astype(np.float32)
        bc = np.zeros((128, 128), np.float32)
        for hi in range(2):
            for dl in range(64):
                bc[:, hi * 64 + dl] = (sl[hi] * (kl[:, 0] - 128.0 * dl)).astype(np.float32)
        ec = np.zeros((128, 8), np.float32)
        for hi in range(2):
            for sub in range(4):
                ec[:, hi * 4 + sub] = np.exp(-sl[hi] * (128.0 * sub + kl[:, 0])).astype(np.float32)
        m = {
            "x_st": x_st, "wq_st": wslice(Wq, c), "wk_st": wslice(Wk, c), "wv_st": wslice(Wv, c),
            "md_st": md, "bc_st": bc, "ec_st": ec,
        }
        if phase == "full":
            m.update(stage_phase2(inp, c))
        in_maps.append(m)
    return in_maps


def stage_phase2(inp, c):
    Wo = np.asarray(inp["L0_attn_Wo"])
    gate = np.asarray(inp["L0_ffn_gate_weight"])
    gu = np.asarray(inp["L0_ffn_gate_up_weights"])
    dn = np.asarray(inp["L0_ffn_down_weights"])
    sgu = np.asarray(inp["L0_ffn_shared_gate_up"])
    sdn = np.asarray(inp["L0_ffn_shared_down"])
    x = np.asarray(inp["L0_attn_x"]).reshape(T, D)

    perm = []
    for r in range(NC):
        ha, hb = head_pair(r)
        perm.extend(range(128 * ha, 128 * ha + 128))
        perm.extend(range(128 * hb, 128 * hb + 128))
    woT = Wo.T[np.asarray(perm), :]
    wo_st = np.ascontiguousarray(woT.reshape(DC, 128, D).transpose(1, 0, 2).reshape(128, DC * D))
    xT = x.T
    xown = xT[:, TPC * c:TPC * (c + 1)]
    xown_st = np.ascontiguousarray(
        xown.reshape(DC, 128, 2, TT).transpose(0, 2, 1, 3).reshape(32, 128, TT))
    gate_st = np.ascontiguousarray(gate.T.reshape(DC, 128, 8).transpose(1, 0, 2).reshape(128, DC * 8))
    idn = np.eye(128, dtype=np.float32)
    sel = np.zeros((8, 8 * 128), np.float32)
    for e in range(8):
        sel[e, e * 128:(e + 1) * 128] = 1.0
    gug = gu[:, :F, :].reshape(E8, FC, 128, DC, 128)
    guu = gu[:, F:, :].reshape(E8, FC, 128, DC, 128)
    gu_st = np.empty((E8, FC, 128, DC * 256), np.float32)
    gg = gug.transpose(0, 1, 4, 3, 2)
    uu = guu.transpose(0, 1, 4, 3, 2)
    gu_st = np.ascontiguousarray(
        np.stack([gg, uu], axis=4).reshape(E8, FC, 128, DC, 2 * 128).reshape(E8, FC, 128, DC * 256))
    dn_st = np.ascontiguousarray(
        dn.reshape(E8, DC, 128, FC, 128).transpose(0, 1, 4, 3, 2).reshape(E8, DC, 128, FC * 128))
    sgug = sgu[:FS, :].reshape(FSC, 128, DC, 128)
    sguu = sgu[FS:, :].reshape(FSC, 128, DC, 128)
    sgu_st = np.ascontiguousarray(
        np.stack([sgug.transpose(0, 3, 2, 1), sguu.transpose(0, 3, 2, 1)], axis=3)
        .reshape(FSC, 128, DC, 256).reshape(FSC, 128, DC * 256))
    sdn_st = np.ascontiguousarray(
        sdn.reshape(DC, 128, FSC, 128).transpose(0, 3, 2, 1).reshape(DC, 128, FSC * 128))
    return {
        "wo_st": wo_st, "xown_st": xown_st, "gate_st": gate_st,
        "idn_st": idn, "sel_st": sel,
        "gu_st": gu_st, "dn_st": dn_st, "sgu_st": sgu_st, "sdn_st": sdn_st,
    }



def kernel(**inputs):
    in_maps = stage_inputs(inputs, phase="full")
    nc = build_nc(phase="full", debug=False)
    res = run_bass_kernel_spmd(nc, in_maps, core_ids=list(range(NC)))
    out = np.empty((1, T, D), np.float32)
    for c in range(NC):
        out[0, TPC * c:TPC * (c + 1), :] = np.asarray(res.results[c]["yT"]).T
    return out


# version 07
# speedup vs baseline: 1.3379x; 1.0016x over previous
import sys
sys.path.insert(0, "/opt/trn_rl_repo")
import numpy as np
from contextlib import ExitStack
import concourse.bass as bass
import concourse.mybir as mybir
from concourse.bass_utils import run_bass_kernel_spmd

F32 = mybir.dt.float32
BF16 = mybir.dt.bfloat16
AF = mybir.ActivationFunctionType
ALU = mybir.AluOpType
AX = mybir.AxisListType
ts = bass.ts

T, D, NH, HD, NC = 8192, 2048, 16, 128, 8
TT = 512
NTT = T // TT
DC = D // 128
TPC = T // NC
E8, F, FS = 8, 1408, 2816
FC, FSC = F // 128, FS // 128
EPS = 1e-5
ISQ = float(1.0 / np.sqrt(HD))
VW = 136
NEG_BIG = -1e30


def head_pair(c):
    return c, 15 - c


def slope(h):
    return 2.0 ** (-(h + 1) / 2.0)


class Ctr:
    def __init__(self, nc, name, ctx):
        self.sem = ctx.enter_context(nc.semaphore(name))
        self.v = 0

    def inc(self, n=1):
        self.v += n
        return self.sem, n


def build_nc(phase="full", debug=False):
    nc = bass.Bass("TRN2", target_bir_lowering=False, debug=False, detect_race_conditions=False)
    ctx = ExitStack()
    dr = lambda name, shape, dt, kind="ExternalInput": nc.dram_tensor(name, shape, dt, kind=kind).ap()

    x_st = dr("x_st", [NTT, 128, DC * TT], F32)
    wq_st = dr("wq_st", [128, DC * 256], F32)
    wk_st = dr("wk_st", [128, DC * 256], F32)
    wv_st = dr("wv_st", [128, DC * 256], F32)
    md_st = dr("md_st", [128, 8 * TT], F32)
    bc_st = dr("bc_st", [128, 128], F32)
    ec_st = dr("ec_st", [128, 8], F32)
    has2 = phase == "full"
    if has2:
        wo_st = dr("wo_st", [128, DC * D], F32)
        xown_st = dr("xown_st", [32, 128, TT], F32)
        gate_st = dr("gate_st", [128, DC * 8], F32)
        idn_st = dr("idn_st", [128, 128], F32)
        sel_st = dr("sel_st", [8, 8 * 128], F32)
        gu_st = dr("gu_st", [E8, FC, 128, DC * 256], F32)
        dn_st = dr("dn_st", [E8, DC, 128, FC * 128], F32)
        sgu_st = dr("sgu_st", [FSC, 128, DC * 256], F32)
        sdn_st = dr("sdn_st", [DC, 128, FSC * 128], F32)
        yT = dr("yT", [D, TPC], F32, "ExternalOutput")
    has1b = phase in ("1b", "full")
    if has1b:
        cc_in = nc.dram_tensor("cc_in", [T, 2 * HD], BF16).ap()
        cc_out = nc.dram_tensor("cc_out", [NC * T, 2 * HD], BF16).ap()
        cc_stage = nc.dram_tensor("cc_stage", [NC * TPC, 2 * HD], BF16).ap()

    dbg = {}
    if debug:
        dbg["qT"] = dr("dbg_qT", [128, 2 * T], BF16, "ExternalOutput")
        dbg["kT"] = dr("dbg_kT", [128, 2 * T], BF16, "ExternalOutput")
        dbg["v"] = dr("dbg_v", [128, 2 * 64 * VW], BF16, "ExternalOutput")
        if has1b:
            dbg["attnT"] = dr("dbg_attnT", [128, DC * TPC], BF16, "ExternalOutput")
        if has2:
            dbg["y1"] = dr("dbg_y1", [128, DC * TPC], F32, "ExternalOutput")
            dbg["xn2"] = dr("dbg_xn2", [128, DC * TPC], BF16, "ExternalOutput")
            dbg["comb"] = dr("dbg_comb", [8, TPC], F32, "ExternalOutput")


    sbuf = lambda name, shape, dt=F32: ctx.enter_context(nc.sbuf_tensor(name, shape, dt))
    pb = [ctx.enter_context(nc.psum_tensor(f"pb{i}", [128, 512], F32)) for i in range(8)]

    S = {}
    for name in ["dx0", "dx1", "dw", "sq", "sqcons", "stats", "v1", "s2", "v2", "bc",
                 "evac", "qk", "qkfix", "vmm", "vevac", "xcons", "setup",
                 "tsu", "vsu", "p1a",
                 "ssc", "sdg", "sex", "sav", "sod", "scmb", "sot", "soc",
                 "scc", "rcv", "pb1",
                 "dwo", "swo", "svo", "y1e", "st2", "v21", "s21", "v22",
                 "bc2", "ev2", "xn2d", "glog", "gcmb", "gsel",
                 "dgu0", "dgu1", "ddn0", "ddn1", "gups", "sil", "uact", "dnps", "dnac",
                 "cbk", "cbt", "dxo0", "dxo1", "s22", "v23", "g1", "g2", "gtr",
                 "yst"]:
        S[name] = Ctr(nc, name, ctx)

    block = ctx.enter_context(nc.Block())

    p1 = ExitStack()
    sb1 = lambda name, shape, dt=F32: p1.enter_context(nc.sbuf_tensor(name, shape, dt))
    wq_sb = sb1("wq_sb", [128, DC * 256], BF16)
    wk_sb = sb1("wk_sb", [128, DC * 256], BF16)
    wv_sb = sb1("wv_sb", [128, DC * 256], BF16)
    xbuf = [sb1(f"xbuf{i}", [128, DC * TT], BF16) for i in range(2)]
    xsq = [sb1(f"xsq{i}", [128, TT], BF16) for i in range(4)]
    Q_sb = sb1("Q_sb", [128, 2 * T], BF16)
    K_sb = sb1("K_sb", [128, 2 * T], BF16)
    V_sb = sb1("V_sb", [128, 2 * 64 * VW], BF16)
    rows = sb1("rows", [1, 4 * TT])
    rw = [rows[0:1, i * TT:(i + 1) * TT] for i in range(4)]
    a_b = sb1("a_b", [128, TT])
    b_b = sb1("b_b", [128, TT])
    colp = [sb1(f"colp{i}", [128, 16]) for i in range(2)]
    negws = sb1("negws", [128, 8])
    wvb = sb1("wvb", [128, 256])
    ones_bf = sb1("ones_bf", [128, 8], BF16)
    onesf = sb1("onesf", [1, 128])
    one1 = sb1("one1", [1, 8])
    wsrow = sb1("wsrow", [1, 3 * 256])
    md_sb = sb1("md_sb", [128, 8 * TT])
    bcc_sb = sb1("bcc_sb", [128, 128])
    ec_sb = sb1("ec_sb", [128, 8])
    P_pool = [sb1(f"Pp{i}", [128, TT], BF16) for i in range(4)]
    tmpS = [sb1(f"tmpS{i}", [128, TT]) for i in range(2)]
    tmpf = tmpS
    tmpv = [tmpS[0][:, 0:256], tmpS[1][:, 0:256]]
    odsb = sb1("odsb", [128, 4 * VW])
    cmb = sb1("cmb", [128, 4 * VW])
    rcp = sb1("rcp", [128, 16])
    outp = sb1("outp", [128, 8 * 128], BF16)

    sched = []
    p_ctr = 0
    dgi_ctr = 0
    diag_pass_of = {}
    for qi in range(NTT):
        for h in range(2):
            ms = []
            for m in range(4 * qi + 4):
                if m < 4 * qi:
                    ms.append((m, p_ctr, "od", 4 * qi - m, None))
                else:
                    diag_pass_of[dgi_ctr] = p_ctr
                    ms.append((m, p_ctr, "dg", m - 4 * qi, dgi_ctr))
                    dgi_ctr += 1
                p_ctr += 1
            sched.append((h, qi, ms))
    NPASS = p_ctr

    @block.gpsimd
    def _(g):
        g.dma_start(out=wq_sb[:, :], in_=wq_st).then_inc(*S["dw"].inc(16))
        g.dma_start(out=wk_sb[:, :], in_=wk_st).then_inc(*S["dw"].inc(16))
        g.dma_start(out=wv_sb[:, :], in_=wv_st).then_inc(*S["dw"].inc(16))
        g.dma_start(out=md_sb[:, :], in_=md_st).then_inc(*S["dw"].inc(16))
        g.dma_start(out=bcc_sb[:, :], in_=bc_st).then_inc(*S["dw"].inc(16))
        g.dma_start(out=ec_sb[:, :], in_=ec_st).then_inc(*S["dw"].inc(16))
        for t in range(NTT):
            if t >= 2:
                g.wait_ge(S["vmm"].sem, 2 * t - 2)
            g.dma_start(out=xbuf[t % 2][:, :], in_=x_st[t]).then_inc(*S[f"dx{t % 2}"].inc(16))
        if has1b:
            for i in range(NC):
                g.wait_ge(S["soc"].sem, 16 * 8 * (2 * i + 2))
                g.collective_compute(
                    "AllGather", ALU.bypass, replica_groups=[list(range(NC))],
                    ins=[cc_in[TPC * i:TPC * (i + 1), :].opt()],
                    outs=[cc_out[NC * TPC * i:NC * TPC * (i + 1), :].opt()],
                ).then_inc(*S["scc"].inc(1))

    @block.scalar
    def _(s):
        for t in range(NTT):
            s.wait_ge(S["v1"].sem, t + 1)
            s.activation(out=rw[3], in_=rw[2], func=AF.Sqrt).then_inc(*S["s2"].inc(1))
        if has1b:
            s.wait_ge(S["dw"].sem, 96)
            for (h, qi, ms) in sched:
                for (m, p, kind, aux, dgi) in ms:
                    if p >= 4:
                        s.wait_ge(S["sav"].sem, p - 3)
                    if kind == "od":
                        s.wait_ge(S["ssc"].sem, p + 1)
                        s.activation(out=P_pool[p % 4][:, :], in_=pb[p % 4][:, :],
                                     func=AF.Exp, bias=bcc_sb[:, h * 64 + aux:h * 64 + aux + 1],
                                     scale=ISQ).then_inc(*S["sex"].inc(1))
                    else:
                        s.wait_ge(S["sdg"].sem, dgi + 1)
                        s.activation(out=P_pool[p % 4][:, :], in_=tmpS[dgi % 2][:, :],
                                     func=AF.Exp).then_inc(*S["sex"].inc(1))

    @block.tensor
    def _(tn):
        tn.wait_ge(S["dw"].sem, 96)
        tn.wait_ge(S["setup"].sem, 1)
        for wi, wsb in enumerate([wq_sb, wk_sb, wv_sb]):
            if wi:
                tn.wait_ge(S["vsu"].sem, wi)
            for di in range(DC):
                mm = tn.matmul(pb[5][0:1, 0:256], ones_bf[:, 0:1], wsb[:, di * 256:(di + 1) * 256],
                               start=(di == 0), stop=(di == DC - 1))
            mm.then_inc(*S["tsu"].inc(1))
        tn.wait_ge(S["vsu"].sem, 3)
        for wi in range(2):
            for fo in range(2):
                cidx = wi * 2 + fo
                tn.matmul(pb[5][:, cidx:cidx + 1],
                          wsrow[0:1, wi * 256 + fo * 128:wi * 256 + (fo + 1) * 128],
                          one1[0:1, 0:1], start=True, stop=True)
        tn.matmul(pb[5][:, 8:264], onesf[0:1, :], wsrow[0:1, 512:768], start=True, stop=True) \
            .then_inc(*S["tsu"].inc(1))
        tn.wait_ge(S["vsu"].sem, 4)
        def stats_1a(t):
            b = t % 2
            tn.wait_ge(S["evac"].sem, t)
            for di in range(DC):
                tn.wait_ge(S["sq"].sem, t * DC + di + 1)
                tn.matmul(pb[7][0:1, :], ones_bf[:, 0:1], xsq[di % 4][:, :],
                          start=(di == 0), stop=(di == DC - 1)) \
                    .then_inc(*S["sqcons"].inc(1))
                mm = tn.matmul(pb[6][0:1, :], ones_bf[:, 0:1], xbuf[b][:, di * TT:(di + 1) * TT],
                               start=(di == 0), stop=(di == DC - 1))
            mm.then_inc(*S["stats"].inc(1))

        def bcast_1a(t):
            tn.wait_ge(S["v2"].sem, t + 1)
            tn.wait_ge(S["evac"].sem, t)
            tn.matmul(pb[6][:, :], onesf[0:1, :], rw[3], start=True, stop=True)
            tn.matmul(pb[7][:, :], onesf[0:1, :], rw[1], start=True, stop=True)
            for sub in range(4):
                tn.matmul(pb[5][:, sub:sub + 1], rw[3][0:1, sub * 128:(sub + 1) * 128],
                          one1[0:1, 0:1], start=True, stop=True)
                mm = tn.matmul(pb[5][:, 4 + sub:5 + sub], rw[1][0:1, sub * 128:(sub + 1) * 128],
                               one1[0:1, 0:1], start=True, stop=True)
            mm.then_inc(*S["bc"].inc(1))

        stats_1a(0)
        bcast_1a(0)
        for t in range(NTT):
            b = t % 2
            tn.wait_ge(S["evac"].sem, t + 1)
            if t > 0:
                tn.wait_ge(S["qkfix"].sem, t)
            for pbo, wsb in ((0, wq_sb), (2, wk_sb)):
                for fo in range(2):
                    for di in range(DC):
                        mm = tn.matmul(pb[pbo + fo][:, :],
                                       wsb[:, di * 256 + fo * 128:di * 256 + (fo + 1) * 128],
                                       xbuf[b][:, di * TT:(di + 1) * TT],
                                       start=(di == 0), stop=(di == DC - 1))
            mm.then_inc(*S["qk"].inc(1))
            if t + 1 < NTT:
                stats_1a(t + 1)
            for rnd in range(2):
                if t > 0 or rnd > 0:
                    tn.wait_ge(S["vevac"].sem, 2 * t + rnd)
                mm = None
                for si in range(2):
                    sub = 2 * rnd + si
                    for di in range(DC):
                        mm = tn.matmul(pb[4][:, si * 256:(si + 1) * 256],
                                       xbuf[b][:, di * TT + sub * 128:di * TT + (sub + 1) * 128],
                                       wv_sb[:, di * 256:(di + 1) * 256],
                                       start=(di == 0), stop=(di == DC - 1))
                mm.then_inc(*S["vmm"].inc(1))
            if t + 1 < NTT:
                bcast_1a(t + 1)
        if has1b:
            tn.wait_ge(S["qkfix"].sem, NTT)
            tn.wait_ge(S["vevac"].sem, 2 * NTT)
            flat = []
            grp = 0
            n_od_ev = 0
            for (h, qi, ms) in sched:
                has_od = qi > 0
                if has_od:
                    n_od_ev += 1
                for (m, p, kind, aux, dgi) in ms:
                    first_od = kind == "od" and m == 0
                    first_dg = kind == "dg" and aux == 0
                    last_od = kind == "od" and m == 4 * qi - 1
                    last_dg = kind == "dg" and aux == 3
                    flat.append(dict(h=h, qi=qi, m=m, p=p, grp=grp, n_od=n_od_ev,
                                     has_od=has_od, first=(first_od or first_dg),
                                     first_dg_od=(first_dg and has_od),
                                     start=(first_od or (first_dg and not has_od)),
                                     stop=(last_od or last_dg)))
                grp += 1
            DELAY = 2
            for idx in range(len(flat) + DELAY):
                if idx < len(flat):
                    f = flat[idx]
                    if f["p"] >= 4:
                        tn.wait_ge(S["sex"].sem, f["p"] - 3)
                    tn.matmul(pb[f["p"] % 4][:, :],
                              K_sb[:, f["h"] * T + f["m"] * 128:f["h"] * T + (f["m"] + 1) * 128],
                              Q_sb[:, f["h"] * T + f["qi"] * TT:f["h"] * T + (f["qi"] + 1) * TT],
                              start=True, stop=True).then_inc(*S["ssc"].inc(1))
                if idx >= DELAY:
                    f = flat[idx - DELAY]
                    if f["start"]:
                        tn.wait_ge(S["scmb"].sem, f["grp"])
                    if f["first_dg_od"] and f["m"] == 4 * f["qi"]:
                        tn.wait_ge(S["sod"].sem, f["n_od"])
                    tn.wait_ge(S["sex"].sem, f["p"] + 1)
                    mm = None
                    for sub in range(4):
                        mm = tn.matmul(pb[4 + sub][:, 0:129],
                                       P_pool[f["p"] % 4][:, sub * 128:(sub + 1) * 128],
                                       V_sb[:, f["h"] * 64 * VW + f["m"] * VW:
                                            f["h"] * 64 * VW + f["m"] * VW + 129],
                                       start=(f["start"] or (f["first_dg_od"])),
                                       stop=f["stop"])
                    mm.then_inc(*S["sav"].inc(1))

    @block.vector
    def _(v):
        v.memset(ones_bf[:, :], 1.0)
        v.memset(onesf[:, :], 1.0)
        v.memset(one1[:, :], 1.0)
        v.memset(V_sb[:, :], 1.0).then_inc(*S["setup"].inc(1))
        for wi in range(3):
            v.wait_ge(S["tsu"].sem, wi + 1)
            if wi < 2:
                v.tensor_scalar_mul(wsrow[0:1, wi * 256:(wi + 1) * 256], pb[5][0:1, 0:256],
                                    -1.0).then_inc(*S["vsu"].inc(1))
            else:
                v.tensor_copy(wsrow[0:1, wi * 256:(wi + 1) * 256],
                              pb[5][0:1, 0:256]).then_inc(*S["vsu"].inc(1))
        v.wait_ge(S["tsu"].sem, 4)
        v.tensor_copy(negws[:, 0:4], pb[5][:, 0:4])
        v.tensor_copy(wvb[:, :], pb[5][:, 8:264]).then_inc(*S["vsu"].inc(1))
        def squares_1a(t):
            b = t % 2
            v.wait_ge(S[f"dx{t % 2}"].sem, 16 * (t // 2 + 1))
            for di in range(DC):
                idx = t * DC + di
                if idx >= 4:
                    v.wait_ge(S["sqcons"].sem, idx - 3)
                v.tensor_mul(xsq[di % 4][:, :], xbuf[b][:, di * TT:(di + 1) * TT],
                             xbuf[b][:, di * TT:(di + 1) * TT]).then_inc(*S["sq"].inc(1))

        def rows_1a(t):
            v.wait_ge(S["stats"].sem, t + 1)
            v.tensor_scalar_mul(rw[0], pb[6][0:1, :], 1.0 / D)
            v.tensor_scalar_mul(rw[1], pb[7][0:1, :], 1.0 / D)
            v.tensor_mul(rw[2], rw[0], rw[0])
            v.tensor_sub(rw[3], rw[1], rw[2])
            v.tensor_scalar_add(rw[1], rw[3], float(EPS))
            v.reciprocal(rw[2], rw[1]).then_inc(*S["v1"].inc(1))
            v.wait_ge(S["s2"].sem, t + 1)
            v.tensor_mul(rw[1], rw[0], rw[3]).then_inc(*S["v2"].inc(1))

        squares_1a(0)
        rows_1a(0)
        for t in range(NTT):
            b2 = t % 2
            v.wait_ge(S["bc"].sem, t + 1)
            v.tensor_copy(a_b[:, :], pb[6][:, :])
            v.tensor_copy(b_b[:, :], pb[7][:, :])
            v.tensor_copy(colp[b2][:, 0:8], pb[5][:, 0:8]).then_inc(*S["evac"].inc(1))
            if t + 1 < NTT:
                squares_1a(t + 1)
                rows_1a(t + 1)
            v.wait_ge(S["qk"].sem, t + 1)
            for pi, (pbo, dst, nwo) in enumerate([(0, Q_sb, 0), (2, K_sb, 2)]):
                for fo in range(2):
                    v.tensor_mul(tmpf[fo][:, :], pb[pbo + fo][:, :], a_b[:, :])
                    last = v.scalar_tensor_tensor(
                        dst[:, fo * T + t * TT:fo * T + (t + 1) * TT],
                        b_b[:, :], negws[:, nwo + fo:nwo + fo + 1], tmpf[fo][:, :],
                        ALU.mult, ALU.add)
            last.then_inc(*S["qkfix"].inc(1))
            for rnd in range(2):
                v.wait_ge(S["vmm"].sem, 2 * t + rnd + 1)
                for si in range(2):
                    sub = 2 * rnd + si
                    g64 = t * 4 + sub
                    v.tensor_scalar(tmpv[0][:, :], wvb[:, :],
                                    colp[b2][:, 4 + sub:5 + sub], None, ALU.mult)
                    v.tensor_scalar(tmpv[1][:, :], pb[4][:, si * 256:(si + 1) * 256],
                                    colp[b2][:, sub:sub + 1], None, ALU.mult)
                    for fo in range(2):
                        last = v.tensor_sub(
                            V_sb[:, fo * 64 * VW + g64 * VW:fo * 64 * VW + g64 * VW + 128],
                            tmpv[1][:, fo * 128:(fo + 1) * 128],
                            tmpv[0][:, fo * 128:(fo + 1) * 128])
                last.then_inc(*S["vevac"].inc(1))
        if has1b:
            v.wait_ge(S["dw"].sem, 96)
            grp = 0
            n_od_ev = 0
            oid = 0
            for (h, qi, ms) in sched:
                has_od = qi > 0
                for (m, p, kind, aux, dgi) in ms:
                    if kind == "dg":
                        v.wait_ge(S["ssc"].sem, p + 1)
                        if dgi >= 2:
                            v.wait_ge(S["sex"].sem, diag_pass_of[dgi - 2] + 1)
                        v.scalar_tensor_tensor(
                            tmpS[dgi % 2][:, :], pb[p % 4][:, :], ISQ,
                            md_sb[:, (h * 4 + aux) * TT:(h * 4 + aux + 1) * TT],
                            ALU.mult, ALU.add).then_inc(*S["sdg"].inc(1))
                    if kind == "od" and m == 4 * qi - 1:
                        v.wait_ge(S["sav"].sem, p + 1)
                        for sub in range(4):
                            last = v.tensor_copy(odsb[:, sub * VW:sub * VW + 129],
                                                 pb[4 + sub][:, 0:129])
                        n_od_ev += 1
                        last.then_inc(*S["sod"].inc(1))
                p_last = ms[-1][1]
                v.wait_ge(S["sav"].sem, p_last + 1)
                for sub in range(4):
                    if has_od:
                        v.tensor_scalar(cmb[:, sub * VW:sub * VW + 129],
                                        odsb[:, sub * VW:sub * VW + 129],
                                        ec_sb[:, h * 4 + sub:h * 4 + sub + 1], None, ALU.mult)
                        last = v.tensor_add(cmb[:, sub * VW:sub * VW + 129],
                                            cmb[:, sub * VW:sub * VW + 129],
                                            pb[4 + sub][:, 0:129])
                    else:
                        last = v.tensor_copy(cmb[:, sub * VW:sub * VW + 129],
                                             pb[4 + sub][:, 0:129])
                last.then_inc(*S["scmb"].inc(1))
                for sub in range(4):
                    v.reciprocal(rcp[:, sub:sub + 1], cmb[:, sub * VW + 128:sub * VW + 129])
                for sub in range(4):
                    slot = oid % 8
                    if oid >= 8:
                        v.wait_ge(S["soc"].sem, 16 * (oid - 7))
                    v.tensor_scalar(outp[:, slot * 128:(slot + 1) * 128],
                                    cmb[:, sub * VW:sub * VW + 128],
                                    rcp[:, sub:sub + 1], None, ALU.mult) \
                        .then_inc(*S["sot"].inc(1))
                    oid += 1
                grp += 1

    @block.sync
    def _(sy):
        if has1b:
            oid = 0
            for (h, qi, ms) in sched:
                for sub in range(4):
                    slot = oid % 8
                    sy.wait_ge(S["sot"].sem, oid + 1)
                    sy.dma_start(out=cc_in[qi * TT + sub * 128:qi * TT + (sub + 1) * 128,
                                           h * 128:(h + 1) * 128],
                                 in_=outp[:, slot * 128:(slot + 1) * 128]) \
                        .then_inc(*S["soc"].inc(16))
                    oid += 1
        if debug:
            sy.wait_ge(S["qkfix"].sem, NTT)
            sy.wait_ge(S["vevac"].sem, 2 * NTT)
            sy.dma_start(out=dbg["qT"], in_=Q_sb[:, :]).then_inc(*S["yst"].inc(16))
            sy.dma_start(out=dbg["kT"], in_=K_sb[:, :]).then_inc(*S["yst"].inc(16))
            sy.dma_start(out=dbg["v"], in_=V_sb[:, :]).then_inc(*S["yst"].inc(16))

    p1.close()
    p2 = ExitStack()
    sb2 = lambda name, shape, dt=F32: p2.enter_context(nc.sbuf_tensor(name, shape, dt))
    if has2:
        Y_sb = sb2("Y_sb", [128, DC * TPC])
        rows2 = sb2("rows2", [1, 4 * TT])
        rw2 = [rows2[0:1, i * TT:(i + 1) * TT] for i in range(4)]
        m2b = sb2("m2b", [128, TT])
        r2b = sb2("r2b", [128, TT])
        onesf2 = sb2("onesf2", [1, 128])
        ones_bf2 = sb2("ones_bf2", [128, 8], BF16)
    p2a = ExitStack()
    if has1b:
        outT_sb = p2a.enter_context(nc.sbuf_tensor("outT_sb", [128, DC * TPC], BF16))

        @block.sync
        def _(sy):
            rid = sy.partition_id()
            sy.wait_ge(S["scc"].sem, rid + 1)
            cc5 = cc_out.rearrange("(s r q) m -> s r q m", r=NC, q=TPC)
            stg = cc_stage.rearrange("(r q) m -> r q m", q=TPC)
            sy.dma_start(out=stg, in_=cc5[ts(rid, 1), :, :, :][0]).then_inc(*S["rcv"].inc(16))
            sy.wait_ge(S["rcv"].sem, 16)
            for fc in range(DC):
                r, cb = fc // 2, fc % 2
                sy.dma_start_transpose(
                    outT_sb[:, fc * TPC:(fc + 1) * TPC],
                    cc_stage[r * TPC:(r + 1) * TPC, cb * 128:(cb + 1) * 128]) \
                    .then_inc(*S["rcv"].inc(16))
            sy.wait_ge(S["rcv"].sem, 16 * 17)
            if debug:
                sy.dma_start(out=dbg["attnT"], in_=outT_sb[:, :]).then_inc(*S["yst"].inc(16))


    if has2:
        woT_sb = p2a.enter_context(nc.sbuf_tensor("woT_sb", [128, DC * D], BF16))
        xow = [p2a.enter_context(nc.sbuf_tensor(f"xow{i}", [128, TT], F32)) for i in range(2)]

        NWO = 32

        @block.vector
        def _(v):
            v.wait_ge(S["soc"].sem, 128 * 16)
            if debug:
                v.wait_ge(S["yst"].sem, 48)
            v.memset(onesf2[:, :], 1.0)
            v.memset(ones_bf2[:, :], 1.0).then_inc(*S["setup"].inc(1))
            for idx in range(NWO):
                v.wait_ge(S["swo"].sem, idx + 1)
                v.wait_ge(S[f"dxo{idx % 2}"].sem, 16 * (idx // 2 + 1))
                fo, tt = idx // 2, idx % 2
                v.tensor_add(Y_sb[:, fo * TPC + tt * TT:fo * TPC + (tt + 1) * TT],
                             pb[idx % 2][:, :], xow[idx % 2][:, :]) \
                    .then_inc(*S["y1e"].inc(1))

        @block.gpsimd
        def _(g):
            if debug:
                g.wait_ge(S["yst"].sem, 48)
            for i in range(4):
                g.dma_start(out=woT_sb[:, i * 4 * D:(i + 1) * 4 * D],
                            in_=wo_st[:, i * 4 * D:(i + 1) * 4 * D]).then_inc(*S["dwo"].inc(16))

        @block.sync
        def _(sy):
            for idx in range(NWO):
                if idx >= 2:
                    sy.wait_ge(S["y1e"].sem, idx - 1)
                sy.dma_start(out=xow[idx % 2][:, :], in_=xown_st[idx]) \
                    .then_inc(*S[f"dxo{idx % 2}"].inc(16))

        @block.tensor
        def _(tn):
            tn.wait_ge(S["dwo"].sem, 64)
            tn.wait_ge(S["rcv"].sem, 16 * 17)
            tn.wait_ge(S["setup"].sem, 2)
            for idx in range(NWO):
                fo, tt = idx // 2, idx % 2
                if idx >= 2:
                    tn.wait_ge(S["y1e"].sem, idx - 1)
                for di in range(DC):
                    mm = tn.matmul(pb[idx % 2][:, :],
                                   woT_sb[:, di * D + fo * 128:di * D + (fo + 1) * 128],
                                   outT_sb[:, di * TPC + tt * TT:di * TPC + (tt + 1) * TT],
                                   start=(di == 0), stop=(di == DC - 1))
                mm.then_inc(*S["swo"].inc(1))

        p2a.close()
        xn2_sb = sb2("xn2_sb", [128, DC * TPC], BF16)
        act_sb = sb2("act_sb", [128, FSC * TPC], BF16)
        gu_buf = [sb2(f"gu_buf{i}", [128, DC * 256], BF16) for i in range(2)]
        dn_buf = [sb2(f"dn_buf{i}", [128, FSC * 128], BF16) for i in range(2)]
        gate_sb = sb2("gate_sb", [128, DC * 8], BF16)
        idn_sb = sb2("idn_sb", [128, 128])
        sel_sb = sb2("sel_sb", [8, 8 * 128])
        cT_sb = sb2("cT_sb", [8, TPC])
        ybf = [sb2(f"ybf{i}", [128, TT], BF16) for i in range(2)]
        ysq = [sb2(f"ysq{i}", [128, TT], BF16) for i in range(2)]
        sg_sb = [sb2(f"sg_sb{i}", [128, TT], BF16) for i in range(2)]
        u1_sb = [sb2(f"u1_sb{i}", [128, TT], BF16) for i in range(2)]
        tmp2 = [sb2("tmp2_0", [128, TT])] * 2
        cbt_ = [m2b, r2b]
        gsc = sb2("gsc", [128, 64])
        gs2 = sb2("gs2", [128, 64])
        gct = sb2("gct", [128, 64])
        gtA = sb2("gtA", [128, 64])
        gtB = sb2("gtB", [128, 64])
        gcl = sb2("gcl", [128, 64])

        JCs = [FC] * E8 + [FSC]
        EORDER = [E8] + list(range(E8))
        JPRE = {}
        acc = 0
        for e in EORDER:
            JPRE[e] = acc
            acc += JCs[e]
        TJC = acc
        CBPOS = {e: i for i, e in enumerate(EORDER) if e < E8}
        jgl = []
        for e in EORDER:
            for j in range(JCs[e]):
                jgl.append((e, j, JPRE[e] + j))
        hgl = []
        for pos, e in enumerate(EORDER):
            for h in range(DC):
                hgl.append((e, h, pos * DC + h))

        @block.gpsimd
        def _(g):
            g.wait_ge(S["y1e"].sem, 32)
            g.dma_start(out=gate_sb[:, :], in_=gate_st).then_inc(*S["dwo"].inc(16))
            g.dma_start(out=idn_sb[:, :], in_=idn_st).then_inc(*S["dwo"].inc(16))
            g.dma_start(out=sel_sb[:, :], in_=sel_st).then_inc(*S["dwo"].inc(16))
            for pos, e in enumerate(EORDER):
                for j in range(JCs[e]):
                    jjx = JPRE[e] + j
                    if jjx >= 2:
                        g.wait_ge(S["gups"].sem, 4 * (jjx - 1))
                    src = gu_st[e, j] if e < E8 else sgu_st[j]
                    g.dma_start(out=gu_buf[jjx % 2][:, :], in_=src) \
                        .then_inc(*S[f"dgu{jjx % 2}"].inc(16))
                for h in range(DC):
                    hhx = pos * DC + h
                    if hhx >= 2:
                        g.wait_ge(S["dnps"].sem, 2 * (hhx - 1))
                    if e < E8:
                        g.dma_start(out=dn_buf[hhx % 2][:, 0:FC * 128], in_=dn_st[e, h]) \
                            .then_inc(*S[f"ddn{hhx % 2}"].inc(16))
                    else:
                        g.dma_start(out=dn_buf[hhx % 2][:, :], in_=sdn_st[h]) \
                            .then_inc(*S[f"ddn{hhx % 2}"].inc(16))

        @block.scalar
        def _(s):
            for tt in range(2):
                for di in range(DC):
                    idx = tt * DC + di
                    s.wait_ge(S["v21"].sem, idx + 1)
                    if idx >= 2:
                        s.wait_ge(S["st2"].sem, 2 * (idx - 1))
                    s.activation(out=ysq[di % 2][:, :], in_=ybf[di % 2][:, :],
                                 func=AF.Square).then_inc(*S["s21"].inc(1))
                s.wait_ge(S["v22"].sem, tt + 1)
                s.activation(out=rw2[3], in_=rw2[2], func=AF.Sqrt).then_inc(*S["s22"].inc(1))
            for (e, j, jjx) in jgl:
                for tt in range(2):
                    s.wait_ge(S["gups"].sem, 4 * jjx + 2 * tt + 1)
                    s.activation(out=sg_sb[tt][:, :], in_=pb[2 * tt][:, :],
                                 func=AF.Silu).then_inc(*S["sil"].inc(1))
                if e == E8 and j == 0:
                    s.wait_ge(S["g1"].sem, 16)
                    for g8 in range(8):
                        s.activation(out=gsc[:, 8 * g8:8 * g8 + 8],
                                     in_=pb[5][:, 8 * g8:8 * g8 + 8],
                                     func=AF.Exp, bias=gcl[:, 8 + g8:9 + g8], scale=1.0) \
                            .then_inc(*S["g2"].inc(1))

        @block.tensor
        def _(tn):
            for tt in range(2):
                for di in range(DC):
                    idx = tt * DC + di
                    tn.wait_ge(S["s21"].sem, idx + 1)
                    if tt > 0 and di == 0:
                        tn.wait_ge(S["ev2"].sem, tt)
                    tn.matmul(pb[7][0:1, :], ones_bf2[:, 0:1], ysq[di % 2][:, :],
                              start=(di == 0), stop=(di == DC - 1)) \
                        .then_inc(*S["st2"].inc(1))
                    mm = tn.matmul(pb[6][0:1, :], ones_bf2[:, 0:1], ybf[di % 2][:, :],
                                   start=(di == 0), stop=(di == DC - 1)) \
                        .then_inc(*S["st2"].inc(1))
                tn.wait_ge(S["v23"].sem, tt + 1)
                tn.matmul(pb[6][:, :], onesf2[0:1, :], rw2[0], start=True, stop=True)
                tn.matmul(pb[7][:, :], onesf2[0:1, :], rw2[3], start=True, stop=True) \
                    .then_inc(*S["bc2"].inc(1))
            tn.wait_ge(S["dwo"].sem, 112)

            def gating_mms():
                for g8 in range(8):
                    for di in range(DC):
                        mm = tn.matmul(pb[5][:, 8 * g8:8 * g8 + 8],
                                       xn2_sb[:, di * TPC + g8 * 128:di * TPC + (g8 + 1) * 128],
                                       gate_sb[:, di * 8:(di + 1) * 8],
                                       start=(di == 0), stop=(di == DC - 1))
                    mm.then_inc(*S["glog"].inc(1))

            def gating_tr(rnd):
                if rnd == 0:
                    tn.wait_ge(S["gcmb"].sem, 8)
                else:
                    tn.wait_ge(S["gsel"].sem, rnd)
                for gi in range(4):
                    g8 = rnd * 4 + gi
                    tn.transpose(pb[4][0:8, gi * 128:(gi + 1) * 128],
                                 gct[:, 8 * g8:8 * g8 + 8], idn_sb[:, :]) \
                        .then_inc(*S["gtr"].inc(1))
            for pos, e in enumerate(EORDER):
                if e < E8:
                    tn.wait_ge(S["dnac"].sem, 32 * pos)
                    tn.wait_ge(S["gsel"].sem, 2)
                    for tt in range(2):
                        tn.matmul(pb[4 + tt][:, :], sel_sb[0:8, e * 128:(e + 1) * 128],
                                  cT_sb[0:8, tt * TT:(tt + 1) * TT], start=True, stop=True) \
                            .then_inc(*S["cbt"].inc(1))
                for j in range(JCs[e]):
                    jjx = JPRE[e] + j
                    if e == E8 and j == 1:
                        gating_mms()
                    if e == E8 and j == 3:
                        gating_tr(0)
                    if e == E8 and j == 5:
                        gating_tr(1)
                    tn.wait_ge(S[f"dgu{jjx % 2}"].sem, 16 * (jjx // 2 + 1))
                    if e < E8 and j == 0:
                        tn.wait_ge(S["cbk"].sem, 2 * CBPOS[e])
                    for tt in range(2):
                        tn.wait_ge(S["xn2d"].sem, 16 * (tt + 1))
                        if jjx >= 1:
                            tn.wait_ge(S["sil"].sem, 2 * (jjx - 1) + tt + 1)
                            tn.wait_ge(S["uact"].sem, 2 * (jjx - 1) + tt + 1)
                        for di in range(DC):
                            mm = tn.matmul(pb[2 * tt][:, :],
                                           gu_buf[jjx % 2][:, di * 256:di * 256 + 128],
                                           xn2_sb[:, di * TPC + tt * TT:di * TPC + (tt + 1) * TT],
                                           start=(di == 0), stop=(di == DC - 1))
                        mm.then_inc(*S["gups"].inc(1))
                        for di in range(DC):
                            mm = tn.matmul(pb[2 * tt + 1][:, :],
                                           gu_buf[jjx % 2][:, di * 256 + 128:di * 256 + 256],
                                           xn2_sb[:, di * TPC + tt * TT:di * TPC + (tt + 1) * TT],
                                           start=(di == 0), stop=(di == DC - 1))
                        mm.then_inc(*S["gups"].inc(1))
                JC = JCs[e]
                for h in range(DC):
                    hhx = pos * DC + h
                    tn.wait_ge(S[f"ddn{hhx % 2}"].sem, 16 * (hhx // 2 + 1))
                    tn.wait_ge(S["uact"].sem, 2 * (JPRE[e] + JC))
                    for tt in range(2):
                        slot = 2 * hhx + tt
                        if slot >= 2:
                            tn.wait_ge(S["dnac"].sem, slot - 1)
                        for fj in range(JC):
                            mm = tn.matmul(pb[4 + tt][:, :],
                                           dn_buf[hhx % 2][:, fj * 128:(fj + 1) * 128],
                                           act_sb[:, fj * TPC + tt * TT:fj * TPC + (tt + 1) * TT],
                                           start=(fj == 0), stop=(fj == JC - 1))
                        mm.then_inc(*S["dnps"].inc(1))

        @block.vector
        def _(v):
            v.wait_ge(S["y1e"].sem, 32)
            for tt in range(2):
                for di in range(DC):
                    idx = tt * DC + di
                    if idx >= 2:
                        v.wait_ge(S["s21"].sem, idx - 1)
                    v.tensor_copy(ybf[di % 2][:, :],
                                  Y_sb[:, di * TPC + tt * TT:di * TPC + (tt + 1) * TT]) \
                        .then_inc(*S["v21"].inc(1))
                v.wait_ge(S["st2"].sem, 32 * (tt + 1))
                v.tensor_scalar_mul(rw2[0], pb[6][0:1, :], 1.0 / D)
                v.tensor_scalar_mul(rw2[1], pb[7][0:1, :], 1.0 / D)
                v.tensor_mul(rw2[2], rw2[0], rw2[0])
                v.tensor_sub(rw2[3], rw2[1], rw2[2])
                v.tensor_scalar_add(rw2[1], rw2[3], float(EPS))
                v.reciprocal(rw2[2], rw2[1]).then_inc(*S["v22"].inc(1))
                v.wait_ge(S["s22"].sem, tt + 1)
                v.engine_nop().then_inc(*S["v23"].inc(1))
                v.wait_ge(S["bc2"].sem, tt + 1)
                v.tensor_copy(m2b[:, :], pb[6][:, :])
                v.tensor_copy(r2b[:, :], pb[7][:, :]).then_inc(*S["ev2"].inc(1))
                for di in range(DC):
                    v.tensor_sub(tmp2[di % 2][:, :],
                                 Y_sb[:, di * TPC + tt * TT:di * TPC + (tt + 1) * TT], m2b[:, :])
                    v.tensor_mul(xn2_sb[:, di * TPC + tt * TT:di * TPC + (tt + 1) * TT],
                                 tmp2[di % 2][:, :], r2b[:, :]).then_inc(*S["xn2d"].inc(1))
            v.wait_ge(S["glog"].sem, 8)
            for g8 in range(8):
                v.reduce_max(gcl[:, g8:g8 + 1], pb[5][:, 8 * g8:8 * g8 + 8], axis=AX.X)
            for g8 in range(8):
                v.tensor_scalar_mul(gcl[:, 8 + g8:9 + g8], gcl[:, g8:g8 + 1], -1.0)
            v.engine_nop().then_inc(*S["g1"].inc(16))
            for g8 in range(8):
                v.wait_ge(S["g2"].sem, g8 + 1)
                v.reduce_sum(gcl[:, 16 + g8:17 + g8], gsc[:, 8 * g8:8 * g8 + 8], axis=AX.X)
            for g8 in range(8):
                v.reciprocal(gcl[:, 24 + g8:25 + g8], gcl[:, 16 + g8:17 + g8])
            for g8 in range(8):
                v.tensor_scalar(gs2[:, 8 * g8:8 * g8 + 8], gsc[:, 8 * g8:8 * g8 + 8],
                                gcl[:, 24 + g8:25 + g8], None, ALU.mult)
            for g8 in range(8):
                v.reduce_max(gcl[:, 32 + g8:33 + g8], gs2[:, 8 * g8:8 * g8 + 8], axis=AX.X)
            for g8 in range(8):
                v.tensor_scalar(gtA[:, 8 * g8:8 * g8 + 8], gs2[:, 8 * g8:8 * g8 + 8],
                                gcl[:, 32 + g8:33 + g8], None, ALU.is_equal)
            for g8 in range(8):
                v.scalar_tensor_tensor(gtB[:, 8 * g8:8 * g8 + 8], gtA[:, 8 * g8:8 * g8 + 8],
                                       -1e9, gs2[:, 8 * g8:8 * g8 + 8], ALU.mult, ALU.add)
            for g8 in range(8):
                v.reduce_max(gcl[:, 40 + g8:41 + g8], gtB[:, 8 * g8:8 * g8 + 8], axis=AX.X)
            for g8 in range(8):
                v.tensor_scalar(gtA[:, 8 * g8:8 * g8 + 8], gs2[:, 8 * g8:8 * g8 + 8],
                                gcl[:, 40 + g8:41 + g8], None, ALU.is_ge)
            for g8 in range(8):
                v.tensor_mul(gct[:, 8 * g8:8 * g8 + 8], gs2[:, 8 * g8:8 * g8 + 8],
                             gtA[:, 8 * g8:8 * g8 + 8]).then_inc(*S["gcmb"].inc(1))
            for pos, e in enumerate(EORDER):
                if e < E8:
                    for tt in range(2):
                        v.wait_ge(S["cbt"].sem, 2 * (CBPOS[e] - 1) + tt + 1)
                        v.tensor_copy(cbt_[tt][:, :], pb[4 + tt][:, :]) \
                            .then_inc(*S["cbk"].inc(1))
                for j in range(JCs[e]):
                    jjx = JPRE[e] + j
                    for tt in range(2):
                        v.wait_ge(S["gups"].sem, 4 * jjx + 2 * tt + 2)
                        v.wait_ge(S["sil"].sem, 2 * jjx + tt + 1)
                        if e < E8:
                            v.tensor_mul(u1_sb[tt][:, :], pb[2 * tt + 1][:, :],
                                         cbt_[tt][:, :])
                            v.tensor_mul(act_sb[:, j * TPC + tt * TT:j * TPC + (tt + 1) * TT],
                                         u1_sb[tt][:, :], sg_sb[tt][:, :]) \
                                .then_inc(*S["uact"].inc(1))
                        else:
                            v.tensor_mul(act_sb[:, j * TPC + tt * TT:j * TPC + (tt + 1) * TT],
                                         pb[2 * tt + 1][:, :], sg_sb[tt][:, :]) \
                                .then_inc(*S["uact"].inc(1))
                    if e == E8 and j in (3, 5):
                        rnd = (j - 3) // 2
                        v.wait_ge(S["gtr"].sem, 4 * (rnd + 1))
                        v.tensor_copy(cT_sb[0:8, rnd * TT:(rnd + 1) * TT], pb[4][0:8, 0:512]) \
                            .then_inc(*S["gsel"].inc(1))
                for h in range(DC):
                    hhx = pos * DC + h
                    for tt in range(2):
                        v.wait_ge(S["dnps"].sem, 2 * hhx + tt + 1)
                        v.tensor_add(Y_sb[:, h * TPC + tt * TT:h * TPC + (tt + 1) * TT],
                                     Y_sb[:, h * TPC + tt * TT:h * TPC + (tt + 1) * TT],
                                     pb[4 + tt][:, :]).then_inc(*S["dnac"].inc(1))

        @block.sync
        def _(sy):
            if debug:
                sy.wait_ge(S["xn2d"].sem, 32)
                sy.dma_start(out=dbg["xn2"], in_=xn2_sb[:, :]).then_inc(*S["yst"].inc(16))
                sy.wait_ge(S["gsel"].sem, 2)
                sy.dma_start(out=dbg["comb"], in_=cT_sb[0:8, :]).then_inc(*S["yst"].inc(16))

            sy.wait_ge(S["dnac"].sem, 32 * 9)
            y3 = yT.rearrange("(a p) m -> a p m", p=128)
            for di in range(DC):
                sy.dma_start(out=y3[di], in_=Y_sb[:, di * TPC:(di + 1) * TPC]) \
                    .then_inc(*S["yst"].inc(16))

    @block.sync
    def _(sy):
        sy.wait_ge(S["yst"].sem, S["yst"].v)

    if not has1b:
        p1.close()
    else:
        if not has2:
            p2a.close()
        p2.close()
    ctx.close()
    return nc



def stage_inputs(inp, phase="full"):
    x = np.asarray(inp["L0_attn_x"]).reshape(T, D)
    Wq = np.asarray(inp["L0_attn_Wq"])
    Wk = np.asarray(inp["L0_attn_Wk"])
    Wv = np.asarray(inp["L0_attn_Wv"])

    x_st = np.ascontiguousarray(
        x.reshape(NTT, TT, DC, 128).transpose(0, 3, 2, 1).reshape(NTT, 128, DC * TT))

    def wslice(W, c):
        hA, hB = head_pair(c)
        WT = W.T
        wc = np.concatenate([WT[:, 128 * hA:128 * (hA + 1)], WT[:, 128 * hB:128 * (hB + 1)]], axis=1)
        return np.ascontiguousarray(wc.reshape(DC, 128, 256).transpose(1, 0, 2).reshape(128, DC * 256))

    in_maps = []
    for c in range(NC):
        hA, hB = head_pair(c)
        sl = (slope(hA), slope(hB))
        kl = np.arange(128, dtype=np.float64)[:, None]
        ql = np.arange(TT, dtype=np.float64)[None, :]
        md = np.empty((128, 8 * TT), np.float32)
        for hi in range(2):
            for j in range(4):
                mm = sl[hi] * (128 * j + kl - ql)
                mm = np.where(128 * j + kl <= ql, mm, NEG_BIG)
                md[:, (hi * 4 + j) * TT:(hi * 4 + j + 1) * TT] = mm.astype(np.float32)
        bc = np.zeros((128, 128), np.float32)
        for hi in range(2):
            for dl in range(64):
                bc[:, hi * 64 + dl] = (sl[hi] * (kl[:, 0] - 128.0 * dl)).astype(np.float32)
        ec = np.zeros((128, 8), np.float32)
        for hi in range(2):
            for sub in range(4):
                ec[:, hi * 4 + sub] = np.exp(-sl[hi] * (128.0 * sub + kl[:, 0])).astype(np.float32)
        m = {
            "x_st": x_st, "wq_st": wslice(Wq, c), "wk_st": wslice(Wk, c), "wv_st": wslice(Wv, c),
            "md_st": md, "bc_st": bc, "ec_st": ec,
        }
        if phase == "full":
            m.update(stage_phase2(inp, c))
        in_maps.append(m)
    return in_maps


def stage_phase2(inp, c):
    Wo = np.asarray(inp["L0_attn_Wo"])
    gate = np.asarray(inp["L0_ffn_gate_weight"])
    gu = np.asarray(inp["L0_ffn_gate_up_weights"])
    dn = np.asarray(inp["L0_ffn_down_weights"])
    sgu = np.asarray(inp["L0_ffn_shared_gate_up"])
    sdn = np.asarray(inp["L0_ffn_shared_down"])
    x = np.asarray(inp["L0_attn_x"]).reshape(T, D)

    perm = []
    for r in range(NC):
        ha, hb = head_pair(r)
        perm.extend(range(128 * ha, 128 * ha + 128))
        perm.extend(range(128 * hb, 128 * hb + 128))
    woT = Wo.T[np.asarray(perm), :]
    wo_st = np.ascontiguousarray(woT.reshape(DC, 128, D).transpose(1, 0, 2).reshape(128, DC * D))
    xT = x.T
    xown = xT[:, TPC * c:TPC * (c + 1)]
    xown_st = np.ascontiguousarray(
        xown.reshape(DC, 128, 2, TT).transpose(0, 2, 1, 3).reshape(32, 128, TT))
    gate_st = np.ascontiguousarray(gate.T.reshape(DC, 128, 8).transpose(1, 0, 2).reshape(128, DC * 8))
    idn = np.eye(128, dtype=np.float32)
    sel = np.zeros((8, 8 * 128), np.float32)
    for e in range(8):
        sel[e, e * 128:(e + 1) * 128] = 1.0
    gug = gu[:, :F, :].reshape(E8, FC, 128, DC, 128)
    guu = gu[:, F:, :].reshape(E8, FC, 128, DC, 128)
    gu_st = np.empty((E8, FC, 128, DC * 256), np.float32)
    gg = gug.transpose(0, 1, 4, 3, 2)
    uu = guu.transpose(0, 1, 4, 3, 2)
    gu_st = np.ascontiguousarray(
        np.stack([gg, uu], axis=4).reshape(E8, FC, 128, DC, 2 * 128).reshape(E8, FC, 128, DC * 256))
    dn_st = np.ascontiguousarray(
        dn.reshape(E8, DC, 128, FC, 128).transpose(0, 1, 4, 3, 2).reshape(E8, DC, 128, FC * 128))
    sgug = sgu[:FS, :].reshape(FSC, 128, DC, 128)
    sguu = sgu[FS:, :].reshape(FSC, 128, DC, 128)
    sgu_st = np.ascontiguousarray(
        np.stack([sgug.transpose(0, 3, 2, 1), sguu.transpose(0, 3, 2, 1)], axis=3)
        .reshape(FSC, 128, DC, 256).reshape(FSC, 128, DC * 256))
    sdn_st = np.ascontiguousarray(
        sdn.reshape(DC, 128, FSC, 128).transpose(0, 3, 2, 1).reshape(DC, 128, FSC * 128))
    return {
        "wo_st": wo_st, "xown_st": xown_st, "gate_st": gate_st,
        "idn_st": idn, "sel_st": sel,
        "gu_st": gu_st, "dn_st": dn_st, "sgu_st": sgu_st, "sdn_st": sdn_st,
    }



def kernel(**inputs):
    in_maps = stage_inputs(inputs, phase="full")
    nc = build_nc(phase="full", debug=False)
    res = run_bass_kernel_spmd(nc, in_maps, core_ids=list(range(NC)))
    out = np.empty((1, T, D), np.float32)
    for c in range(NC):
        out[0, TPC * c:TPC * (c + 1), :] = np.asarray(res.results[c]["yT"]).T
    return out


# version 08
# speedup vs baseline: 1.3821x; 1.0331x over previous
import sys
sys.path.insert(0, "/opt/trn_rl_repo")
import numpy as np
from contextlib import ExitStack
import concourse.bass as bass
import concourse.mybir as mybir
from concourse.bass_utils import run_bass_kernel_spmd

F32 = mybir.dt.float32
BF16 = mybir.dt.bfloat16
AF = mybir.ActivationFunctionType
ALU = mybir.AluOpType
AX = mybir.AxisListType
ts = bass.ts

T, D, NH, HD, NC = 8192, 2048, 16, 128, 8
TT = 512
NTT = T // TT
DC = D // 128
TPC = T // NC
E8, F, FS = 8, 1408, 2816
FC, FSC = F // 128, FS // 128
EPS = 1e-5
ISQ = float(1.0 / np.sqrt(HD))
VW = 136
NEG_BIG = -1e30


def head_pair(c):
    return c, 15 - c


def slope(h):
    return 2.0 ** (-(h + 1) / 2.0)


class Ctr:
    def __init__(self, nc, name, ctx):
        self.sem = ctx.enter_context(nc.semaphore(name))
        self.v = 0

    def inc(self, n=1):
        self.v += n
        return self.sem, n


def build_nc(phase="full", debug=False):
    nc = bass.Bass("TRN2", target_bir_lowering=False, debug=False, detect_race_conditions=False)
    ctx = ExitStack()
    dr = lambda name, shape, dt, kind="ExternalInput": nc.dram_tensor(name, shape, dt, kind=kind).ap()

    x_st = dr("x_st", [NTT, 128, DC * TT], F32)
    wq_st = dr("wq_st", [128, DC * 256], F32)
    wk_st = dr("wk_st", [128, DC * 256], F32)
    wv_st = dr("wv_st", [128, DC * 256], F32)
    md_st = dr("md_st", [128, 8 * TT], F32)
    bc_st = dr("bc_st", [128, 128], F32)
    ec_st = dr("ec_st", [128, 8], F32)
    has2 = phase == "full"
    if has2:
        wo_st = dr("wo_st", [128, DC * D], F32)
        xown_st = dr("xown_st", [32, 128, TT], F32)
        gate_st = dr("gate_st", [128, DC * 8], F32)
        idn_st = dr("idn_st", [128, 128], F32)
        sel_st = dr("sel_st", [8, 8 * 128], F32)
        gu_st = dr("gu_st", [E8, FC, 128, DC * 256], F32)
        dn_st = dr("dn_st", [E8, DC, 128, FC * 128], F32)
        sgu_st = dr("sgu_st", [FSC, 128, DC * 256], F32)
        sdn_st = dr("sdn_st", [DC, 128, FSC * 128], F32)
        yT = dr("yT", [D, TPC], F32, "ExternalOutput")
    has1b = phase in ("1b", "full")
    if has1b:
        cc_in = nc.dram_tensor("cc_in", [T, 2 * HD], BF16).ap()
        cc_out = nc.dram_tensor("cc_out", [NC * T, 2 * HD], BF16).ap()
        cc_stage = nc.dram_tensor("cc_stage", [NC * TPC, 2 * HD], BF16).ap()

    dbg = {}
    if debug:
        dbg["qT"] = dr("dbg_qT", [128, 2 * T], BF16, "ExternalOutput")
        dbg["kT"] = dr("dbg_kT", [128, 2 * T], BF16, "ExternalOutput")
        dbg["v"] = dr("dbg_v", [128, 2 * 64 * VW], BF16, "ExternalOutput")
        if has1b:
            dbg["attnT"] = dr("dbg_attnT", [128, DC * TPC], BF16, "ExternalOutput")
        if has2:
            dbg["y1"] = dr("dbg_y1", [128, DC * TPC], F32, "ExternalOutput")
            dbg["xn2"] = dr("dbg_xn2", [128, DC * TPC], BF16, "ExternalOutput")
            dbg["comb"] = dr("dbg_comb", [8, TPC], F32, "ExternalOutput")


    sbuf = lambda name, shape, dt=F32: ctx.enter_context(nc.sbuf_tensor(name, shape, dt))
    pb = [ctx.enter_context(nc.psum_tensor(f"pb{i}", [128, 512], F32)) for i in range(8)]

    S = {}
    for name in ["dx0", "dx1", "dw", "sq", "sqcons", "stats", "v1", "s2", "v2", "bc",
                 "evac", "qk", "qkfix", "vmm", "vevac", "xcons", "setup",
                 "tsu", "vsu", "p1a",
                 "ssc", "sdg", "sex", "sav", "sod", "scmb", "sot", "soc",
                 "scc", "rcv", "pb1",
                 "dwo", "swo", "svo", "y1e", "st2", "v21", "s21", "v22",
                 "bc2", "ev2", "xn2d", "glog", "gcmb", "gsel",
                 "dgu0", "dgu1", "ddn0", "ddn1", "gups", "sil", "uact", "dnps", "dnac",
                 "cbk", "cbt", "dxo0", "dxo1", "s22", "v23", "g1", "g2", "gtr",
                 "yst"]:
        S[name] = Ctr(nc, name, ctx)

    block = ctx.enter_context(nc.Block())

    p1 = ExitStack()
    sb1 = lambda name, shape, dt=F32: p1.enter_context(nc.sbuf_tensor(name, shape, dt))
    wq_sb = sb1("wq_sb", [128, DC * 256], BF16)
    wk_sb = sb1("wk_sb", [128, DC * 256], BF16)
    wv_sb = sb1("wv_sb", [128, DC * 256], BF16)
    xbuf = [sb1(f"xbuf{i}", [128, DC * TT], BF16) for i in range(2)]
    xsq = [sb1(f"xsq{i}", [128, TT], BF16) for i in range(4)]
    Q_sb = sb1("Q_sb", [128, 2 * T], BF16)
    K_sb = sb1("K_sb", [128, 2 * T], BF16)
    V_sb = sb1("V_sb", [128, 2 * 64 * VW], BF16)
    rows = sb1("rows", [1, 4 * TT])
    rw = [rows[0:1, i * TT:(i + 1) * TT] for i in range(4)]
    a_b = sb1("a_b", [128, TT])
    colp = [sb1(f"colp{i}", [128, 16]) for i in range(2)]
    ones_bf = sb1("ones_bf", [128, 8], BF16)
    onesf = sb1("onesf", [1, 128])
    one1 = sb1("one1", [1, 8])
    wsrow = sb1("wsrow", [1, 3 * 256])
    wsbf = sb1("wsbf", [1, 3 * 256], BF16)
    negmb = sb1("negmb", [1, 2 * TT], BF16)
    md_sb = sb1("md_sb", [128, 8 * TT])
    bcc_sb = sb1("bcc_sb", [128, 128])
    ec_sb = sb1("ec_sb", [128, 8])
    P_pool = [sb1(f"Pp{i}", [128, TT], BF16) for i in range(4)]
    tmpS = [sb1(f"tmpS{i}", [128, TT]) for i in range(2)]
    tmpf = tmpS
    tmpv = [tmpS[0][:, 0:256], tmpS[1][:, 0:256]]
    odsb = sb1("odsb", [128, 4 * VW])
    cmb = sb1("cmb", [128, 4 * VW])
    rcp = sb1("rcp", [128, 16])
    outp = sb1("outp", [128, 8 * 128], BF16)

    sched = []
    p_ctr = 0
    dgi_ctr = 0
    diag_pass_of = {}
    for qi in range(NTT):
        for h in range(2):
            ms = []
            for m in range(4 * qi + 4):
                if m < 4 * qi:
                    ms.append((m, p_ctr, "od", 4 * qi - m, None))
                else:
                    diag_pass_of[dgi_ctr] = p_ctr
                    ms.append((m, p_ctr, "dg", m - 4 * qi, dgi_ctr))
                    dgi_ctr += 1
                p_ctr += 1
            sched.append((h, qi, ms))
    NPASS = p_ctr

    @block.gpsimd
    def _(g):
        g.dma_start(out=wq_sb[:, :], in_=wq_st).then_inc(*S["dw"].inc(16))
        g.dma_start(out=wk_sb[:, :], in_=wk_st).then_inc(*S["dw"].inc(16))
        g.dma_start(out=wv_sb[:, :], in_=wv_st).then_inc(*S["dw"].inc(16))
        g.dma_start(out=md_sb[:, :], in_=md_st).then_inc(*S["dw"].inc(16))
        g.dma_start(out=bcc_sb[:, :], in_=bc_st).then_inc(*S["dw"].inc(16))
        g.dma_start(out=ec_sb[:, :], in_=ec_st).then_inc(*S["dw"].inc(16))
        for t in range(NTT):
            if t >= 2:
                g.wait_ge(S["vmm"].sem, 2 * t - 2)
            g.dma_start(out=xbuf[t % 2][:, :], in_=x_st[t]).then_inc(*S[f"dx{t % 2}"].inc(16))
        if has1b:
            for i in range(NC):
                g.wait_ge(S["soc"].sem, 16 * 8 * (2 * i + 2))
                g.collective_compute(
                    "AllGather", ALU.bypass, replica_groups=[list(range(NC))],
                    ins=[cc_in[TPC * i:TPC * (i + 1), :].opt()],
                    outs=[cc_out[NC * TPC * i:NC * TPC * (i + 1), :].opt()],
                ).then_inc(*S["scc"].inc(1))

    @block.scalar
    def _(s):
        for t in range(NTT):
            s.wait_ge(S["v1"].sem, t + 1)
            s.activation(out=rw[3], in_=rw[2], func=AF.Sqrt).then_inc(*S["s2"].inc(1))
        if has1b:
            s.wait_ge(S["dw"].sem, 96)
            for (h, qi, ms) in sched:
                for (m, p, kind, aux, dgi) in ms:
                    if p >= 4:
                        s.wait_ge(S["sav"].sem, p - 3)
                    if kind == "od":
                        s.wait_ge(S["ssc"].sem, p + 1)
                        s.activation(out=P_pool[p % 4][:, :], in_=pb[p % 4][:, :],
                                     func=AF.Exp, bias=bcc_sb[:, h * 64 + aux:h * 64 + aux + 1],
                                     scale=ISQ).then_inc(*S["sex"].inc(1))
                    else:
                        s.wait_ge(S["sdg"].sem, dgi + 1)
                        s.activation(out=P_pool[p % 4][:, :], in_=tmpS[dgi % 2][:, :],
                                     func=AF.Exp).then_inc(*S["sex"].inc(1))

    @block.tensor
    def _(tn):
        tn.wait_ge(S["dw"].sem, 96)
        tn.wait_ge(S["setup"].sem, 1)
        for wi, wsb in enumerate([wq_sb, wk_sb, wv_sb]):
            if wi:
                tn.wait_ge(S["vsu"].sem, wi)
            for di in range(DC):
                mm = tn.matmul(pb[5][0:1, 0:256], ones_bf[:, 0:1], wsb[:, di * 256:(di + 1) * 256],
                               start=(di == 0), stop=(di == DC - 1))
            mm.then_inc(*S["tsu"].inc(1))
        tn.wait_ge(S["vsu"].sem, 4)
        def stats_1a(t):
            b = t % 2
            tn.wait_ge(S["evac"].sem, t)
            for di in range(DC):
                tn.wait_ge(S["sq"].sem, t * DC + di + 1)
                tn.matmul(pb[7][0:1, :], ones_bf[:, 0:1], xsq[di % 4][:, :],
                          start=(di == 0), stop=(di == DC - 1)) \
                    .then_inc(*S["sqcons"].inc(1))
                mm = tn.matmul(pb[6][0:1, :], ones_bf[:, 0:1], xbuf[b][:, di * TT:(di + 1) * TT],
                               start=(di == 0), stop=(di == DC - 1))
            mm.then_inc(*S["stats"].inc(1))

        def bcast_1a(t):
            tn.wait_ge(S["s2"].sem, t + 1)
            tn.wait_ge(S["evac"].sem, t)
            tn.matmul(pb[6][:, :], onesf[0:1, :], rw[3], start=True, stop=True)
            for sub in range(4):
                mm = tn.matmul(pb[5][:, sub:sub + 1], rw[3][0:1, sub * 128:(sub + 1) * 128],
                               one1[0:1, 0:1], start=True, stop=True)
            mm.then_inc(*S["bc"].inc(1))

        stats_1a(0)
        bcast_1a(0)
        for t in range(NTT):
            b = t % 2
            tn.wait_ge(S["evac"].sem, t + 1)
            if t > 0:
                tn.wait_ge(S["qkfix"].sem, t)
            for pi, (pbo, wsb) in enumerate(((0, wq_sb), (2, wk_sb))):
                for fo in range(2):
                    for di in range(DC):
                        tn.matmul(pb[pbo + fo][:, :],
                                  wsb[:, di * 256 + fo * 128:di * 256 + (fo + 1) * 128],
                                  xbuf[b][:, di * TT:(di + 1) * TT],
                                  start=(di == 0), stop=False)
                    mm = tn.matmul(pb[pbo + fo][:, :],
                                   wsbf[0:1, pi * 256 + fo * 128:pi * 256 + (fo + 1) * 128],
                                   negmb[0:1, b * TT:(b + 1) * TT],
                                   start=False, stop=True)
            mm.then_inc(*S["qk"].inc(1))
            if t + 1 < NTT:
                stats_1a(t + 1)
            for rnd in range(2):
                if t > 0 or rnd > 0:
                    tn.wait_ge(S["vevac"].sem, 2 * t + rnd)
                mm = None
                for si in range(2):
                    sub = 2 * rnd + si
                    for di in range(DC):
                        tn.matmul(pb[4][:, si * 256:(si + 1) * 256],
                                  xbuf[b][:, di * TT + sub * 128:di * TT + (sub + 1) * 128],
                                  wv_sb[:, di * 256:(di + 1) * 256],
                                  start=(di == 0), stop=False)
                    mm = tn.matmul(pb[4][:, si * 256:(si + 1) * 256],
                                   negmb[0:1, b * TT + sub * 128:b * TT + (sub + 1) * 128],
                                   wsbf[0:1, 512:768],
                                   start=False, stop=True)
                mm.then_inc(*S["vmm"].inc(1))
            if t + 1 < NTT:
                bcast_1a(t + 1)
        if has1b:
            tn.wait_ge(S["qkfix"].sem, NTT)
            tn.wait_ge(S["vevac"].sem, 2 * NTT)
            flat = []
            grp = 0
            n_od_ev = 0
            for (h, qi, ms) in sched:
                has_od = qi > 0
                if has_od:
                    n_od_ev += 1
                for (m, p, kind, aux, dgi) in ms:
                    first_od = kind == "od" and m == 0
                    first_dg = kind == "dg" and aux == 0
                    last_od = kind == "od" and m == 4 * qi - 1
                    last_dg = kind == "dg" and aux == 3
                    flat.append(dict(h=h, qi=qi, m=m, p=p, grp=grp, n_od=n_od_ev,
                                     has_od=has_od, first=(first_od or first_dg),
                                     first_dg_od=(first_dg and has_od),
                                     start=(first_od or (first_dg and not has_od)),
                                     stop=(last_od or last_dg)))
                grp += 1
            DELAY = 2
            for idx in range(len(flat) + DELAY):
                if idx < len(flat):
                    f = flat[idx]
                    if f["p"] >= 4:
                        tn.wait_ge(S["sex"].sem, f["p"] - 3)
                    tn.matmul(pb[f["p"] % 4][:, :],
                              K_sb[:, f["h"] * T + f["m"] * 128:f["h"] * T + (f["m"] + 1) * 128],
                              Q_sb[:, f["h"] * T + f["qi"] * TT:f["h"] * T + (f["qi"] + 1) * TT],
                              start=True, stop=True).then_inc(*S["ssc"].inc(1))
                if idx >= DELAY:
                    f = flat[idx - DELAY]
                    if f["start"]:
                        tn.wait_ge(S["scmb"].sem, f["grp"])
                    if f["first_dg_od"] and f["m"] == 4 * f["qi"]:
                        tn.wait_ge(S["sod"].sem, f["n_od"])
                    tn.wait_ge(S["sex"].sem, f["p"] + 1)
                    mm = None
                    for sub in range(4):
                        mm = tn.matmul(pb[4 + sub][:, 0:129],
                                       P_pool[f["p"] % 4][:, sub * 128:(sub + 1) * 128],
                                       V_sb[:, f["h"] * 64 * VW + f["m"] * VW:
                                            f["h"] * 64 * VW + f["m"] * VW + 129],
                                       start=(f["start"] or (f["first_dg_od"])),
                                       stop=f["stop"])
                    mm.then_inc(*S["sav"].inc(1))

    @block.vector
    def _(v):
        v.memset(ones_bf[:, :], 1.0)
        v.memset(onesf[:, :], 1.0)
        v.memset(one1[:, :], 1.0)
        v.memset(V_sb[:, :], 1.0).then_inc(*S["setup"].inc(1))
        for wi in range(3):
            v.wait_ge(S["tsu"].sem, wi + 1)
            v.tensor_copy(wsrow[0:1, wi * 256:(wi + 1) * 256],
                          pb[5][0:1, 0:256]).then_inc(*S["vsu"].inc(1))
        v.tensor_copy(wsbf[:, :], wsrow[:, :]).then_inc(*S["vsu"].inc(1))
        def squares_1a(t):
            b = t % 2
            v.wait_ge(S[f"dx{t % 2}"].sem, 16 * (t // 2 + 1))
            for di in range(DC):
                idx = t * DC + di
                if idx >= 4:
                    v.wait_ge(S["sqcons"].sem, idx - 3)
                v.tensor_mul(xsq[di % 4][:, :], xbuf[b][:, di * TT:(di + 1) * TT],
                             xbuf[b][:, di * TT:(di + 1) * TT]).then_inc(*S["sq"].inc(1))

        def rows_1a(t):
            v.wait_ge(S["stats"].sem, t + 1)
            v.tensor_scalar_mul(rw[0], pb[6][0:1, :], 1.0 / D)
            v.tensor_scalar_mul(rw[1], pb[7][0:1, :], 1.0 / D)
            v.tensor_scalar_mul(negmb[0:1, (t % 2) * TT:(t % 2 + 1) * TT], rw[0], -1.0)
            v.tensor_mul(rw[2], rw[0], rw[0])
            v.tensor_sub(rw[3], rw[1], rw[2])
            v.tensor_scalar_add(rw[1], rw[3], float(EPS))
            v.reciprocal(rw[2], rw[1]).then_inc(*S["v1"].inc(1))

        squares_1a(0)
        rows_1a(0)
        for t in range(NTT):
            b2 = t % 2
            v.wait_ge(S["bc"].sem, t + 1)
            v.tensor_copy(a_b[:, :], pb[6][:, :])
            v.tensor_copy(colp[b2][:, 0:4], pb[5][:, 0:4]).then_inc(*S["evac"].inc(1))
            if t + 1 < NTT:
                squares_1a(t + 1)
                rows_1a(t + 1)
            v.wait_ge(S["qk"].sem, t + 1)
            for pi, (pbo, dst) in enumerate([(0, Q_sb), (2, K_sb)]):
                for fo in range(2):
                    last = v.tensor_mul(dst[:, fo * T + t * TT:fo * T + (t + 1) * TT],
                                        pb[pbo + fo][:, :], a_b[:, :])
            last.then_inc(*S["qkfix"].inc(1))
            for rnd in range(2):
                v.wait_ge(S["vmm"].sem, 2 * t + rnd + 1)
                for si in range(2):
                    sub = 2 * rnd + si
                    g64 = t * 4 + sub
                    for fo in range(2):
                        last = v.tensor_scalar(
                            V_sb[:, fo * 64 * VW + g64 * VW:fo * 64 * VW + g64 * VW + 128],
                            pb[4][:, si * 256 + fo * 128:si * 256 + (fo + 1) * 128],
                            colp[b2][:, sub:sub + 1], None, ALU.mult)
                last.then_inc(*S["vevac"].inc(1))
        if has1b:
            v.wait_ge(S["dw"].sem, 96)
            grp = 0
            n_od_ev = 0
            oid = 0
            for (h, qi, ms) in sched:
                has_od = qi > 0
                for (m, p, kind, aux, dgi) in ms:
                    if kind == "dg":
                        v.wait_ge(S["ssc"].sem, p + 1)
                        if dgi >= 2:
                            v.wait_ge(S["sex"].sem, diag_pass_of[dgi - 2] + 1)
                        v.scalar_tensor_tensor(
                            tmpS[dgi % 2][:, :], pb[p % 4][:, :], ISQ,
                            md_sb[:, (h * 4 + aux) * TT:(h * 4 + aux + 1) * TT],
                            ALU.mult, ALU.add).then_inc(*S["sdg"].inc(1))
                    if kind == "od" and m == 4 * qi - 1:
                        v.wait_ge(S["sav"].sem, p + 1)
                        for sub in range(4):
                            last = v.tensor_copy(odsb[:, sub * VW:sub * VW + 129],
                                                 pb[4 + sub][:, 0:129])
                        n_od_ev += 1
                        last.then_inc(*S["sod"].inc(1))
                p_last = ms[-1][1]
                v.wait_ge(S["sav"].sem, p_last + 1)
                for sub in range(4):
                    if has_od:
                        v.tensor_scalar(cmb[:, sub * VW:sub * VW + 129],
                                        odsb[:, sub * VW:sub * VW + 129],
                                        ec_sb[:, h * 4 + sub:h * 4 + sub + 1], None, ALU.mult)
                        last = v.tensor_add(cmb[:, sub * VW:sub * VW + 129],
                                            cmb[:, sub * VW:sub * VW + 129],
                                            pb[4 + sub][:, 0:129])
                    else:
                        last = v.tensor_copy(cmb[:, sub * VW:sub * VW + 129],
                                             pb[4 + sub][:, 0:129])
                last.then_inc(*S["scmb"].inc(1))
                for sub in range(4):
                    v.reciprocal(rcp[:, sub:sub + 1], cmb[:, sub * VW + 128:sub * VW + 129])
                for sub in range(4):
                    slot = oid % 8
                    if oid >= 8:
                        v.wait_ge(S["soc"].sem, 16 * (oid - 7))
                    v.tensor_scalar(outp[:, slot * 128:(slot + 1) * 128],
                                    cmb[:, sub * VW:sub * VW + 128],
                                    rcp[:, sub:sub + 1], None, ALU.mult) \
                        .then_inc(*S["sot"].inc(1))
                    oid += 1
                grp += 1

    @block.sync
    def _(sy):
        if has1b:
            oid = 0
            for (h, qi, ms) in sched:
                for sub in range(4):
                    slot = oid % 8
                    sy.wait_ge(S["sot"].sem, oid + 1)
                    sy.dma_start(out=cc_in[qi * TT + sub * 128:qi * TT + (sub + 1) * 128,
                                           h * 128:(h + 1) * 128],
                                 in_=outp[:, slot * 128:(slot + 1) * 128]) \
                        .then_inc(*S["soc"].inc(16))
                    oid += 1
        if debug:
            sy.wait_ge(S["qkfix"].sem, NTT)
            sy.wait_ge(S["vevac"].sem, 2 * NTT)
            sy.dma_start(out=dbg["qT"], in_=Q_sb[:, :]).then_inc(*S["yst"].inc(16))
            sy.dma_start(out=dbg["kT"], in_=K_sb[:, :]).then_inc(*S["yst"].inc(16))
            sy.dma_start(out=dbg["v"], in_=V_sb[:, :]).then_inc(*S["yst"].inc(16))

    p1.close()
    p2 = ExitStack()
    sb2 = lambda name, shape, dt=F32: p2.enter_context(nc.sbuf_tensor(name, shape, dt))
    if has2:
        Y_sb = sb2("Y_sb", [128, DC * TPC])
        rows2 = sb2("rows2", [1, 4 * TT])
        rw2 = [rows2[0:1, i * TT:(i + 1) * TT] for i in range(4)]
        m2b = sb2("m2b", [128, TT])
        r2b = sb2("r2b", [128, TT])
        onesf2 = sb2("onesf2", [1, 128])
        ones_bf2 = sb2("ones_bf2", [128, 8], BF16)
    p2a = ExitStack()
    if has1b:
        outT_sb = p2a.enter_context(nc.sbuf_tensor("outT_sb", [128, DC * TPC], BF16))

        @block.sync
        def _(sy):
            rid = sy.partition_id()
            sy.wait_ge(S["scc"].sem, rid + 1)
            cc5 = cc_out.rearrange("(s r q) m -> s r q m", r=NC, q=TPC)
            stg = cc_stage.rearrange("(r q) m -> r q m", q=TPC)
            sy.dma_start(out=stg, in_=cc5[ts(rid, 1), :, :, :][0]).then_inc(*S["rcv"].inc(16))
            sy.wait_ge(S["rcv"].sem, 16)
            for fc in range(DC):
                r, cb = fc // 2, fc % 2
                sy.dma_start_transpose(
                    outT_sb[:, fc * TPC:(fc + 1) * TPC],
                    cc_stage[r * TPC:(r + 1) * TPC, cb * 128:(cb + 1) * 128]) \
                    .then_inc(*S["rcv"].inc(16))
            sy.wait_ge(S["rcv"].sem, 16 * 17)
            if debug:
                sy.dma_start(out=dbg["attnT"], in_=outT_sb[:, :]).then_inc(*S["yst"].inc(16))


    if has2:
        woT_sb = p2a.enter_context(nc.sbuf_tensor("woT_sb", [128, DC * D], BF16))
        xow = [p2a.enter_context(nc.sbuf_tensor(f"xow{i}", [128, TT], F32)) for i in range(2)]

        NWO = 32

        @block.vector
        def _(v):
            v.wait_ge(S["soc"].sem, 128 * 16)
            if debug:
                v.wait_ge(S["yst"].sem, 48)
            v.memset(onesf2[:, :], 1.0)
            v.memset(ones_bf2[:, :], 1.0).then_inc(*S["setup"].inc(1))
            for idx in range(NWO):
                v.wait_ge(S["swo"].sem, idx + 1)
                v.wait_ge(S[f"dxo{idx % 2}"].sem, 16 * (idx // 2 + 1))
                fo, tt = idx // 2, idx % 2
                v.tensor_add(Y_sb[:, fo * TPC + tt * TT:fo * TPC + (tt + 1) * TT],
                             pb[idx % 2][:, :], xow[idx % 2][:, :]) \
                    .then_inc(*S["y1e"].inc(1))

        @block.gpsimd
        def _(g):
            if debug:
                g.wait_ge(S["yst"].sem, 48)
            for i in range(4):
                g.dma_start(out=woT_sb[:, i * 4 * D:(i + 1) * 4 * D],
                            in_=wo_st[:, i * 4 * D:(i + 1) * 4 * D]).then_inc(*S["dwo"].inc(16))

        @block.sync
        def _(sy):
            for idx in range(NWO):
                if idx >= 2:
                    sy.wait_ge(S["y1e"].sem, idx - 1)
                sy.dma_start(out=xow[idx % 2][:, :], in_=xown_st[idx]) \
                    .then_inc(*S[f"dxo{idx % 2}"].inc(16))

        @block.tensor
        def _(tn):
            tn.wait_ge(S["dwo"].sem, 64)
            tn.wait_ge(S["rcv"].sem, 16 * 17)
            tn.wait_ge(S["setup"].sem, 2)
            for idx in range(NWO):
                fo, tt = idx // 2, idx % 2
                if idx >= 2:
                    tn.wait_ge(S["y1e"].sem, idx - 1)
                for di in range(DC):
                    mm = tn.matmul(pb[idx % 2][:, :],
                                   woT_sb[:, di * D + fo * 128:di * D + (fo + 1) * 128],
                                   outT_sb[:, di * TPC + tt * TT:di * TPC + (tt + 1) * TT],
                                   start=(di == 0), stop=(di == DC - 1))
                mm.then_inc(*S["swo"].inc(1))

        p2a.close()
        xn2_sb = sb2("xn2_sb", [128, DC * TPC], BF16)
        act_sb = sb2("act_sb", [128, FSC * TPC], BF16)
        gu_buf = [sb2(f"gu_buf{i}", [128, DC * 256], BF16) for i in range(2)]
        dn_buf = [sb2(f"dn_buf{i}", [128, FSC * 128], BF16) for i in range(2)]
        gate_sb = sb2("gate_sb", [128, DC * 8], BF16)
        idn_sb = sb2("idn_sb", [128, 128])
        sel_sb = sb2("sel_sb", [8, 8 * 128])
        cT_sb = sb2("cT_sb", [8, TPC])
        ybf = [sb2(f"ybf{i}", [128, TT], BF16) for i in range(2)]
        ysq = [sb2(f"ysq{i}", [128, TT], BF16) for i in range(2)]
        sg_sb = [sb2(f"sg_sb{i}", [128, TT], BF16) for i in range(2)]
        u1_sb = [sb2(f"u1_sb{i}", [128, TT], BF16) for i in range(2)]
        tmp2 = [sb2("tmp2_0", [128, TT])] * 2
        cbt_ = [m2b, r2b]
        gsc = sb2("gsc", [128, 64])
        gs2 = sb2("gs2", [128, 64])
        gct = sb2("gct", [128, 64])
        gtA = sb2("gtA", [128, 64])
        gtB = sb2("gtB", [128, 64])
        gcl = sb2("gcl", [128, 64])

        JCs = [FC] * E8 + [FSC]
        EORDER = [E8] + list(range(E8))
        JPRE = {}
        acc = 0
        for e in EORDER:
            JPRE[e] = acc
            acc += JCs[e]
        TJC = acc
        CBPOS = {e: i for i, e in enumerate(EORDER) if e < E8}
        jgl = []
        for e in EORDER:
            for j in range(JCs[e]):
                jgl.append((e, j, JPRE[e] + j))
        hgl = []
        for pos, e in enumerate(EORDER):
            for h in range(DC):
                hgl.append((e, h, pos * DC + h))

        @block.gpsimd
        def _(g):
            g.wait_ge(S["y1e"].sem, 32)
            g.dma_start(out=gate_sb[:, :], in_=gate_st).then_inc(*S["dwo"].inc(16))
            g.dma_start(out=idn_sb[:, :], in_=idn_st).then_inc(*S["dwo"].inc(16))
            g.dma_start(out=sel_sb[:, :], in_=sel_st).then_inc(*S["dwo"].inc(16))
            for pos, e in enumerate(EORDER):
                for j in range(JCs[e]):
                    jjx = JPRE[e] + j
                    if jjx >= 2:
                        g.wait_ge(S["gups"].sem, 4 * (jjx - 1))
                    src = gu_st[e, j] if e < E8 else sgu_st[j]
                    g.dma_start(out=gu_buf[jjx % 2][:, :], in_=src) \
                        .then_inc(*S[f"dgu{jjx % 2}"].inc(16))
                for h in range(DC):
                    hhx = pos * DC + h
                    if hhx >= 2:
                        g.wait_ge(S["dnps"].sem, 2 * (hhx - 1))
                    if e < E8:
                        g.dma_start(out=dn_buf[hhx % 2][:, 0:FC * 128], in_=dn_st[e, h]) \
                            .then_inc(*S[f"ddn{hhx % 2}"].inc(16))
                    else:
                        g.dma_start(out=dn_buf[hhx % 2][:, :], in_=sdn_st[h]) \
                            .then_inc(*S[f"ddn{hhx % 2}"].inc(16))

        @block.scalar
        def _(s):
            for tt in range(2):
                for di in range(DC):
                    idx = tt * DC + di
                    s.wait_ge(S["v21"].sem, idx + 1)
                    if idx >= 2:
                        s.wait_ge(S["st2"].sem, 2 * (idx - 1))
                    s.activation(out=ysq[di % 2][:, :], in_=ybf[di % 2][:, :],
                                 func=AF.Square).then_inc(*S["s21"].inc(1))
                s.wait_ge(S["v22"].sem, tt + 1)
                s.activation(out=rw2[3], in_=rw2[2], func=AF.Sqrt).then_inc(*S["s22"].inc(1))
            for (e, j, jjx) in jgl:
                for tt in range(2):
                    s.wait_ge(S["gups"].sem, 4 * jjx + 2 * tt + 1)
                    s.activation(out=sg_sb[tt][:, :], in_=pb[2 * tt][:, :],
                                 func=AF.Silu).then_inc(*S["sil"].inc(1))
                if e == E8 and j == 0:
                    s.wait_ge(S["g1"].sem, 16)
                    for g8 in range(8):
                        s.activation(out=gsc[:, 8 * g8:8 * g8 + 8],
                                     in_=pb[5][:, 8 * g8:8 * g8 + 8],
                                     func=AF.Exp, bias=gcl[:, 8 + g8:9 + g8], scale=1.0) \
                            .then_inc(*S["g2"].inc(1))

        @block.tensor
        def _(tn):
            for tt in range(2):
                for di in range(DC):
                    idx = tt * DC + di
                    tn.wait_ge(S["s21"].sem, idx + 1)
                    if tt > 0 and di == 0:
                        tn.wait_ge(S["ev2"].sem, tt)
                    tn.matmul(pb[7][0:1, :], ones_bf2[:, 0:1], ysq[di % 2][:, :],
                              start=(di == 0), stop=(di == DC - 1)) \
                        .then_inc(*S["st2"].inc(1))
                    mm = tn.matmul(pb[6][0:1, :], ones_bf2[:, 0:1], ybf[di % 2][:, :],
                                   start=(di == 0), stop=(di == DC - 1)) \
                        .then_inc(*S["st2"].inc(1))
                tn.wait_ge(S["v23"].sem, tt + 1)
                tn.matmul(pb[6][:, :], onesf2[0:1, :], rw2[0], start=True, stop=True)
                tn.matmul(pb[7][:, :], onesf2[0:1, :], rw2[3], start=True, stop=True) \
                    .then_inc(*S["bc2"].inc(1))
            tn.wait_ge(S["dwo"].sem, 112)

            def gating_mms():
                for g8 in range(8):
                    for di in range(DC):
                        mm = tn.matmul(pb[5][:, 8 * g8:8 * g8 + 8],
                                       xn2_sb[:, di * TPC + g8 * 128:di * TPC + (g8 + 1) * 128],
                                       gate_sb[:, di * 8:(di + 1) * 8],
                                       start=(di == 0), stop=(di == DC - 1))
                    mm.then_inc(*S["glog"].inc(1))

            def gating_tr(rnd):
                if rnd == 0:
                    tn.wait_ge(S["gcmb"].sem, 8)
                else:
                    tn.wait_ge(S["gsel"].sem, rnd)
                for gi in range(4):
                    g8 = rnd * 4 + gi
                    tn.transpose(pb[4][0:8, gi * 128:(gi + 1) * 128],
                                 gct[:, 8 * g8:8 * g8 + 8], idn_sb[:, :]) \
                        .then_inc(*S["gtr"].inc(1))
            for pos, e in enumerate(EORDER):
                if e < E8:
                    tn.wait_ge(S["dnac"].sem, 32 * pos)
                    tn.wait_ge(S["gsel"].sem, 2)
                    for tt in range(2):
                        tn.matmul(pb[4 + tt][:, :], sel_sb[0:8, e * 128:(e + 1) * 128],
                                  cT_sb[0:8, tt * TT:(tt + 1) * TT], start=True, stop=True) \
                            .then_inc(*S["cbt"].inc(1))
                for j in range(JCs[e]):
                    jjx = JPRE[e] + j
                    if e == E8 and j == 1:
                        gating_mms()
                    if e == E8 and j == 3:
                        gating_tr(0)
                    if e == E8 and j == 5:
                        gating_tr(1)
                    tn.wait_ge(S[f"dgu{jjx % 2}"].sem, 16 * (jjx // 2 + 1))
                    if e < E8 and j == 0:
                        tn.wait_ge(S["cbk"].sem, 2 * CBPOS[e])
                    for tt in range(2):
                        tn.wait_ge(S["xn2d"].sem, 16 * (tt + 1))
                        if jjx >= 1:
                            tn.wait_ge(S["sil"].sem, 2 * (jjx - 1) + tt + 1)
                            tn.wait_ge(S["uact"].sem, 2 * (jjx - 1) + tt + 1)
                        for di in range(DC):
                            mm = tn.matmul(pb[2 * tt][:, :],
                                           gu_buf[jjx % 2][:, di * 256:di * 256 + 128],
                                           xn2_sb[:, di * TPC + tt * TT:di * TPC + (tt + 1) * TT],
                                           start=(di == 0), stop=(di == DC - 1))
                        mm.then_inc(*S["gups"].inc(1))
                        for di in range(DC):
                            mm = tn.matmul(pb[2 * tt + 1][:, :],
                                           gu_buf[jjx % 2][:, di * 256 + 128:di * 256 + 256],
                                           xn2_sb[:, di * TPC + tt * TT:di * TPC + (tt + 1) * TT],
                                           start=(di == 0), stop=(di == DC - 1))
                        mm.then_inc(*S["gups"].inc(1))
                JC = JCs[e]
                for h in range(DC):
                    hhx = pos * DC + h
                    tn.wait_ge(S[f"ddn{hhx % 2}"].sem, 16 * (hhx // 2 + 1))
                    tn.wait_ge(S["uact"].sem, 2 * (JPRE[e] + JC))
                    for tt in range(2):
                        slot = 2 * hhx + tt
                        if slot >= 2:
                            tn.wait_ge(S["dnac"].sem, slot - 1)
                        for fj in range(JC):
                            mm = tn.matmul(pb[4 + tt][:, :],
                                           dn_buf[hhx % 2][:, fj * 128:(fj + 1) * 128],
                                           act_sb[:, fj * TPC + tt * TT:fj * TPC + (tt + 1) * TT],
                                           start=(fj == 0), stop=(fj == JC - 1))
                        mm.then_inc(*S["dnps"].inc(1))

        @block.vector
        def _(v):
            v.wait_ge(S["y1e"].sem, 32)
            for tt in range(2):
                for di in range(DC):
                    idx = tt * DC + di
                    if idx >= 2:
                        v.wait_ge(S["s21"].sem, idx - 1)
                    v.tensor_copy(ybf[di % 2][:, :],
                                  Y_sb[:, di * TPC + tt * TT:di * TPC + (tt + 1) * TT]) \
                        .then_inc(*S["v21"].inc(1))
                v.wait_ge(S["st2"].sem, 32 * (tt + 1))
                v.tensor_scalar_mul(rw2[0], pb[6][0:1, :], 1.0 / D)
                v.tensor_scalar_mul(rw2[1], pb[7][0:1, :], 1.0 / D)
                v.tensor_mul(rw2[2], rw2[0], rw2[0])
                v.tensor_sub(rw2[3], rw2[1], rw2[2])
                v.tensor_scalar_add(rw2[1], rw2[3], float(EPS))
                v.reciprocal(rw2[2], rw2[1]).then_inc(*S["v22"].inc(1))
                v.wait_ge(S["s22"].sem, tt + 1)
                v.engine_nop().then_inc(*S["v23"].inc(1))
                v.wait_ge(S["bc2"].sem, tt + 1)
                v.tensor_copy(m2b[:, :], pb[6][:, :])
                v.tensor_copy(r2b[:, :], pb[7][:, :]).then_inc(*S["ev2"].inc(1))
                for di in range(DC):
                    v.tensor_sub(tmp2[di % 2][:, :],
                                 Y_sb[:, di * TPC + tt * TT:di * TPC + (tt + 1) * TT], m2b[:, :])
                    v.tensor_mul(xn2_sb[:, di * TPC + tt * TT:di * TPC + (tt + 1) * TT],
                                 tmp2[di % 2][:, :], r2b[:, :]).then_inc(*S["xn2d"].inc(1))
            v.wait_ge(S["glog"].sem, 8)
            for g8 in range(8):
                v.reduce_max(gcl[:, g8:g8 + 1], pb[5][:, 8 * g8:8 * g8 + 8], axis=AX.X)
            for g8 in range(8):
                v.tensor_scalar_mul(gcl[:, 8 + g8:9 + g8], gcl[:, g8:g8 + 1], -1.0)
            v.engine_nop().then_inc(*S["g1"].inc(16))
            for g8 in range(8):
                v.wait_ge(S["g2"].sem, g8 + 1)
                v.reduce_sum(gcl[:, 16 + g8:17 + g8], gsc[:, 8 * g8:8 * g8 + 8], axis=AX.X)
            for g8 in range(8):
                v.reciprocal(gcl[:, 24 + g8:25 + g8], gcl[:, 16 + g8:17 + g8])
            for g8 in range(8):
                v.tensor_scalar(gs2[:, 8 * g8:8 * g8 + 8], gsc[:, 8 * g8:8 * g8 + 8],
                                gcl[:, 24 + g8:25 + g8], None, ALU.mult)
            for g8 in range(8):
                v.reduce_max(gcl[:, 32 + g8:33 + g8], gs2[:, 8 * g8:8 * g8 + 8], axis=AX.X)
            for g8 in range(8):
                v.tensor_scalar(gtA[:, 8 * g8:8 * g8 + 8], gs2[:, 8 * g8:8 * g8 + 8],
                                gcl[:, 32 + g8:33 + g8], None, ALU.is_equal)
            for g8 in range(8):
                v.scalar_tensor_tensor(gtB[:, 8 * g8:8 * g8 + 8], gtA[:, 8 * g8:8 * g8 + 8],
                                       -1e9, gs2[:, 8 * g8:8 * g8 + 8], ALU.mult, ALU.add)
            for g8 in range(8):
                v.reduce_max(gcl[:, 40 + g8:41 + g8], gtB[:, 8 * g8:8 * g8 + 8], axis=AX.X)
            for g8 in range(8):
                v.tensor_scalar(gtA[:, 8 * g8:8 * g8 + 8], gs2[:, 8 * g8:8 * g8 + 8],
                                gcl[:, 40 + g8:41 + g8], None, ALU.is_ge)
            for g8 in range(8):
                v.tensor_mul(gct[:, 8 * g8:8 * g8 + 8], gs2[:, 8 * g8:8 * g8 + 8],
                             gtA[:, 8 * g8:8 * g8 + 8]).then_inc(*S["gcmb"].inc(1))
            for pos, e in enumerate(EORDER):
                if e < E8:
                    for tt in range(2):
                        v.wait_ge(S["cbt"].sem, 2 * (CBPOS[e] - 1) + tt + 1)
                        v.tensor_copy(cbt_[tt][:, :], pb[4 + tt][:, :]) \
                            .then_inc(*S["cbk"].inc(1))
                for j in range(JCs[e]):
                    jjx = JPRE[e] + j
                    for tt in range(2):
                        v.wait_ge(S["gups"].sem, 4 * jjx + 2 * tt + 2)
                        v.wait_ge(S["sil"].sem, 2 * jjx + tt + 1)
                        if e < E8:
                            v.tensor_mul(u1_sb[tt][:, :], pb[2 * tt + 1][:, :],
                                         cbt_[tt][:, :])
                            v.tensor_mul(act_sb[:, j * TPC + tt * TT:j * TPC + (tt + 1) * TT],
                                         u1_sb[tt][:, :], sg_sb[tt][:, :]) \
                                .then_inc(*S["uact"].inc(1))
                        else:
                            v.tensor_mul(act_sb[:, j * TPC + tt * TT:j * TPC + (tt + 1) * TT],
                                         pb[2 * tt + 1][:, :], sg_sb[tt][:, :]) \
                                .then_inc(*S["uact"].inc(1))
                    if e == E8 and j in (3, 5):
                        rnd = (j - 3) // 2
                        v.wait_ge(S["gtr"].sem, 4 * (rnd + 1))
                        v.tensor_copy(cT_sb[0:8, rnd * TT:(rnd + 1) * TT], pb[4][0:8, 0:512]) \
                            .then_inc(*S["gsel"].inc(1))
                for h in range(DC):
                    hhx = pos * DC + h
                    for tt in range(2):
                        v.wait_ge(S["dnps"].sem, 2 * hhx + tt + 1)
                        v.tensor_add(Y_sb[:, h * TPC + tt * TT:h * TPC + (tt + 1) * TT],
                                     Y_sb[:, h * TPC + tt * TT:h * TPC + (tt + 1) * TT],
                                     pb[4 + tt][:, :]).then_inc(*S["dnac"].inc(1))

        @block.sync
        def _(sy):
            if debug:
                sy.wait_ge(S["xn2d"].sem, 32)
                sy.dma_start(out=dbg["xn2"], in_=xn2_sb[:, :]).then_inc(*S["yst"].inc(16))
                sy.wait_ge(S["gsel"].sem, 2)
                sy.dma_start(out=dbg["comb"], in_=cT_sb[0:8, :]).then_inc(*S["yst"].inc(16))

            sy.wait_ge(S["dnac"].sem, 32 * 9)
            y3 = yT.rearrange("(a p) m -> a p m", p=128)
            for di in range(DC):
                sy.dma_start(out=y3[di], in_=Y_sb[:, di * TPC:(di + 1) * TPC]) \
                    .then_inc(*S["yst"].inc(16))

    @block.sync
    def _(sy):
        sy.wait_ge(S["yst"].sem, S["yst"].v)

    if not has1b:
        p1.close()
    else:
        if not has2:
            p2a.close()
        p2.close()
    ctx.close()
    return nc



def stage_inputs(inp, phase="full"):
    x = np.asarray(inp["L0_attn_x"]).reshape(T, D)
    Wq = np.asarray(inp["L0_attn_Wq"])
    Wk = np.asarray(inp["L0_attn_Wk"])
    Wv = np.asarray(inp["L0_attn_Wv"])

    x_st = np.ascontiguousarray(
        x.reshape(NTT, TT, DC, 128).transpose(0, 3, 2, 1).reshape(NTT, 128, DC * TT))

    def wslice(W, c):
        hA, hB = head_pair(c)
        WT = W.T
        wc = np.concatenate([WT[:, 128 * hA:128 * (hA + 1)], WT[:, 128 * hB:128 * (hB + 1)]], axis=1)
        return np.ascontiguousarray(wc.reshape(DC, 128, 256).transpose(1, 0, 2).reshape(128, DC * 256))

    in_maps = []
    for c in range(NC):
        hA, hB = head_pair(c)
        sl = (slope(hA), slope(hB))
        kl = np.arange(128, dtype=np.float64)[:, None]
        ql = np.arange(TT, dtype=np.float64)[None, :]
        md = np.empty((128, 8 * TT), np.float32)
        for hi in range(2):
            for j in range(4):
                mm = sl[hi] * (128 * j + kl - ql)
                mm = np.where(128 * j + kl <= ql, mm, NEG_BIG)
                md[:, (hi * 4 + j) * TT:(hi * 4 + j + 1) * TT] = mm.astype(np.float32)
        bc = np.zeros((128, 128), np.float32)
        for hi in range(2):
            for dl in range(64):
                bc[:, hi * 64 + dl] = (sl[hi] * (kl[:, 0] - 128.0 * dl)).astype(np.float32)
        ec = np.zeros((128, 8), np.float32)
        for hi in range(2):
            for sub in range(4):
                ec[:, hi * 4 + sub] = np.exp(-sl[hi] * (128.0 * sub + kl[:, 0])).astype(np.float32)
        m = {
            "x_st": x_st, "wq_st": wslice(Wq, c), "wk_st": wslice(Wk, c), "wv_st": wslice(Wv, c),
            "md_st": md, "bc_st": bc, "ec_st": ec,
        }
        if phase == "full":
            m.update(stage_phase2(inp, c))
        in_maps.append(m)
    return in_maps


def stage_phase2(inp, c):
    Wo = np.asarray(inp["L0_attn_Wo"])
    gate = np.asarray(inp["L0_ffn_gate_weight"])
    gu = np.asarray(inp["L0_ffn_gate_up_weights"])
    dn = np.asarray(inp["L0_ffn_down_weights"])
    sgu = np.asarray(inp["L0_ffn_shared_gate_up"])
    sdn = np.asarray(inp["L0_ffn_shared_down"])
    x = np.asarray(inp["L0_attn_x"]).reshape(T, D)

    perm = []
    for r in range(NC):
        ha, hb = head_pair(r)
        perm.extend(range(128 * ha, 128 * ha + 128))
        perm.extend(range(128 * hb, 128 * hb + 128))
    woT = Wo.T[np.asarray(perm), :]
    wo_st = np.ascontiguousarray(woT.reshape(DC, 128, D).transpose(1, 0, 2).reshape(128, DC * D))
    xT = x.T
    xown = xT[:, TPC * c:TPC * (c + 1)]
    xown_st = np.ascontiguousarray(
        xown.reshape(DC, 128, 2, TT).transpose(0, 2, 1, 3).reshape(32, 128, TT))
    gate_st = np.ascontiguousarray(gate.T.reshape(DC, 128, 8).transpose(1, 0, 2).reshape(128, DC * 8))
    idn = np.eye(128, dtype=np.float32)
    sel = np.zeros((8, 8 * 128), np.float32)
    for e in range(8):
        sel[e, e * 128:(e + 1) * 128] = 1.0
    gug = gu[:, :F, :].reshape(E8, FC, 128, DC, 128)
    guu = gu[:, F:, :].reshape(E8, FC, 128, DC, 128)
    gu_st = np.empty((E8, FC, 128, DC * 256), np.float32)
    gg = gug.transpose(0, 1, 4, 3, 2)
    uu = guu.transpose(0, 1, 4, 3, 2)
    gu_st = np.ascontiguousarray(
        np.stack([gg, uu], axis=4).reshape(E8, FC, 128, DC, 2 * 128).reshape(E8, FC, 128, DC * 256))
    dn_st = np.ascontiguousarray(
        dn.reshape(E8, DC, 128, FC, 128).transpose(0, 1, 4, 3, 2).reshape(E8, DC, 128, FC * 128))
    sgug = sgu[:FS, :].reshape(FSC, 128, DC, 128)
    sguu = sgu[FS:, :].reshape(FSC, 128, DC, 128)
    sgu_st = np.ascontiguousarray(
        np.stack([sgug.transpose(0, 3, 2, 1), sguu.transpose(0, 3, 2, 1)], axis=3)
        .reshape(FSC, 128, DC, 256).reshape(FSC, 128, DC * 256))
    sdn_st = np.ascontiguousarray(
        sdn.reshape(DC, 128, FSC, 128).transpose(0, 3, 2, 1).reshape(DC, 128, FSC * 128))
    return {
        "wo_st": wo_st, "xown_st": xown_st, "gate_st": gate_st,
        "idn_st": idn, "sel_st": sel,
        "gu_st": gu_st, "dn_st": dn_st, "sgu_st": sgu_st, "sdn_st": sdn_st,
    }



def kernel(**inputs):
    in_maps = stage_inputs(inputs, phase="full")
    nc = build_nc(phase="full", debug=False)
    res = run_bass_kernel_spmd(nc, in_maps, core_ids=list(range(NC)))
    out = np.empty((1, T, D), np.float32)
    for c in range(NC):
        out[0, TPC * c:TPC * (c + 1), :] = np.asarray(res.results[c]["yT"]).T
    return out
